# version 19
# baseline (speedup 1.0000x reference)
import math
from contextlib import ExitStack
from collections import Counter

import numpy as np
import ml_dtypes

import concourse.bass as bass
import concourse.mybir as mybir
from concourse.bass import ds

F32 = mybir.dt.float32
BF16 = mybir.dt.bfloat16
I32 = mybir.dt.int32
AF = mybir.ActivationFunctionType
ALU = mybir.AluOpType
AX = mybir.AxisListType

D = 2048
SEQ = 2048
TOK = 1024
NHL = 8
HID = 8192
INV_SQRT_DH = float(np.float32(1.0) / np.float32(np.sqrt(np.float32(128.0))))
EPS = 1e-5
SHRINKS_F32 = [float(np.float32(s)) for s in np.linspace(0.4, 1.0, 16)]
INV_SHRINKS_F32 = [float(np.float32(1.0) / np.float32(s)) for s in SHRINKS_F32]
NS = 16
QMAX = 15.0
INV_QMAX = float(np.float32(1.0) / np.float32(15.0))
QF = 512
NQT_FC = 32
NQT = 64
NBLK = QF // 64
NXT = 16
NMG = 12
NEGBIG = -1e30

CO_ID = 0
CO_ONE = 128
CO_INVS = 256
CO_NEGS = 272
CO_S = 288
CO_W = 304


CAP0 = 212832


def _arena(nc, ctx):
    CAP = CAP0
    backing = ctx.enter_context(
        nc.sbuf_tensor("arena", [128, CAP], mybir.dt.uint8))
    base = nc.lookup_mloc(backing).addr

    def place(name, shape, dtype, off):
        per_part = int(np.prod(shape[1:])) * mybir.dt.size(dtype)
        assert off % 32 == 0 and off + per_part <= CAP, (name, off, per_part)
        h = nc.virtual_sbuf_tensor(name, list(shape), dtype)
        nc._tensor(h.name, list(h.shape), h.dtype, type="SB")
        mloc = nc.lookup_mloc(h)
        mloc.allocated = True
        mloc.addr = base + off
        return h

    return place


def build(stage: int):
    nc = bass.Bass("TRN2", target_bir_lowering=False, debug=False)

    xb = nc.dram_tensor("xb", [SEQ, D], F32, kind="ExternalInput")
    wqkv = nc.dram_tensor("wqkv", [D, 3 * NHL * 128], F32, kind="ExternalInput")
    wo = nc.dram_tensor("wo", [D, D], F32, kind="ExternalInput")
    wfc = nc.dram_tensor("wfc", [1024, D], F32, kind="ExternalInput")
    wpj = nc.dram_tensor("wpj", [256, HID], F32, kind="ExternalInput")
    consts = nc.dram_tensor("consts", [128, CO_W], F32, kind="ExternalInput")
    xrows = nc.dram_tensor("xrows", [TOK, D], F32, kind="ExternalInput")

    wqkv_bf = nc.dram_tensor("wqkv_bf", [D, 3 * NHL * 128], BF16, kind="Internal")
    vals_sh = nc.dram_tensor("vals_sh", [2, NHL, 128, SEQ], BF16,
                             kind="Internal", addr_space="Shared")
    bar_in = nc.dram_tensor("bar_in", [1, 4], F32, kind="Internal")
    bar_out = nc.dram_tensor("bar_out", [2, 1, 4], F32, kind="Internal")
    x2_spill = nc.dram_tensor("x2_spill", [TOK, D], F32, kind="Internal")
    y_part = nc.dram_tensor("y_part", [TOK, D], F32, kind="Internal")
    wfc_nat = nc.dram_tensor("wfc_nat", [1024, D], BF16, kind="Internal")
    wpj_nat = nc.dram_tensor("wpj_nat", [256, HID], BF16, kind="Internal")
    wfcT_shc = nc.dram_tensor("wfcT_shc", [4, D, 256], BF16, kind="Internal")
    wpjT_shc = nc.dram_tensor("wpjT_shc", [4, 2048, 256], BF16, kind="Internal")
    wfcT_allc = nc.dram_tensor("wfcT_allc", [4, 8, D, 256], BF16, kind="Internal")
    wpjT_allc = nc.dram_tensor("wpjT_allc", [4, 8, 2048, 256], BF16, kind="Internal")

    outT = {}
    if stage >= 5:
        outT["out"] = nc.dram_tensor("out", [TOK, D], F32, kind="ExternalOutput")
    if stage == 1:
        outT["d_qkv"] = nc.dram_tensor("d_qkv", [24, 128, SEQ], BF16, kind="ExternalOutput")
    if stage == 2:
        outT["d_vals"] = nc.dram_tensor("d_vals", [NHL, 128, SEQ], BF16, kind="ExternalOutput")
    if stage == 3:
        outT["d_x2"] = nc.dram_tensor("d_x2", [TOK, D], F32, kind="ExternalOutput")
        outT["d_ln2t"] = nc.dram_tensor("d_ln2t", [16, 128, TOK], BF16, kind="ExternalOutput")
    if stage == 4:
        outT["d_x2"] = nc.dram_tensor("d_x2", [TOK, D], F32, kind="ExternalOutput")
        outT["d_wfcT"] = nc.dram_tensor("d_wfcT", [4, D, 256], BF16, kind="ExternalOutput")
        outT["d_wpjT"] = nc.dram_tensor("d_wpjT", [4, 2048, 256], BF16, kind="ExternalOutput")

    with ExitStack() as ctx:
        place = _arena(nc, ctx)
        consts_sb = place("consts_sb", [128, CO_W], F32, 0)
        identb = place("identb", [128, 128], BF16, 1216)
        onesb = place("onesb", [128, 128], BF16, 1472)
        zbw = place("zbw", [128, 2, 128], BF16, 1728)
        lnstat = place("lnstat", [128, 4, 32], F32, 2752)
        zpart = place("zpart", [128, 2, 4], F32, 3264)
        zinv = place("zinv", [128, 2, 1], F32, 3296)
        qblk = place("qblk", [128, 2, NBLK, 8], F32, 3328)
        qerr = place("qerr", [128, 2, NBLK, NS], F32, 3840)
        qmsk = place("qmsk", [128, NBLK, NS], F32, 4864)
        qsel = place("qsel", [128, NBLK, NS], F32, 5376)
        A_END = 5888
        B0 = A_END
        ln1T = place("ln1T", [128, 16, TOK], BF16, B0)
        vxs = place("vxs", [128, 2 * NHL, TOK], BF16, B0)
        hsqA = place("hsqA", [128, 32, 512], BF16, B0)
        B_END = B0 + 32768
        C0 = B_END
        wosf = place("wosf", [128, 8, D], BF16, C0)
        htmp = place("htmp", [128, 2, 512], F32, C0 + 32768)
        wstr2 = place("wstr2", [128, 2, 8, 256], BF16, C0 + 36864)
        outsb4 = place("outsb4", [128, 4, D], F32, C0)
        C_END = C0 + 65536
        qkvs = place("qkvs", [128, 24, SEQ], BF16, C0)
        D0 = C_END
        ln2T = place("ln2T", [128, 16, TOK], BF16, D0)
        D_END = D0 + 32768
        E0 = D_END
        wqbB = place("wqbB", [128, 2, 16, 256], BF16, E0)
        pcst = place("pcst", [128, 2, 3072], BF16, E0 + 16384)
        x2t = place("x2t", [128, 4, D], F32, E0)
        hsqB = place("hsqB", [128, 32, 512], BF16, E0)
        E_END = E0 + 32768
        F0 = E_END
        xs = place("xs", [128, 2, D], F32, F0)
        Etile = place("Etile", [128, 2, SEQ], BF16, F0)
        ln2st = place("ln2st", [128, 4, D], BF16, F0)
        wstr = place("wstr", [128, 2, 16, 256], BF16, F0)
        F_END = F0 + 16384
        G0 = F_END
        lntile = place("lntile", [128, 2, D], BF16, G0)
        stg = place("stg", [128, 2, 512], F32, G0)
        qrl = place("qrl", [128, 2, QF], BF16, G0 + 4096)
        deqT = place("deqT", [128, 2, 4, 128], BF16, G0 + 6144)
        G_END = G0 + 8192
        Q0 = G_END
        qt_t = place("qt_t", [128, 2, QF], F32, Q0)
        qi = place("qi", [128, 4, QF], mybir.dt.int16, Q0 + 4096)
        qq = place("qq", [128, 4, QF], F32, Q0 + 8192)
        qsq = place("qsq", [128, 4, QF], F32, Q0 + 8192)
        deqb = place("deqb", [128, 2, QF], BF16, Q0 + 16384)
        qdq = place("qdq", [128, 2, QF], mybir.dt.int16, Q0 + 16384)
        Q_END = Q0 + 18432
        assert Q_END <= CAP0
        assert Q_END <= 212832, Q_END

        pMM = ctx.enter_context(nc.psum_tensor("pMM", [128, 2048], F32))
        pSC = ctx.enter_context(nc.psum_tensor("pSC", [128, 1024], F32))
        pTa = ctx.enter_context(nc.psum_tensor("pTa", [128, 1024], BF16))
        pTb = ctx.enter_context(nc.psum_tensor("pTb", [128, 1024], BF16))

        semnames = [
            "sDVE", "sPOOL", "sLNa", "sTR", "sQKV", "sQKVE",
            "sSC", "sEXP", "sCS", "sCC", "sTRA", "sVS0", "sVS1", "sVX0", "sVX1",
            "sOP", "sSTG",
            "sFT", "sLN2a", "sQI", "sQS", "sQQ",
            "sH", "sRELU", "sF2", "sSTG2", "sFT2",
            "sX0", "sX1", "sWQ0", "sWQ1", "sQW0", "sQW1",
            "sCB0", "sCB1", "sWOS0", "sWOS1", "sXN0", "sXN1", "sXN2", "sXN3",
            "sDNW0", "sDNW1", "sQRL0", "sQRL1", "sDW0", "sDW1",
            "sFW0", "sFW1", "sF2W0", "sF2W1",
            "sXR20", "sXR21", "sXR22", "sXR23",
            "sYS0", "sYS1", "sYS2", "sYS3",
            "sOUT0", "sOUT1", "sOUT2", "sOUT3",
            "sX2S0", "sX2S1", "sX2S2", "sX2S3",
            "sDBG", "sCONST", "sCONSTg", "sWQC", "sWQS0", "sWQS1",
        ]
        S = {n: ctx.enter_context(nc.semaphore(n)) for n in semnames}

        block = ctx.enter_context(nc.Block())

        def emit(eng, E):
            C = Counter()
            EV = {}

            def inc(sname, amt=1):
                C[sname] += amt
                return C[sname]

            def mark(ins, sname, amt=1):
                if ins is not None:
                    ins.then_inc(S[sname], amt)
                return inc(sname, amt)

            def ev(key):
                pass

            def wev(key):
                sem, val = EV[key]
                if E is not None and val > 0:
                    E.wait_ge(S[sem], val)

            def weng(target, sname, val):
                if eng == target and E is not None and val > 0:
                    E.wait_ge(S[sname], val)

            def _chain(fn_name, *a, **k):
                if eng == "vector":
                    E.wait_ge(S["sDVE"], C["sDVE"])
                    ins = getattr(E, fn_name)(*a, **k)
                    ins.then_inc(S["sDVE"], 1)
                else:
                    ins = None
                inc("sDVE")
                return ins

            def _pchain(fn_name, *a, **k):
                if eng == "gpsimd":
                    E.wait_ge(S["sPOOL"], C["sPOOL"])
                    ins = getattr(E, fn_name)(*a, **k)
                    ins.then_inc(S["sPOOL"], 1)
                else:
                    ins = None
                inc("sPOOL")
                return ins

            def v_ts(*a, **k): return _chain("tensor_scalar", *a, **k)
            def v_tt(*a, **k): return _chain("tensor_tensor", *a, **k)
            def v_stt(*a, **k): return _chain("scalar_tensor_tensor", *a, **k)
            def v_red(*a, **k): return _chain("tensor_reduce", *a, **k)
            def v_cp(*a, **k): return _chain("tensor_copy", *a, **k)
            def v_rcp(*a, **k): return _chain("reciprocal", *a, **k)
            def v_sel(*a, **k): return _chain("select", *a, **k)
            def v_bns(*a, **k): return _chain("bn_stats", *a, **k)
            def v_bna(*a, **k): return _chain("bn_aggr", *a, **k)
            def v_ms(*a, **k): return _chain("memset", *a, **k)
            def p_ts(*a, **k): return _pchain("tensor_scalar", *a, **k)
            def p_red(*a, **k): return _pchain("tensor_reduce", *a, **k)

            def dve_ev(key):
                EV[key] = ("sDVE", C["sDVE"])

            def pool_ev(key):
                EV[key] = ("sPOOL", C["sPOOL"])

            def act(*a, **k):
                return E.activation(*a, **k) if eng == "scalar" else None

            def t_mm(*a, **k):
                return E.matmul(*a, **k) if eng == "tensor" else None

            def t_tr(*a, **k):
                return E.transpose(*a, **k) if eng == "tensor" else None

            def dma_s(*a, **k):
                return E.dma_start(*a, **k) if eng == "sync" else None

            def dma_a(*a, **k):
                return E.dma_start(*a, **k) if eng == "scalar" else None

            def dma_g(*a, **k):
                return E.dma_start(*a, **k) if eng == "gpsimd" else None



            q_items = []
            if stage >= 4:
                tail = []
                for qt in range(NQT):
                    q_items.append(("setup", qt, 0))
                    for t in range(NS + 6):
                        if t % 2 == 0 and 0 <= t - 4 < NS:
                            q_items.append(("st2", qt, t - 4))
                        if t % 2 == 1 and 0 <= t - 5 < NS:
                            q_items.append(("sqp", qt, t - 5))
                        if t % 2 == 0 and 0 <= t - 6 < NS:
                            q_items.append(("redp", qt, t - 6))
                        if t % 2 == 0 and 0 <= t - 2 < NS:
                            q_items.append(("clip2", qt, t - 2))
                        if t < NS:
                            q_items.append(("cvt", qt, t))
                        if t == 2 and tail:
                            q_items.append(tail.pop(0))
                        if t == 4 and tail:
                            q_items.append(tail.pop(0))
                    tail = [("trk", qt, 0), ("deq", qt, 0)]
                q_items.extend(tail)
            q_pos = [0]

            def q_src(qt):
                if qt < NQT_FC:
                    return wfc[128 * (qt // 4):128 * (qt // 4 + 1),
                               QF * (qt % 4):QF * (qt % 4 + 1)]
                lq = qt - NQT_FC
                rt, ct = lq % 2, lq // 2
                return wpj[128 * rt:128 * (rt + 1), QF * ct:QF * (ct + 1)]

            def q_nat(qt):
                if qt < NQT_FC:
                    return wfc_nat[128 * (qt // 4):128 * (qt // 4 + 1),
                                   QF * (qt % 4):QF * (qt % 4 + 1)]
                lq = qt - NQT_FC
                rt, ct = lq % 2, lq // 2
                return wpj_nat[128 * rt:128 * (rt + 1), QF * ct:QF * (ct + 1)]

            def t3(sl):
                return sl.rearrange("p (b k) -> p b k", k=64)

            def emit_q_item(it):
                kind, qt, s = it
                p = qt % 2
                if kind == "setup":
                    if eng == "gpsimd" and qt >= 2:
                        wev(("DQ", qt - 2))
                    i = dma_g(out=qt_t[:, p, :].bitcast(F32), in_=q_src(qt))
                    mark(i, "sQW%d" % (qt % 2), 16)
                    if eng == "vector":
                        E.wait_ge(S["sQW%d" % (qt % 2)], 16 * (qt // 2 + 1))
                        if qt >= 2:
                            wev(("DQ", qt - 2))
                    v_red(qblk[:, p, :, 0:1], t3(qt_t[:, p, :]), AX.X, ALU.max,
                          apply_absolute_value=True)
                    v_ts(qblk[:, p, :, 1:2], qblk[:, p, :, 0:1], 1e-8,
                         INV_QMAX, ALU.max, ALU.mult)
                    v_rcp(qblk[:, p, :, 2:3], qblk[:, p, :, 1:2])
                    v_tt(t3(qt_t[:, p, :]), t3(qt_t[:, p, :]),
                         qblk[:, p, :, 2:3].broadcast_to([128, NBLK, 64]),
                         ALU.mult)
                    dve_ev(("QSU", qt))
                elif kind == "cvt":
                    n = NS * qt + s
                    i4 = n % 4
                    if eng == "scalar":
                        wev(("QSU", qt))
                        if n >= 4:
                            wev(("QDd", (n - 4) | 1))
                    i = act(qi[:, i4, :], qt_t[:, p, :], AF.Identity,
                            bias=0.0, scale=INV_SHRINKS_F32[s])
                    mark(i, "sQI")
                elif kind == "clip2":
                    n = NS * qt + s
                    i4 = n % 4
                    if eng == "gpsimd":
                        E.wait_ge(S["sQI"], n + 2)
                    p_ts(qi[:, i4:i4 + 2, :].rearrange("p a b -> p (a b)"),
                         qi[:, i4:i4 + 2, :].rearrange("p a b -> p (a b)"),
                         QMAX, -QMAX, ALU.min, ALU.max)
                    pool_ev(("QC", n + 1))
                elif kind == "st2":
                    n = NS * qt + s
                    s4 = n % 4
                    if eng == "vector":
                        wev(("QC", n + 1))
                        if n >= 4:
                            wev(("QR", (n - 4) | 1))
                    v_stt(qq[:, s4, :], qi[:, s4, :], -SHRINKS_F32[s],
                          qt_t[:, p, :], ALU.mult, ALU.add)
                    v_stt(qq[:, s4 + 1, :], qi[:, s4 + 1, :],
                          -SHRINKS_F32[s + 1], qt_t[:, p, :],
                          ALU.mult, ALU.add)
                    dve_ev(("QDd", n + 1))
                elif kind == "sqp":
                    n = NS * qt + s
                    s4 = n % 4
                    if eng == "scalar":
                        wev(("QDd", n + 1))
                    i = act(qsq[:, s4:s4 + 2, :].rearrange("p a b -> p (a b)"),
                            qq[:, s4:s4 + 2, :].rearrange("p a b -> p (a b)"),
                            AF.Square)
                    mark(i, "sQS")
                elif kind == "redp":
                    n = NS * qt + s
                    s4 = n % 4
                    if eng == "vector":
                        E.wait_ge(S["sQS"], C["sQS"])
                        if s == 0 and qt >= 2:
                            wev(("QT", qt - 2))
                    v_red(qerr[:, p, :, s:s + 2]
                          .rearrange("p b a -> p a b"),
                          qsq[:, s4:s4 + 2, :]
                          .rearrange("p a (b k) -> p a b k", k=64),
                          AX.X, ALU.add)
                    dve_ev(("QR", n + 1))
                elif kind == "trk":
                    if eng == "vector":
                        E.wait_ge(S["sCONST"], 16)
                        wev(("QR", NS * qt + NS - 1))
                    err = qerr[:, p, :, :]
                    v_red(qblk[:, p, :, 3:4], err, AX.X, ALU.min)
                    v_tt(qmsk[:, :, :], err,
                         qblk[:, p, :, 3:4].broadcast_to([128, NBLK, NS]),
                         ALU.is_equal)
                    v_tt(qsel[:, :, :], qmsk[:, :, :],
                         consts_sb[:, CO_INVS:CO_INVS + NS]
                         .rearrange("p (o s) -> p o s", o=1).broadcast_to([128, NBLK, NS]),
                         ALU.mult)
                    v_red(qblk[:, p, :, 4:5], qsel[:, :, :], AX.X, ALU.max)
                    v_tt(qmsk[:, :, :],
                         consts_sb[:, CO_INVS:CO_INVS + NS]
                         .rearrange("p (o s) -> p o s", o=1)
                         .broadcast_to([128, NBLK, NS]),
                         qblk[:, p, :, 4:5].broadcast_to([128, NBLK, NS]),
                         ALU.is_equal)
                    v_tt(qsel[:, :, :], qmsk[:, :, :],
                         consts_sb[:, CO_S:CO_S + NS]
                         .rearrange("p (o s) -> p o s", o=1)
                         .broadcast_to([128, NBLK, NS]),
                         ALU.mult)
                    v_red(qblk[:, p, :, 5:6], qsel[:, :, :], AX.X, ALU.max)
                    v_tt(qblk[:, p, :, 6:7], qblk[:, p, :, 1:2],
                         qblk[:, p, :, 5:6], ALU.mult)
                    dve_ev(("QT", qt))
                elif kind == "deq":
                    if eng == "vector":
                        if qt >= 2:
                            E.wait_ge(S["sDNW%d" % (qt % 2)], 16 * (qt // 2))
                    v_tt(t3(qt_t[:, p, :]), t3(qt_t[:, p, :]),
                         qblk[:, p, :, 4:5].broadcast_to([128, NBLK, 64]),
                         ALU.mult)
                    dve_ev(("U2", qt))
                    if eng == "scalar":
                        wev(("U2", qt))
                        if qt >= 1:
                            E.wait_ge(S["sDNW%d" % ((qt - 1) % 2)],
                                      16 * ((qt - 1) // 2 + 1))
                    i = act(qdq[:, 1 - p, :], qt_t[:, p, :], AF.Identity,
                            bias=0.0, scale=1.0)
                    mark(i, "sQQ")
                    if eng == "gpsimd":
                        E.wait_ge(S["sQQ"], C["sQQ"])
                    p_ts(qdq[:, 1 - p, :], qdq[:, 1 - p, :], QMAX, -QMAX,
                         ALU.min, ALU.max)
                    pool_ev(("QC2", qt))
                    if eng == "vector":
                        wev(("QC2", qt))
                    v_tt(t3(deqb[:, p, :]), t3(qdq[:, 1 - p, :]),
                         qblk[:, p, :, 6:7].broadcast_to([128, NBLK, 64]),
                         ALU.mult)
                    dve_ev(("DQ", qt))
                    if eng == "gpsimd":
                        wev(("DQ", qt))
                    i = dma_g(out=q_nat(qt), in_=deqb[:, p, :])
                    mark(i, "sDNW%d" % (qt % 2), 16)

            def q_step(k):
                if stage < 4:
                    return
                n = 0
                while n < k and q_pos[0] < len(q_items):
                    emit_q_item(q_items[q_pos[0]])
                    q_pos[0] += 1
                    n += 1

            def q_flush_until(qt_lim):
                if stage < 4:
                    return
                last = -1
                for ii in range(len(q_items)):
                    if q_items[ii][1] < qt_lim:
                        last = ii
                while q_pos[0] <= last:
                    emit_q_item(q_items[q_pos[0]])
                    q_pos[0] += 1

            LQG = [0]
            lateT_hist = []

            def lateT(qt_list, shW_of):
                for qt in qt_list:
                    g = LQG[0]
                    sl = g % 2
                    if eng == "gpsimd":
                        E.wait_ge(S["sDNW%d" % (qt % 2)], 16 * (qt // 2 + 1))
                        if g >= 2:
                            wev(("DT", lateT_hist[g - 2]))
                    i = dma_g(out=qrl[:, sl, :], in_=q_nat(qt))
                    mark(i, "sQRL%d" % sl, 16)
                    G = TRG[0]
                    pT = pTa if G % 2 == 0 else pTb
                    if eng == "tensor":
                        E.wait_ge(S["sQRL%d" % sl], C["sQRL%d" % sl])
                        if G >= 2:
                            wev(("TRE", G - 2))
                    for k in range(4):
                        i = t_tr(pT[:, 128 * k:128 * (k + 1)],
                                 qrl[:, sl, 128 * k:128 * (k + 1)],
                                 identb[:, :])
                        if k == 3:
                            mark(i, "sTR")
                    if eng == "vector":
                        E.wait_ge(S["sTR"], C["sTR"])
                        E.wait_ge(S["sDW%d" % sl], C["sDW%d" % sl])
                    v_cp(deqT[:, sl, :, :],
                         pT[:, 0:512].rearrange("p (a b) -> p a b", b=128))
                    dve_ev(("DT", qt))
                    dve_ev(("TRE", G))
                    if eng == "gpsimd":
                        wev(("DT", qt))
                    i = dma_g(out=shW_of(qt), in_=deqT[:, sl, :, :])
                    mark(i, "sDW%d" % sl, 16)
                    TRG[0] += 1
                    lateT_hist.append(qt)
                    LQG[0] += 1

            def wfc_shW(qt):
                cj = qt // 8
                c0 = 128 * ((qt // 4) % 2)
                f0 = QF * (qt % 4)
                return wfcT_shc[cj, f0:f0 + 512, c0:c0 + 128] \
                    .rearrange("(k p) c -> p k c", p=128)

            def wpj_shW(qt):
                lq = qt - NQT_FC
                cj = lq // 8
                h0 = 512 * ((lq // 2) % 4)
                c0 = 128 * (lq % 2)
                return wpjT_shc[cj, h0:h0 + 512, c0:c0 + 128] \
                    .rearrange("(k p) c -> p k c", p=128)

            CCN = {}

            def emit_cc(name, ins_l, outs_l, groups=None):
                if eng == "gpsimd":
                    E.wait_ge(S["sDW0"], C["sDW0"])
                    E.wait_ge(S["sDW1"], C["sDW1"])
                i = (E.collective_compute(
                    "AllGather", ALU.bypass,
                    replica_groups=(groups if groups is not None
                                    else [[0, 1, 2, 3, 4, 5, 6, 7]]),
                    ins=ins_l, outs=outs_l)
                    if eng == "gpsimd" else None)
                mark(i, "sCC")
                CCN[name] = C["sCC"]

            i = dma_s(out=consts_sb[:, :], in_=consts[:, :])
            mark(i, "sCONST", 16)
            weng("sync", "sCONST", 16)
            i = dma_s(out=bar_in[:, :], in_=consts_sb[0:1, 0:4])
            mark(i, "sCONST", 16)
            weng("gpsimd", "sCONST", 16)
            i = dma_g(out=identb[:, :], in_=consts_sb[:, CO_ID:CO_ID + 128])
            mark(i, "sCONSTg", 16)
            i = dma_g(out=onesb[:, :], in_=consts_sb[:, CO_ONE:CO_ONE + 128])
            mark(i, "sCONSTg", 16)

            for k in range(16):
                if eng == "gpsimd" and k >= 2:
                    E.wait_ge(S["sWQS%d" % (k % 2)], 16 * (k // 2))
                srcv = wqkv[128 * k:128 * (k + 1), :]
                stv = pcst[:, k % 2, :]
                i = dma_g(out=stv, in_=srcv)
                mark(i, "sWQC", 16)
                weng("gpsimd", "sWQC", C["sWQC"])
                i = dma_g(out=wqkv_bf[128 * k:128 * (k + 1), :], in_=stv)
                mark(i, "sWQS%d" % (k % 2), 16)

            q_step(60)

            TRG = [0]
            trev = {}

            def emit_ln1_tile(i_t, xsrc, stoff, sema, lsl, evtag):
                for sb4 in range(4):
                    v_bns(lnstat[:, lsl, 6 * sb4:6 * sb4 + 6],
                          xsrc[:, 512 * sb4:512 * (sb4 + 1)])
                v_bna(lnstat[:, lsl, 24:26], lnstat[:, lsl, 0:24])
                v_ts(lnstat[:, lsl, 25:26], lnstat[:, lsl, 25:26],
                     EPS, None, ALU.add)
                dve_ev((evtag, i_t, "e"))
                if eng == "scalar":
                    wev((evtag, i_t, "e"))
                i = act(lnstat[:, lsl, 26:27], lnstat[:, lsl, 25:26], AF.Sqrt)
                mark(i, sema)
                if eng == "vector":
                    E.wait_ge(S[sema], C[sema])
                v_rcp(lnstat[:, lsl, 27:28], lnstat[:, lsl, 26:27])
                v_stt(lnstat[:, lsl, 28:29], lnstat[:, lsl, 24:25],
                      -1.0, lnstat[:, lsl, 27:28], ALU.mult, ALU.mult)
                dve_ev((evtag, i_t, "n"))

            def ln1_tile(i_t):
                if eng == "sync" and i_t >= 2:
                    E.wait_ge(S["sLNa"], 2 * (i_t - 2) + 2)
                i = dma_s(out=xs[:, i_t % 2, :], in_=xb[128 * i_t:128 * (i_t + 1), :])
                mark(i, "sX%d" % (i_t % 2), 16)
                if eng == "vector":
                    E.wait_ge(S["sX%d" % (i_t % 2)], 16 * (i_t // 2 + 1))
                emit_ln1_tile(i_t, xs[:, i_t % 2, :], 0, "sLNa", i_t % 2, "L1")
                if eng == "scalar":
                    wev(("L1", i_t, "n"))
                    if i_t >= 2:
                        E.wait_ge(S["sTR"], 2 * (i_t - 2) + 2)
                i = act(lntile[:, i_t % 2, :], xs[:, i_t % 2, :], AF.Identity,
                        bias=lnstat[:, i_t % 2, 28:29],
                        scale=lnstat[:, i_t % 2, 27:28])
                mark(i, "sLNa")

            def ln1T_tile(i_t, half):
                for g in range(2):
                    G = TRG[0]
                    pT = pTa if G % 2 == 0 else pTb
                    if eng == "tensor":
                        E.wait_ge(S["sLNa"], 2 * i_t + 2)
                        E.wait_ge(S["sCONSTg"], 32)
                        if G >= 2:
                            wev(("TRE", G - 2))
                    for k in range(8):
                        fc = 8 * g + k
                        i = t_tr(pT[:, 128 * k:128 * (k + 1)],
                                 lntile[:, i_t % 2, 128 * fc:128 * (fc + 1)],
                                 identb[:, :])
                        if k == 7:
                            mark(i, "sTR")
                    if eng == "vector":
                        E.wait_ge(S["sTR"], G + 1)
                        if G == 16:
                            E.wait_ge(S["sQKV"], 24)
                    tloc = (i_t % 8) * 128
                    v_cp(ln1T[:, 8 * g:8 * g + 8, tloc:tloc + 128],
                         pT[:, :].rearrange("p (a b) -> p a b", b=128))
                    dve_ev(("TRE", G))
                    TRG[0] += 1

            def qkv_half(half):
                for mg in range(NMG):
                    gmg = half * NMG + mg
                    for k in range(16):
                        if eng == "sync":
                            if gmg == 0:
                                E.wait_ge(S["sWQS%d" % (k % 2)],
                                          16 * (k // 2 + 1))
                            if gmg == 1 and k == 0:
                                E.wait_ge(S["sWQS0"], 16 * 8)
                                E.wait_ge(S["sWQS1"], 16 * 8)
                            if gmg >= 2 and k == 0:
                                E.wait_ge(S["sQKV"], 2 * (gmg - 1))
                        i = dma_s(out=wqbB[:, gmg % 2, k, :],
                                  in_=wqkv_bf[128 * k:128 * (k + 1),
                                              256 * mg:256 * (mg + 1)])
                        mark(i, "sWQ%d" % (gmg % 2), 16)
                    for m2 in range(2):
                        m = 2 * mg + m2
                        Mg = 24 * half + m
                        bp = Mg % 2
                        if eng == "tensor":
                            if m2 == 0:
                                E.wait_ge(S["sWQ%d" % (gmg % 2)],
                                          256 * (gmg // 2 + 1))
                                wev(("TRE", 16 * (half + 1) - 1))
                            if Mg >= 2:
                                E.wait_ge(S["sQKVE"], 2 * Mg - 2)
                        for k in range(16):
                            for n in range(2):
                                i = t_mm(pMM[:, 1024 * bp + 512 * n:
                                             1024 * bp + 512 * (n + 1)],
                                         wqbB[:, gmg % 2, k, 128 * m2:128 * (m2 + 1)],
                                         ln1T[:, k, 512 * n:512 * (n + 1)],
                                         start=(k == 0), stop=(k == 15))
                                if k == 15 and n == 1:
                                    mark(i, "sQKV")
                        scl = INV_SQRT_DH if m < 8 else 1.0
                        for n in range(2):
                            if eng == "scalar":
                                E.wait_ge(S["sQKV"], Mg + 1)
                            i = act(qkvs[:, m, 1024 * half + 512 * n:
                                          1024 * half + 512 * (n + 1)],
                                    pMM[:, 1024 * bp + 512 * n:
                                        1024 * bp + 512 * (n + 1)],
                                    AF.Identity, bias=0.0, scale=scl)
                            mark(i, "sQKVE")
                    q_step(30)

            for i_t in range(10):
                ln1_tile(i_t)
                if i_t < 8:
                    ln1T_tile(i_t, 0)
            qkv_half(0)
            for i_t in range(8, 16):
                if 10 <= i_t + 2 < 16:
                    ln1_tile(i_t + 2)
                ln1T_tile(i_t, 1)
            qkv_half(1)

            if stage == 1:
                for m in range(24):
                    weng("sync", "sQKVE", 96)
                    i = dma_s(out=outT["d_qkv"][m, :, :], in_=qkvs[:, m, :])
                    mark(i, "sDBG", 16)
                weng("sync", "sDBG", 16 * 24)
                return

            if eng == "sync":
                pidv = E.partition_id()
                offmy = (pidv % 2) * 1024
            pend_cs = [None]

            def colsum_emit(slot):
                h_, qt_ = slot // 16, slot % 16
                if eng == "tensor":
                    wev(("ZB", slot))
                    if qt_ == 0 and h_ >= 1:
                        wev(("VAL", h_ - 1))
                for kt in range(4):
                    i = t_mm(pMM[:, 512 * kt:512 * (kt + 1)],
                             zbw[:, slot % 2, :],
                             Etile[:, qt_ % 2, 512 * kt:512 * (kt + 1)],
                             start=(qt_ == 0), stop=(qt_ == 15))
                    if kt == 3:
                        mark(i, "sCS")

            def head_finish(h):
                if eng == "vector":
                    E.wait_ge(S["sCS"], 16 * (h + 1))
                for kt in range(4):
                    v_tt(qkvs[:, 16 + h, 512 * kt:512 * (kt + 1)],
                         qkvs[:, 16 + h, 512 * kt:512 * (kt + 1)],
                         pMM[:, 512 * kt:512 * (kt + 1)], ALU.mult)
                dve_ev(("VAL", h))
                if eng == "sync":
                    wev(("VAL", h))
                i = dma_s(out=vals_sh[ds(pidv % 2, 1), h, :, :]
                          if eng == "sync" else None,
                          in_=qkvs[:, 16 + h, :])
                mark(i, "sVS%d" % (h % 2), 16)

            for h in range(NHL):
                if h >= 1:
                    head_finish(h - 1)
                for qt in range(16):
                    slot = 16 * h + qt
                    for kq in range(4):
                        qidx = 4 * slot + kq
                        sb5 = 512 * (qidx % 2)
                        if eng == "tensor":
                            if qidx == 0:
                                E.wait_ge(S["sQKVE"], 96)
                            if qidx >= 2:
                                E.wait_ge(S["sEXP"], qidx - 1)
                        i = t_mm(pSC[:, sb5:sb5 + 512],
                                 qkvs[:, h, 128 * qt:128 * (qt + 1)],
                                 qkvs[:, 8 + h, 512 * kq:512 * (kq + 1)],
                                 start=True, stop=True)
                        mark(i, "sSC")
                        if eng == "scalar":
                            E.wait_ge(S["sSC"], qidx + 1)
                            if slot >= 2 and kq == 0:
                                E.wait_ge(S["sCS"], slot - 1)
                        i = act(Etile[:, qt % 2, 512 * kq:512 * (kq + 1)],
                                pSC[:, sb5:sb5 + 512], AF.Exp,
                                bias=0.0, scale=1.0,
                                accum_out=zpart[:, slot % 2, kq:kq + 1])
                        mark(i, "sEXP")
                    if eng == "vector":
                        E.wait_ge(S["sEXP"], 4 * slot + 4)
                        if slot >= 2:
                            E.wait_ge(S["sCS"], slot - 1)
                    v_red(zinv[:, slot % 2, 0:1],
                          zpart[:, slot % 2, :]
                          .rearrange("p (o s) -> p o s", o=1),
                          AX.X, ALU.add)
                    v_rcp(zinv[:, slot % 2, 0:1], zinv[:, slot % 2, 0:1])
                    v_ts(zbw[:, slot % 2, :], onesb[:, 0:128],
                         zinv[:, slot % 2, 0:1], None, ALU.mult)
                    dve_ev(("ZB", slot))
                    if pend_cs[0] is not None:
                        colsum_emit(pend_cs[0])
                    pend_cs[0] = slot
                if stage >= 4 and h == 4:
                    q_flush_until(8)
                    lateT(list(range(0, 8)), wfc_shW)
                    emit_cc("wfc0", [wfcT_shc[0, :, :]],
                            [wfcT_allc[0, :, :, :]])
            if pend_cs[0] is not None:
                colsum_emit(pend_cs[0])
                pend_cs[0] = None
            head_finish(NHL - 1)
            if eng == "gpsimd":
                E.wait_ge(S["sCONST"], 32)
                E.wait_ge(S["sVS0"], 16 * 4)
                E.wait_ge(S["sVS1"], 16 * 4)
            emit_cc("bar", [bar_in[:, :]], [bar_out[:, :, :]],
                    groups=[[0, 1], [2, 3], [4, 5], [6, 7]])
            if eng == "gpsimd":
                pidv_g = E.partition_id()
                offmy_g = (pidv_g % 2) * 1024
                E.wait_ge(S["sCC"], CCN["bar"])
            for r in range(2):
                for hl in range(NHL):
                    i = dma_g(out=vxs[:, 8 * r + hl, :],
                              in_=vals_sh[r, hl, :, ds(offmy_g, TOK)]
                              if eng == "gpsimd" else None)
                    mark(i, "sVX%d" % (hl % 2), 16)

            if stage == 2:
                for hl in range(NHL):
                    if eng == "sync":
                        wev(("VAL", NHL - 1))
                    i = dma_s(out=outT["d_vals"][hl, :, :], in_=qkvs[:, 16 + hl, :])
                    mark(i, "sDBG", 16)
                weng("sync", "sDBG", 16 * NHL)
                return

            q_step(60)

            FG = [0]
            fgev = {}
            for n in range(2):
                for p4 in range(4):
                    tl = 4 * n + p4
                    if eng == "sync":
                        E.wait_ge(S["sQKV"], 48)
                        if n == 1:
                            E.wait_ge(S["sLN2a"], 2 * (tl - 4) + 2)
                            E.wait_ge(S["sX2S%d" % p4], 16)
                    i = dma_s(out=x2t[:, p4, :],
                              in_=xrows[128 * tl:128 * (tl + 1), :])
                    mark(i, "sXN%d" % p4, 16)
                for p8 in range(2):
                    wk = 2 * n + p8
                    for hc in range(8):
                        if eng == "gpsimd":
                            if wk == 0 and hc == 0:
                                E.wait_ge(S["sCS"], 128)
                            if wk >= 1 and hc == 0:
                                E.wait_ge(S["sOP"], 16 * wk)
                        i = dma_g(out=wosf[:, hc, :],
                                  in_=wo[1024 * p8 + 128 * hc:
                                         1024 * p8 + 128 * (hc + 1), :])
                        mark(i, "sWOS%d" % (wk % 2), 16)
                    for fc in range(16):
                        gidx = wk * 16 + fc
                        r2 = gidx % 2
                        if eng == "tensor":
                            if fc == 0:
                                E.wait_ge(S["sVX0"], 16 * NHL)
                                E.wait_ge(S["sVX1"], 16 * NHL)
                                E.wait_ge(S["sWOS%d" % (wk % 2)],
                                          128 * (wk // 2 + 1))
                            if gidx >= 2:
                                E.wait_ge(S["sSTG"], gidx - 1)
                            if gidx < 2:
                                E.wait_ge(S["sEXP"], 512)
                        for hc in range(8):
                            rhs = vxs[:, 8 * p8 + hc, 512 * n:512 * (n + 1)]
                            i = t_mm(pSC[:, 512 * r2:512 * (r2 + 1)],
                                     wosf[:, hc, 128 * fc:128 * (fc + 1)],
                                     rhs, start=(hc == 0), stop=(hc == 7))
                            if hc == 7:
                                mark(i, "sOP")
                        if eng == "scalar":
                            E.wait_ge(S["sOP"], gidx + 1)
                            if gidx >= 2:
                                E.wait_ge(S["sFT"], 4 * (gidx - 1))
                        i = act(stg[:, r2, :], pSC[:, 512 * r2:512 * (r2 + 1)],
                                AF.Identity, bias=0.0, scale=1.0)
                        mark(i, "sSTG")
                        G2 = FG[0]
                        pT32 = pMM[:, 512 * (G2 % 2):512 * (G2 % 2) + 512]
                        if eng == "tensor":
                            E.wait_ge(S["sSTG"], gidx + 1)
                            if G2 >= 2:
                                wev(("FG", G2 - 2))
                            if G2 < 2:
                                wev(("VAL", NHL - 1))
                        for p4 in range(4):
                            i = t_tr(pT32[:, 128 * p4:128 * (p4 + 1)],
                                     stg[:, r2, 128 * p4:128 * (p4 + 1)],
                                     consts_sb[:, CO_ID:CO_ID + 128])
                            if p4 == 3:
                                mark(i, "sFT", 4)
                        for p4 in range(4):
                            tl = 4 * n + p4
                            if eng == "vector":
                                E.wait_ge(S["sFT"], 4 * gidx + 4)
                                if p8 == 0 and fc == 0:
                                    E.wait_ge(S["sXN%d" % p4], 16 * (n + 1))
                            v_tt(x2t[:, p4, 128 * fc:128 * (fc + 1)],
                                 x2t[:, p4, 128 * fc:128 * (fc + 1)],
                                 pT32[:, 128 * p4:128 * (p4 + 1)],
                                 ALU.add)
                        dve_ev(("FG", G2))
                        FG[0] += 1
                        q_step(7)
                for p4 in range(4):
                    t = 4 * n + p4
                    if eng == "sync":
                        wev(("FG", (2 * n + 1) * 16 + 15))
                    i = dma_s(out=x2_spill[128 * t:128 * (t + 1), :],
                              in_=x2t[:, p4, :])
                    mark(i, "sX2S%d" % p4, 16)
                if n == 0:
                    for p4 in range(4):
                        t = p4
                        if eng == "vector":
                            wev(("FG", 31))
                        emit_ln1_tile(t, x2t[:, p4, :], 0, "sLN2a",
                                      2 + t % 2, "L2")
                        if eng == "scalar":
                            wev(("L2", t, "n"))
                            E.wait_ge(S["sCS"], 128)
                        i = act(ln2st[:, t % 4, :], x2t[:, p4, :], AF.Identity,
                                bias=lnstat[:, 2 + t % 2, 28:29],
                                scale=lnstat[:, 2 + t % 2, 27:28])
                        mark(i, "sLN2a")
                q_step(50)
                if stage >= 4 and n == 0:
                    q_flush_until(16)
                    lateT(list(range(8, 16)), wfc_shW)
                    emit_cc("wfc1", [wfcT_shc[1, :, :]],
                            [wfcT_allc[1, :, :, :]])

            def ln2T_tiles(trange):
                for t in trange:
                    for g in range(2):
                        G = TRG[0]
                        pT = pTa if G % 2 == 0 else pTb
                        if eng == "tensor":
                            E.wait_ge(S["sLN2a"], 2 * t + 2)
                            E.wait_ge(S["sOP"], 64)
                            if G >= 2:
                                wev(("TRE", G - 2))
                        for k in range(8):
                            fc = 8 * g + k
                            i = t_tr(pT[:, 128 * k:128 * (k + 1)],
                                     ln2st[:, t % 4, 128 * fc:128 * (fc + 1)],
                                     identb[:, :])
                            if k == 7:
                                mark(i, "sTR")
                            else:
                                mark(i, "sTRA")
                        if eng == "vector":
                            E.wait_ge(S["sTR"], G + 1)
                            if G == 32:
                                E.wait_ge(S["sOP"], 64)
                        tloc = 128 * t
                        v_cp(ln2T[:, 8 * g:8 * g + 8, tloc:tloc + 128],
                             pT[:, :].rearrange("p (a b) -> p a b", b=128))
                        dve_ev(("TRE", G))
                        TRG[0] += 1

            ln2T_tiles(range(0, 4))
            q_step(60)
            for p4 in range(4):
                t = 4 + p4
                if eng == "vector":
                    wev(("FG", 63))
                emit_ln1_tile(t, x2t[:, p4, :], 0, "sLN2a", 2 + t % 2, "L2")
                if eng == "scalar":
                    wev(("L2", t, "n"))
                    E.wait_ge(S["sTR"], 32 + 2 * (t - 4) + 2)
                i = act(ln2st[:, t % 4, :], x2t[:, p4, :], AF.Identity,
                        bias=lnstat[:, 2 + t % 2, 28:29],
                        scale=lnstat[:, 2 + t % 2, 27:28])
                mark(i, "sLN2a")
            q_step(60)
            ln2T_tiles(range(4, 8))

            if stage == 3:
                for _p4 in range(4):
                    weng("sync", "sX2S%d" % _p4, 32)
                for t in range(8):
                    i = dma_s(out=outT["d_x2"][128 * t:128 * (t + 1), :],
                              in_=x2_spill[128 * t:128 * (t + 1), :])
                    mark(i, "sDBG", 16)
                for fcg in range(16):
                    if eng == "sync":
                        wev(("TRE", 47))
                    i = dma_s(out=outT["d_ln2t"][fcg, :, :], in_=ln2T[:, fcg, :])
                    mark(i, "sDBG", 16)
                weng("sync", "sDBG", 16 * 24)
                return

            if stage == 4:
                for cj in range(2, 4):
                    q_flush_until(8 * (cj + 1))
                    lateT(list(range(8 * cj, 8 * (cj + 1))), wfc_shW)
                    emit_cc("wfc%d" % cj,
                            [wfcT_shc[cj, :, :]], [wfcT_allc[cj, :, :, :]])
                for cj in range(4):
                    q_flush_until(NQT_FC + 8 * (cj + 1))
                    lateT(list(range(NQT_FC + 8 * cj, NQT_FC + 8 * (cj + 1))),
                          wpj_shW)
                    emit_cc("wpj%d" % cj,
                            [wpjT_shc[cj, :, :]], [wpjT_allc[cj, :, :, :]])
                for _p4 in range(4):
                    weng("sync", "sX2S%d" % _p4, 32)
                for t in range(8):
                    i = dma_s(out=outT["d_x2"][128 * t:128 * (t + 1), :],
                              in_=x2_spill[128 * t:128 * (t + 1), :])
                    mark(i, "sDBG", 16)
                weng("sync", "sDW0", 16 * NQT // 2)
                weng("sync", "sDW1", 16 * NQT // 2)
                for cj in range(4):
                    for r in range(4):
                        i = dma_s(out=outT["d_wfcT"][cj, 512 * r:512 * (r + 1), :],
                                  in_=wfcT_shc[cj, 512 * r:512 * (r + 1), :])
                        mark(i, "sDBG", 16)
                    for r in range(4):
                        i = dma_s(out=outT["d_wpjT"][cj, 512 * r:512 * (r + 1), :],
                                  in_=wpjT_shc[cj, 512 * r:512 * (r + 1), :])
                        mark(i, "sDBG", 16)
                weng("sync", "sDBG", 16 * 40)
                return

            ghx_hist = []
            gitc = [0]
            lfgc = [0]
            gLc = [0]
            gfcc = [0]
            for hh in range(2):
              for th in range(2):
                for itl in range(16):
                    j, rl = itl // 4, itl % 4
                    hg = 4 * (4 * hh + rl) + j
                    hgl = 4 * rl + j
                    git = gitc[0]
                    gitc[0] += 1
                    it = git
                    for fc in range(16):
                        if eng == "sync":
                            if git == 0 and fc == 0:
                                E.wait_ge(S["sTR"], C["sTR"])
                                E.wait_ge(S["sTRA"], C["sTRA"])
                            if hh == 0 and th == 0 and itl % 4 == 0 \
                                    and fc == 0:
                                E.wait_ge(S["sCC"], CCN["wfc%d" % j])
                            if git >= 2 and fc == 0:
                                E.wait_ge(S["sH"], 2 * (git - 1))
                        i = dma_s(out=wstr[:, git % 2, fc, :],
                                  in_=wfcT_allc[j, 4 * hh + rl,
                                                128 * fc:128 * (fc + 1), :])
                        mark(i, "sFW%d" % (git % 2), 16)
                    for cc in range(2):
                        hcx = 2 * hgl + cc
                        ghx = 64 * hh + 32 * th + hcx
                        ghx_hist.append(ghx)
                        gx = 2 * git + cc
                        bk = 2 * (git % 2) + cc
                        if eng == "tensor":
                            if cc == 0:
                                E.wait_ge(S["sFW%d" % (git % 2)],
                                          256 * (git // 2 + 1))
                                if git == 0:
                                    wev(("TRE", 47))
                                    for _p4 in range(4):
                                        E.wait_ge(S["sX2S%d" % _p4], 32)
                                    E.wait_ge(S["sSTG"], 64)
                                if git in (32, 33):
                                    E.wait_ge(S["sSTG2"], 32)
                                if git >= 2:
                                    wev(("HE", ghx_hist[2 * (git - 2) + 1]))
                        i = None
                        for fc in range(16):
                            i = t_mm(pMM[:, 512 * bk:512 * (bk + 1)],
                                     wstr[:, git % 2, fc,
                                          128 * cc:128 * (cc + 1)],
                                     ln2T[:, fc, 512 * th:512 * (th + 1)],
                                     start=(fc == 0), stop=(fc == 15))
                        mark(i, "sH")
                        if eng == "scalar":
                            E.wait_ge(S["sH"], 2 * git + cc + 1)
                            if gx >= 2:
                                wev(("HE", ghx_hist[gx - 2]))
                        i = act(htmp[:, cc, :],
                                pMM[:, 512 * bk:512 * (bk + 1)],
                                AF.Relu, bias=0.0, scale=1.0)
                        mark(i, "sRELU")
                        if eng == "vector":
                            E.wait_ge(S["sRELU"], C["sRELU"])
                            if git == 0 and cc == 0:
                                for _p4 in range(4):
                                    E.wait_ge(S["sX2S%d" % _p4], 32)
                            if git == 32 and cc == 0:
                                E.wait_ge(S["sF2"], 64)
                        dst = (hsqA[:, hcx, :] if th == 0
                               else hsqB[:, hcx, :])
                        v_tt(dst, htmp[:, cc, :], htmp[:, cc, :],
                             ALU.mult)
                        dve_ev(("HE", ghx))
                    if hh == 0:
                        q_step(44)
                        if th == 0 and itl == 1:
                            q_flush_until(24)
                            lateT(list(range(16, 24)), wfc_shW)
                            emit_cc("wfc2", [wfcT_shc[2, :, :]],
                                    [wfcT_allc[2, :, :, :]])
                        elif th == 0 and itl == 5:
                            q_flush_until(32)
                            lateT(list(range(24, 32)), wfc_shW)
                            emit_cc("wfc3", [wfcT_shc[3, :, :]],
                                    [wfcT_allc[3, :, :, :]])
                        elif (th, itl) in ((0, 9), (0, 13)):
                            cjp = (itl - 9) // 4
                            q_flush_until(NQT_FC + 8 * (cjp + 1))
                            lateT(list(range(NQT_FC + 8 * cjp,
                                             NQT_FC + 8 * (cjp + 1))),
                                  wpj_shW)
                            emit_cc("wpj%d" % cjp, [wpjT_shc[cjp, :, :]],
                                    [wpjT_allc[cjp, :, :, :]])
                    else:
                        q_step(30)
                        if th == 0 and itl in (1, 5):
                            cjp = 2 + (itl - 1) // 4
                            q_flush_until(NQT_FC + 8 * (cjp + 1))
                            lateT(list(range(NQT_FC + 8 * cjp,
                                             NQT_FC + 8 * (cjp + 1))),
                                  wpj_shW)
                            emit_cc("wpj%d" % cjp, [wpjT_shc[cjp, :, :]],
                                    [wpjT_allc[cjp, :, :, :]])
              for th in range(2):
                for p4 in range(4):
                    tl = 4 * th + p4
                    if eng == "sync":
                        if hh == 0 and th == 0:
                            E.wait_ge(S["sOP"], 64)
                            E.wait_ge(S["sX2S%d" % p4], 32)
                        elif hh == 0:
                            E.wait_ge(S["sYS%d" % p4], 16)
                        else:
                            E.wait_ge(S["sYS%d" % p4], 32)
                            if th == 1:
                                E.wait_ge(S["sOUT%d" % p4], 16)
                    i = dma_s(out=outsb4[:, p4, :],
                              in_=(x2_spill if hh == 0 else y_part)
                              [128 * tl:128 * (tl + 1), :])
                    mark(i, "sXR2%d" % p4, 16)
                for fg in range(8):
                    lfg = lfgc[0]
                    lfgc[0] += 1
                    for hcgl in range(4):
                        cw = 2 * hh + hcgl // 2
                        L = gLc[0]
                        gLc[0] += 1
                        if eng == "sync":
                            if th == 0 and fg == 0:
                                E.wait_ge(S["sCC"], CCN["wpj%d" % cw])
                            if L >= 2:
                                E.wait_ge(S["sF2"], L - 1)
                        i = dma_s(out=wstr2[:, L % 2, :, :],
                                  in_=wpjT_allc[cw, fg,
                                                1024 * (hcgl % 2):
                                                1024 * (hcgl % 2 + 1), :]
                                  .rearrange("(k p) c -> p k c", p=128))
                        mark(i, "sF2W%d" % (L % 2), 16)
                        bk2 = 2 * (fg % 2)
                        if eng == "tensor":
                            E.wait_ge(S["sF2W%d" % (L % 2)], 16 * (L // 2 + 1))
                            if hcgl == 0:
                                if fg == 0:
                                    wev(("HE", 64 * hh + 31))
                                    wev(("HE", 64 * hh + 63))
                                if lfg >= 2:
                                    E.wait_ge(S["sSTG2"], 2 * (lfg - 1))
                        for hc8 in range(8):
                            hc = 8 * hcgl + hc8
                            for cc in range(2):
                                i = t_mm(pMM[:, 512 * (bk2 + cc):
                                             512 * (bk2 + cc + 1)],
                                         wstr2[:, L % 2, hc8,
                                               128 * cc:128 * (cc + 1)],
                                         hsqA[:, hc, :] if th == 0
                                         else hsqB[:, hc, :],
                                         start=(hc == 0), stop=(hc == 31))
                                if hc8 == 7 and cc == 1:
                                    mark(i, "sF2")
                    for cc in range(2):
                        fc = 2 * fg + cc
                        gfc = gfcc[0]
                        gfcc[0] += 1
                        bk2 = 2 * (fg % 2)
                        if eng == "scalar":
                            E.wait_ge(S["sF2"], 4 * (lfg + 1))
                            if gfc >= 2:
                                E.wait_ge(S["sFT2"], 4 * (gfc - 1))
                        i = act(stg[:, gfc % 2, :],
                                pMM[:, 512 * (bk2 + cc):512 * (bk2 + cc + 1)],
                                AF.Identity, bias=0.0, scale=1.0)
                        mark(i, "sSTG2")
                        G2 = FG[0]
                        pT32 = pSC[:, 512 * (G2 % 2):512 * (G2 % 2) + 512]
                        if eng == "tensor":
                            E.wait_ge(S["sSTG2"], gfc + 1)
                            if G2 >= 2:
                                wev(("FG", G2 - 2))
                        for p4 in range(4):
                            i = t_tr(pT32[:, 128 * p4:128 * (p4 + 1)],
                                     stg[:, gfc % 2, 128 * p4:128 * (p4 + 1)],
                                     consts_sb[:, CO_ID:CO_ID + 128])
                            if p4 == 3:
                                mark(i, "sFT2", 4)
                        for p4 in range(4):
                            if eng == "vector":
                                E.wait_ge(S["sFT2"], 4 * gfc + 4)
                                if fc == 0:
                                    E.wait_ge(S["sXR2%d" % p4],
                                              C["sXR2%d" % p4])
                            v_tt(outsb4[:, p4, 128 * fc:128 * (fc + 1)],
                                 outsb4[:, p4, 128 * fc:128 * (fc + 1)],
                                 pT32[:, 128 * p4:128 * (p4 + 1)], ALU.add)
                        dve_ev(("FG", G2))
                        dve_ev(("O", hh, th, fc))
                        FG[0] += 1
                for p4 in range(4):
                    tl = 4 * th + p4
                    if eng == "sync":
                        wev(("O", hh, th, 15))
                    if hh == 0:
                        i = dma_s(out=y_part[128 * tl:128 * (tl + 1), :],
                                  in_=outsb4[:, p4, :])
                        mark(i, "sYS%d" % p4, 16)
                    else:
                        i = dma_s(out=outT["out"][128 * tl:128 * (tl + 1), :],
                                  in_=outsb4[:, p4, :])
                        mark(i, "sOUT%d" % p4, 16)
            for _p4 in range(4):
                weng("sync", "sOUT%d" % _p4, 32)

        @block.sync
        def _(e):
            emit("sync", e)

        @block.gpsimd
        def _(e):
            emit("gpsimd", e)

        @block.vector
        def _(e):
            emit("vector", e)

        @block.scalar
        def _(e):
            emit("scalar", e)

        @block.tensor
        def _(e):
            emit("tensor", e)

    return nc, outT


from concourse.bass_utils import run_bass_kernel_spmd


def _make_consts():
    c = np.zeros((128, CO_W), np.float32)
    c[:, CO_ID:CO_ID + 128] = np.eye(128)
    c[:, CO_ONE:CO_ONE + 128] = 1.0
    c[:, CO_INVS:CO_INVS + 16] = np.array(INV_SHRINKS_F32, np.float32)[None, :]
    c[:, CO_NEGS:CO_NEGS + 16] = -np.array(SHRINKS_F32, np.float32)[None, :]
    c[:, CO_S:CO_S + 16] = np.array(SHRINKS_F32, np.float32)[None, :]
    return c


def kernel(L0_attn_x, L0_attn_w_qkv, L0_attn_w_o, L0_ffn_w_fc, L0_ffn_w_proj):
    x = np.ascontiguousarray(np.asarray(L0_attn_x, np.float32))
    w_qkv = np.ascontiguousarray(np.asarray(L0_attn_w_qkv, np.float32))
    w_o = np.ascontiguousarray(np.asarray(L0_attn_w_o, np.float32))
    w_fc = np.ascontiguousarray(np.asarray(L0_ffn_w_fc, np.float32))
    w_pj = np.ascontiguousarray(np.asarray(L0_ffn_w_proj, np.float32))

    nc, outT = build(5)
    consts = _make_consts()

    wqkv_j = []
    for j in range(2):
        cols = []
        for qkv3 in range(3):
            for h in range(8 * j, 8 * j + 8):
                cols.extend(range(qkv3 * 2048 + h * 128,
                                  qkv3 * 2048 + (h + 1) * 128))
        wqkv_j.append(np.ascontiguousarray(w_qkv[:, cols]))

    in_maps = []
    for c in range(8):
        b, j = c // 2, c % 2
        in_maps.append({
            "xb": x[b],
            "xrows": np.ascontiguousarray(x[b, 1024 * j:1024 * (j + 1), :]),
            "wqkv": wqkv_j[j],
            "wo": w_o,
            "wfc": np.ascontiguousarray(w_fc[1024 * c:1024 * (c + 1), :]),
            "wpj": np.ascontiguousarray(w_pj[256 * c:256 * (c + 1), :]),
            "consts": consts,
        })

    res = run_bass_kernel_spmd(nc, in_maps, core_ids=list(range(8))).results

    out = np.empty((4, 2048, 2048), np.float32)
    for c in range(8):
        b, j = c // 2, c % 2
        out[b, 1024 * j:1024 * (j + 1), :] = np.asarray(res[c]["out"])
    return out


# version 20
# speedup vs baseline: 1.0085x; 1.0085x over previous
import math
from contextlib import ExitStack
from collections import Counter

import numpy as np
import ml_dtypes

import concourse.bass as bass
import concourse.mybir as mybir
from concourse.bass import ds

F32 = mybir.dt.float32
BF16 = mybir.dt.bfloat16
I32 = mybir.dt.int32
AF = mybir.ActivationFunctionType
ALU = mybir.AluOpType
AX = mybir.AxisListType

D = 2048
SEQ = 2048
TOK = 1024
NHL = 8
HID = 8192
INV_SQRT_DH = float(np.float32(1.0) / np.float32(np.sqrt(np.float32(128.0))))
EPS = 1e-5
SHRINKS_F32 = [float(np.float32(s)) for s in np.linspace(0.4, 1.0, 16)]
INV_SHRINKS_F32 = [float(np.float32(1.0) / np.float32(s)) for s in SHRINKS_F32]
NS = 16
QMAX = 15.0
INV_QMAX = float(np.float32(1.0) / np.float32(15.0))
QF = 512
NQT_FC = 32
NQT = 64
NBLK = QF // 64
NXT = 16
NMG = 12
NEGBIG = -1e30

CO_ID = 0
CO_ONE = 128
CO_INVS = 256
CO_NEGS = 272
CO_S = 288
CO_W = 304


CAP0 = 212832


def _arena(nc, ctx):
    CAP = CAP0
    backing = ctx.enter_context(
        nc.sbuf_tensor("arena", [128, CAP], mybir.dt.uint8))
    base = nc.lookup_mloc(backing).addr

    def place(name, shape, dtype, off):
        per_part = int(np.prod(shape[1:])) * mybir.dt.size(dtype)
        assert off % 32 == 0 and off + per_part <= CAP, (name, off, per_part)
        h = nc.virtual_sbuf_tensor(name, list(shape), dtype)
        nc._tensor(h.name, list(h.shape), h.dtype, type="SB")
        mloc = nc.lookup_mloc(h)
        mloc.allocated = True
        mloc.addr = base + off
        return h

    return place


def build(stage: int):
    nc = bass.Bass("TRN2", target_bir_lowering=False, debug=False)

    xb = nc.dram_tensor("xb", [SEQ, D], F32, kind="ExternalInput")
    wqkv = nc.dram_tensor("wqkv", [D, 3 * NHL * 128], F32, kind="ExternalInput")
    wo = nc.dram_tensor("wo", [D, D], F32, kind="ExternalInput")
    wfc = nc.dram_tensor("wfc", [1024, D], F32, kind="ExternalInput")
    wpj = nc.dram_tensor("wpj", [256, HID], F32, kind="ExternalInput")
    consts = nc.dram_tensor("consts", [128, CO_W], F32, kind="ExternalInput")
    xrows = nc.dram_tensor("xrows", [TOK, D], F32, kind="ExternalInput")

    wqkv_bf = nc.dram_tensor("wqkv_bf", [D, 3 * NHL * 128], BF16, kind="Internal")
    vals_sh = nc.dram_tensor("vals_sh", [2, NHL, 128, SEQ], BF16,
                             kind="Internal", addr_space="Shared")
    bar_in = nc.dram_tensor("bar_in", [1, 4], F32, kind="Internal")
    bar_out = nc.dram_tensor("bar_out", [2, 1, 4], F32, kind="Internal")
    x2_spill = nc.dram_tensor("x2_spill", [TOK, D], F32, kind="Internal")
    y_part = nc.dram_tensor("y_part", [TOK, D], F32, kind="Internal")
    wfc_nat = nc.dram_tensor("wfc_nat", [1024, D], BF16, kind="Internal")
    wpj_nat = nc.dram_tensor("wpj_nat", [256, HID], BF16, kind="Internal")
    wfcT_shc = nc.dram_tensor("wfcT_shc", [4, D, 256], BF16, kind="Internal")
    wpjT_shc = nc.dram_tensor("wpjT_shc", [4, 2048, 256], BF16, kind="Internal")
    wfcT_allc = nc.dram_tensor("wfcT_allc", [4, 8, D, 256], BF16, kind="Internal")
    wpjT_allc = nc.dram_tensor("wpjT_allc", [4, 8, 2048, 256], BF16, kind="Internal")

    outT = {}
    if stage >= 5:
        outT["out"] = nc.dram_tensor("out", [TOK, D], F32, kind="ExternalOutput")
    if stage == 1:
        outT["d_qkv"] = nc.dram_tensor("d_qkv", [24, 128, SEQ], BF16, kind="ExternalOutput")
    if stage == 2:
        outT["d_vals"] = nc.dram_tensor("d_vals", [NHL, 128, SEQ], BF16, kind="ExternalOutput")
    if stage == 3:
        outT["d_x2"] = nc.dram_tensor("d_x2", [TOK, D], F32, kind="ExternalOutput")
        outT["d_ln2t"] = nc.dram_tensor("d_ln2t", [16, 128, TOK], BF16, kind="ExternalOutput")
    if stage == 4:
        outT["d_x2"] = nc.dram_tensor("d_x2", [TOK, D], F32, kind="ExternalOutput")
        outT["d_wfcT"] = nc.dram_tensor("d_wfcT", [4, D, 256], BF16, kind="ExternalOutput")
        outT["d_wpjT"] = nc.dram_tensor("d_wpjT", [4, 2048, 256], BF16, kind="ExternalOutput")

    with ExitStack() as ctx:
        place = _arena(nc, ctx)
        consts_sb = place("consts_sb", [128, CO_W], F32, 0)
        identb = place("identb", [128, 128], BF16, 1216)
        onesb = place("onesb", [128, 128], BF16, 1472)
        zbw = place("zbw", [128, 2, 128], BF16, 1728)
        lnstat = place("lnstat", [128, 4, 32], F32, 2752)
        zpart = place("zpart", [128, 2, 4], F32, 3264)
        zinv = place("zinv", [128, 2, 1], F32, 3296)
        qblk = place("qblk", [128, 2, NBLK, 8], F32, 3328)
        qerr = place("qerr", [128, 2, NBLK, NS], F32, 3840)
        qmsk = place("qmsk", [128, NBLK, NS], F32, 4864)
        qsel = place("qsel", [128, NBLK, NS], F32, 5376)
        A_END = 5888
        B0 = A_END
        ln1T = place("ln1T", [128, 16, TOK], BF16, B0)
        vxs = place("vxs", [128, 2 * NHL, TOK], BF16, B0)
        hsqA = place("hsqA", [128, 32, 512], BF16, B0)
        B_END = B0 + 32768
        C0 = B_END
        wosf = place("wosf", [128, 8, D], BF16, C0)
        htmp = place("htmp", [128, 2, 512], F32, C0 + 32768)
        wstr2 = place("wstr2", [128, 2, 8, 256], BF16, C0 + 36864)
        outsb4 = place("outsb4", [128, 4, D], F32, C0)
        C_END = C0 + 65536
        qkvs = place("qkvs", [128, 24, SEQ], BF16, C0)
        D0 = C_END
        ln2T = place("ln2T", [128, 16, TOK], BF16, D0)
        D_END = D0 + 32768
        E0 = D_END
        wqbB = place("wqbB", [128, 2, 16, 256], BF16, E0)
        pcst = place("pcst", [128, 2, 3072], BF16, E0 + 16384)
        x2t = place("x2t", [128, 4, D], F32, E0)
        hsqB = place("hsqB", [128, 32, 512], BF16, E0)
        E_END = E0 + 32768
        F0 = E_END
        xs = place("xs", [128, 2, D], F32, F0)
        Etile = place("Etile", [128, 2, SEQ], BF16, F0)
        ln2st = place("ln2st", [128, 4, D], BF16, F0)
        wstr = place("wstr", [128, 2, 16, 256], BF16, F0)
        F_END = F0 + 16384
        G0 = F_END
        lntile = place("lntile", [128, 2, D], BF16, G0)
        stg = place("stg", [128, 2, 512], F32, G0)
        qrl = place("qrl", [128, 2, QF], BF16, G0 + 4096)
        deqT = place("deqT", [128, 2, 4, 128], BF16, G0 + 6144)
        G_END = G0 + 8192
        Q0 = G_END
        qt_t = place("qt_t", [128, 2, QF], F32, Q0)
        qi = place("qi", [128, 4, QF], mybir.dt.int16, Q0 + 4096)
        qq = place("qq", [128, 4, QF], F32, Q0 + 8192)
        qsq = place("qsq", [128, 4, QF], F32, Q0 + 8192)
        deqb = place("deqb", [128, 2, QF], BF16, Q0 + 16384)
        qdq = place("qdq", [128, 2, QF], mybir.dt.int16, Q0 + 16384)
        Q_END = Q0 + 18432
        assert Q_END <= CAP0
        assert Q_END <= 212832, Q_END

        pMM = ctx.enter_context(nc.psum_tensor("pMM", [128, 2048], F32))
        pSC = ctx.enter_context(nc.psum_tensor("pSC", [128, 1024], F32))
        pTa = ctx.enter_context(nc.psum_tensor("pTa", [128, 1024], BF16))
        pTb = ctx.enter_context(nc.psum_tensor("pTb", [128, 1024], BF16))

        semnames = [
            "sDVE", "sPOOL", "sLNa", "sTR", "sQKV", "sQKVE",
            "sSC", "sEXP", "sCS", "sCC", "sTRA", "sVS0", "sVS1", "sVX0", "sVX1",
            "sOP", "sSTG",
            "sFT", "sLN2a", "sQI", "sQS", "sQQ",
            "sH", "sRELU", "sF2", "sSTG2", "sFT2",
            "sX0", "sX1", "sWQ0", "sWQ1", "sQW0", "sQW1",
            "sCB0", "sCB1", "sWOS0", "sWOS1", "sXN0", "sXN1", "sXN2", "sXN3",
            "sDNW0", "sDNW1", "sQRL0", "sQRL1", "sDW0", "sDW1",
            "sFW0", "sFW1", "sF2W0", "sF2W1",
            "sXR20", "sXR21", "sXR22", "sXR23",
            "sYS0", "sYS1", "sYS2", "sYS3",
            "sOUT0", "sOUT1", "sOUT2", "sOUT3",
            "sX2S0", "sX2S1", "sX2S2", "sX2S3",
            "sDBG", "sCONST", "sCONSTg", "sWQC", "sWQS0", "sWQS1",
        ]
        S = {n: ctx.enter_context(nc.semaphore(n)) for n in semnames}

        block = ctx.enter_context(nc.Block())

        def emit(eng, E):
            C = Counter()
            EV = {}

            def inc(sname, amt=1):
                C[sname] += amt
                return C[sname]

            def mark(ins, sname, amt=1):
                if ins is not None:
                    ins.then_inc(S[sname], amt)
                return inc(sname, amt)

            def ev(key):
                pass

            def wev(key):
                sem, val = EV[key]
                if E is not None and val > 0:
                    E.wait_ge(S[sem], val)

            def weng(target, sname, val):
                if eng == target and E is not None and val > 0:
                    E.wait_ge(S[sname], val)

            def _chain(fn_name, *a, **k):
                if eng == "vector":
                    E.wait_ge(S["sDVE"], C["sDVE"])
                    ins = getattr(E, fn_name)(*a, **k)
                    ins.then_inc(S["sDVE"], 1)
                else:
                    ins = None
                inc("sDVE")
                return ins

            def _pchain(fn_name, *a, **k):
                if eng == "gpsimd":
                    E.wait_ge(S["sPOOL"], C["sPOOL"])
                    ins = getattr(E, fn_name)(*a, **k)
                    ins.then_inc(S["sPOOL"], 1)
                else:
                    ins = None
                inc("sPOOL")
                return ins

            def v_ts(*a, **k): return _chain("tensor_scalar", *a, **k)
            def v_tt(*a, **k): return _chain("tensor_tensor", *a, **k)
            def v_stt(*a, **k): return _chain("scalar_tensor_tensor", *a, **k)
            def v_red(*a, **k): return _chain("tensor_reduce", *a, **k)
            def v_cp(*a, **k): return _chain("tensor_copy", *a, **k)
            def v_rcp(*a, **k): return _chain("reciprocal", *a, **k)
            def v_sel(*a, **k): return _chain("select", *a, **k)
            def v_bns(*a, **k): return _chain("bn_stats", *a, **k)
            def v_bna(*a, **k): return _chain("bn_aggr", *a, **k)
            def v_ms(*a, **k): return _chain("memset", *a, **k)
            def p_ts(*a, **k): return _pchain("tensor_scalar", *a, **k)
            def p_red(*a, **k): return _pchain("tensor_reduce", *a, **k)

            def dve_ev(key):
                EV[key] = ("sDVE", C["sDVE"])

            def pool_ev(key):
                EV[key] = ("sPOOL", C["sPOOL"])

            def act(*a, **k):
                return E.activation(*a, **k) if eng == "scalar" else None

            def t_mm(*a, **k):
                return E.matmul(*a, **k) if eng == "tensor" else None

            def t_tr(*a, **k):
                return E.transpose(*a, **k) if eng == "tensor" else None

            def dma_s(*a, **k):
                return E.dma_start(*a, **k) if eng == "sync" else None

            def dma_a(*a, **k):
                return E.dma_start(*a, **k) if eng == "scalar" else None

            def dma_g(*a, **k):
                return E.dma_start(*a, **k) if eng == "gpsimd" else None



            q_items = []
            if stage >= 4:
                tail = []
                for qt in range(NQT):
                    q_items.append(("setup", qt, 0))
                    for t in range(NS + 6):
                        if t % 2 == 0 and 0 <= t - 4 < NS:
                            q_items.append(("st2", qt, t - 4))
                        if t % 2 == 1 and 0 <= t - 5 < NS:
                            q_items.append(("sqp", qt, t - 5))
                        if t % 2 == 0 and 0 <= t - 6 < NS:
                            q_items.append(("redp", qt, t - 6))
                        if t % 2 == 0 and 0 <= t - 2 < NS:
                            q_items.append(("clip2", qt, t - 2))
                        if t < NS:
                            q_items.append(("cvt", qt, t))
                        if t == 2 and tail:
                            q_items.append(tail.pop(0))
                        if t == 4 and tail:
                            q_items.append(tail.pop(0))
                    tail = [("trk", qt, 0), ("deq", qt, 0)]
                q_items.extend(tail)
            q_pos = [0]

            def q_src(qt):
                if qt < NQT_FC:
                    return wfc[128 * (qt // 4):128 * (qt // 4 + 1),
                               QF * (qt % 4):QF * (qt % 4 + 1)]
                lq = qt - NQT_FC
                rt, ct = lq % 2, lq // 2
                return wpj[128 * rt:128 * (rt + 1), QF * ct:QF * (ct + 1)]

            def q_nat(qt):
                if qt < NQT_FC:
                    return wfc_nat[128 * (qt // 4):128 * (qt // 4 + 1),
                                   QF * (qt % 4):QF * (qt % 4 + 1)]
                lq = qt - NQT_FC
                rt, ct = lq % 2, lq // 2
                return wpj_nat[128 * rt:128 * (rt + 1), QF * ct:QF * (ct + 1)]

            def t3(sl):
                return sl.rearrange("p (b k) -> p b k", k=64)

            def emit_q_item(it):
                kind, qt, s = it
                p = qt % 2
                if kind == "setup":
                    if eng == "gpsimd" and qt >= 2:
                        wev(("DQ", qt - 2))
                    i = dma_g(out=qt_t[:, p, :].bitcast(F32), in_=q_src(qt))
                    mark(i, "sQW%d" % (qt % 2), 16)
                    if eng == "vector":
                        E.wait_ge(S["sQW%d" % (qt % 2)], 16 * (qt // 2 + 1))
                        if qt >= 2:
                            wev(("DQ", qt - 2))
                    v_red(qblk[:, p, :, 0:1], t3(qt_t[:, p, :]), AX.X, ALU.max,
                          apply_absolute_value=True)
                    v_ts(qblk[:, p, :, 1:2], qblk[:, p, :, 0:1], 1e-8,
                         INV_QMAX, ALU.max, ALU.mult)
                    v_rcp(qblk[:, p, :, 2:3], qblk[:, p, :, 1:2])
                    v_tt(t3(qt_t[:, p, :]), t3(qt_t[:, p, :]),
                         qblk[:, p, :, 2:3].broadcast_to([128, NBLK, 64]),
                         ALU.mult)
                    dve_ev(("QSU", qt))
                elif kind == "cvt":
                    n = NS * qt + s
                    i4 = n % 4
                    if eng == "scalar":
                        wev(("QSU", qt))
                        if n >= 4:
                            wev(("QDd", (n - 4) | 1))
                    i = act(qi[:, i4, :], qt_t[:, p, :], AF.Identity,
                            bias=0.0, scale=INV_SHRINKS_F32[s])
                    mark(i, "sQI")
                elif kind == "clip2":
                    n = NS * qt + s
                    i4 = n % 4
                    if eng == "gpsimd":
                        E.wait_ge(S["sQI"], n + 2)
                    p_ts(qi[:, i4:i4 + 2, :].rearrange("p a b -> p (a b)"),
                         qi[:, i4:i4 + 2, :].rearrange("p a b -> p (a b)"),
                         QMAX, -QMAX, ALU.min, ALU.max)
                    pool_ev(("QC", n + 1))
                elif kind == "st2":
                    n = NS * qt + s
                    s4 = n % 4
                    if eng == "vector":
                        wev(("QC", n + 1))
                        if n >= 4:
                            wev(("QR", (n - 4) | 1))
                    v_stt(qq[:, s4, :], qi[:, s4, :], -SHRINKS_F32[s],
                          qt_t[:, p, :], ALU.mult, ALU.add)
                    v_stt(qq[:, s4 + 1, :], qi[:, s4 + 1, :],
                          -SHRINKS_F32[s + 1], qt_t[:, p, :],
                          ALU.mult, ALU.add)
                    dve_ev(("QDd", n + 1))
                elif kind == "sqp":
                    n = NS * qt + s
                    s4 = n % 4
                    if eng == "scalar":
                        wev(("QDd", n + 1))
                    i = act(qsq[:, s4:s4 + 2, :].rearrange("p a b -> p (a b)"),
                            qq[:, s4:s4 + 2, :].rearrange("p a b -> p (a b)"),
                            AF.Square)
                    mark(i, "sQS")
                elif kind == "redp":
                    n = NS * qt + s
                    s4 = n % 4
                    if eng == "vector":
                        E.wait_ge(S["sQS"], C["sQS"])
                        if s == 0 and qt >= 2:
                            wev(("QT", qt - 2))
                    v_red(qerr[:, p, :, s:s + 2]
                          .rearrange("p b a -> p a b"),
                          qsq[:, s4:s4 + 2, :]
                          .rearrange("p a (b k) -> p a b k", k=64),
                          AX.X, ALU.add)
                    dve_ev(("QR", n + 1))
                elif kind == "trk":
                    if eng == "vector":
                        E.wait_ge(S["sCONST"], 16)
                        wev(("QR", NS * qt + NS - 1))
                    err = qerr[:, p, :, :]
                    v_red(qblk[:, p, :, 3:4], err, AX.X, ALU.min)
                    v_tt(qmsk[:, :, :], err,
                         qblk[:, p, :, 3:4].broadcast_to([128, NBLK, NS]),
                         ALU.is_equal)
                    v_tt(qsel[:, :, :], qmsk[:, :, :],
                         consts_sb[:, CO_INVS:CO_INVS + NS]
                         .rearrange("p (o s) -> p o s", o=1).broadcast_to([128, NBLK, NS]),
                         ALU.mult)
                    v_red(qblk[:, p, :, 4:5], qsel[:, :, :], AX.X, ALU.max)
                    v_tt(qmsk[:, :, :],
                         consts_sb[:, CO_INVS:CO_INVS + NS]
                         .rearrange("p (o s) -> p o s", o=1)
                         .broadcast_to([128, NBLK, NS]),
                         qblk[:, p, :, 4:5].broadcast_to([128, NBLK, NS]),
                         ALU.is_equal)
                    v_tt(qsel[:, :, :], qmsk[:, :, :],
                         consts_sb[:, CO_S:CO_S + NS]
                         .rearrange("p (o s) -> p o s", o=1)
                         .broadcast_to([128, NBLK, NS]),
                         ALU.mult)
                    v_red(qblk[:, p, :, 5:6], qsel[:, :, :], AX.X, ALU.max)
                    v_tt(qblk[:, p, :, 6:7], qblk[:, p, :, 1:2],
                         qblk[:, p, :, 5:6], ALU.mult)
                    dve_ev(("QT", qt))
                elif kind == "deq":
                    if eng == "vector":
                        if qt >= 2:
                            E.wait_ge(S["sDNW%d" % (qt % 2)], 16 * (qt // 2))
                    v_tt(t3(qt_t[:, p, :]), t3(qt_t[:, p, :]),
                         qblk[:, p, :, 4:5].broadcast_to([128, NBLK, 64]),
                         ALU.mult)
                    dve_ev(("U2", qt))
                    if eng == "scalar":
                        wev(("U2", qt))
                        if qt >= 1:
                            E.wait_ge(S["sDNW%d" % ((qt - 1) % 2)],
                                      16 * ((qt - 1) // 2 + 1))
                    i = act(qdq[:, 1 - p, :], qt_t[:, p, :], AF.Identity,
                            bias=0.0, scale=1.0)
                    mark(i, "sQQ")
                    if eng == "gpsimd":
                        E.wait_ge(S["sQQ"], C["sQQ"])
                    p_ts(qdq[:, 1 - p, :], qdq[:, 1 - p, :], QMAX, -QMAX,
                         ALU.min, ALU.max)
                    pool_ev(("QC2", qt))
                    if eng == "vector":
                        wev(("QC2", qt))
                    v_tt(t3(deqb[:, p, :]), t3(qdq[:, 1 - p, :]),
                         qblk[:, p, :, 6:7].broadcast_to([128, NBLK, 64]),
                         ALU.mult)
                    dve_ev(("DQ", qt))
                    if eng == "gpsimd":
                        wev(("DQ", qt))
                    i = dma_g(out=q_nat(qt), in_=deqb[:, p, :])
                    mark(i, "sDNW%d" % (qt % 2), 16)

            def q_step(k):
                if stage < 4:
                    return
                n = 0
                while n < k and q_pos[0] < len(q_items):
                    emit_q_item(q_items[q_pos[0]])
                    q_pos[0] += 1
                    n += 1

            def q_flush_until(qt_lim):
                if stage < 4:
                    return
                last = -1
                for ii in range(len(q_items)):
                    if q_items[ii][1] < qt_lim:
                        last = ii
                while q_pos[0] <= last:
                    emit_q_item(q_items[q_pos[0]])
                    q_pos[0] += 1

            LQG = [0]
            lateT_hist = []

            def lateT(qt_list, shW_of):
                for qt in qt_list:
                    g = LQG[0]
                    sl = g % 2
                    if eng == "gpsimd":
                        E.wait_ge(S["sDNW%d" % (qt % 2)], 16 * (qt // 2 + 1))
                        if g >= 2:
                            wev(("DT", lateT_hist[g - 2]))
                    i = dma_g(out=qrl[:, sl, :], in_=q_nat(qt))
                    mark(i, "sQRL%d" % sl, 16)
                    G = TRG[0]
                    pT = pTa if G % 2 == 0 else pTb
                    if eng == "tensor":
                        E.wait_ge(S["sQRL%d" % sl], C["sQRL%d" % sl])
                        if G >= 2:
                            wev(("TRE", G - 2))
                    for k in range(4):
                        i = t_tr(pT[:, 128 * k:128 * (k + 1)],
                                 qrl[:, sl, 128 * k:128 * (k + 1)],
                                 identb[:, :])
                        if k == 3:
                            mark(i, "sTR")
                    if eng == "vector":
                        E.wait_ge(S["sTR"], C["sTR"])
                        E.wait_ge(S["sDW%d" % sl], C["sDW%d" % sl])
                    v_cp(deqT[:, sl, :, :],
                         pT[:, 0:512].rearrange("p (a b) -> p a b", b=128))
                    dve_ev(("DT", qt))
                    dve_ev(("TRE", G))
                    if eng == "gpsimd":
                        wev(("DT", qt))
                    i = dma_g(out=shW_of(qt), in_=deqT[:, sl, :, :])
                    mark(i, "sDW%d" % sl, 16)
                    TRG[0] += 1
                    lateT_hist.append(qt)
                    LQG[0] += 1

            def wfc_shW(qt):
                cj = qt // 8
                c0 = 128 * ((qt // 4) % 2)
                f0 = QF * (qt % 4)
                return wfcT_shc[cj, f0:f0 + 512, c0:c0 + 128] \
                    .rearrange("(k p) c -> p k c", p=128)

            def wpj_shW(qt):
                lq = qt - NQT_FC
                cj = lq // 8
                h0 = 512 * ((lq // 2) % 4)
                c0 = 128 * (lq % 2)
                return wpjT_shc[cj, h0:h0 + 512, c0:c0 + 128] \
                    .rearrange("(k p) c -> p k c", p=128)

            CCN = {}

            def emit_cc(name, ins_l, outs_l, groups=None):
                if eng == "gpsimd":
                    E.wait_ge(S["sDW0"], C["sDW0"])
                    E.wait_ge(S["sDW1"], C["sDW1"])
                i = (E.collective_compute(
                    "AllGather", ALU.bypass,
                    replica_groups=(groups if groups is not None
                                    else [[0, 1, 2, 3, 4, 5, 6, 7]]),
                    ins=ins_l, outs=outs_l)
                    if eng == "gpsimd" else None)
                mark(i, "sCC")
                CCN[name] = C["sCC"]

            i = dma_s(out=consts_sb[:, :], in_=consts[:, :])
            mark(i, "sCONST", 16)
            weng("sync", "sCONST", 16)
            i = dma_s(out=bar_in[:, :], in_=consts_sb[0:1, 0:4])
            mark(i, "sCONST", 16)
            weng("gpsimd", "sCONST", 16)
            i = dma_g(out=identb[:, :], in_=consts_sb[:, CO_ID:CO_ID + 128])
            mark(i, "sCONSTg", 16)
            i = dma_g(out=onesb[:, :], in_=consts_sb[:, CO_ONE:CO_ONE + 128])
            mark(i, "sCONSTg", 16)

            for k in range(16):
                if eng == "gpsimd" and k >= 2:
                    E.wait_ge(S["sWQS%d" % (k % 2)], 16 * (k // 2))
                srcv = wqkv[128 * k:128 * (k + 1), :]
                stv = pcst[:, k % 2, :]
                i = dma_g(out=stv, in_=srcv)
                mark(i, "sWQC", 16)
                weng("gpsimd", "sWQC", C["sWQC"])
                i = dma_g(out=wqkv_bf[128 * k:128 * (k + 1), :], in_=stv)
                mark(i, "sWQS%d" % (k % 2), 16)

            q_step(60)

            TRG = [0]
            trev = {}

            def emit_ln1_tile(i_t, xsrc, stoff, sema, lsl, evtag):
                for sb4 in range(4):
                    v_bns(lnstat[:, lsl, 6 * sb4:6 * sb4 + 6],
                          xsrc[:, 512 * sb4:512 * (sb4 + 1)])
                v_bna(lnstat[:, lsl, 24:26], lnstat[:, lsl, 0:24])
                v_ts(lnstat[:, lsl, 25:26], lnstat[:, lsl, 25:26],
                     EPS, None, ALU.add)
                dve_ev((evtag, i_t, "e"))
                if eng == "scalar":
                    wev((evtag, i_t, "e"))
                i = act(lnstat[:, lsl, 26:27], lnstat[:, lsl, 25:26], AF.Sqrt)
                mark(i, sema)
                if eng == "vector":
                    E.wait_ge(S[sema], C[sema])
                v_rcp(lnstat[:, lsl, 27:28], lnstat[:, lsl, 26:27])
                v_stt(lnstat[:, lsl, 28:29], lnstat[:, lsl, 24:25],
                      -1.0, lnstat[:, lsl, 27:28], ALU.mult, ALU.mult)
                dve_ev((evtag, i_t, "n"))

            def ln1_tile(i_t):
                if eng == "sync" and i_t >= 2:
                    E.wait_ge(S["sLNa"], 2 * (i_t - 2) + 2)
                i = dma_s(out=xs[:, i_t % 2, :], in_=xb[128 * i_t:128 * (i_t + 1), :])
                mark(i, "sX%d" % (i_t % 2), 16)
                if eng == "vector":
                    E.wait_ge(S["sX%d" % (i_t % 2)], 16 * (i_t // 2 + 1))
                emit_ln1_tile(i_t, xs[:, i_t % 2, :], 0, "sLNa", i_t % 2, "L1")
                if eng == "scalar":
                    wev(("L1", i_t, "n"))
                    if i_t >= 2:
                        E.wait_ge(S["sTR"], 2 * (i_t - 2) + 2)
                i = act(lntile[:, i_t % 2, :], xs[:, i_t % 2, :], AF.Identity,
                        bias=lnstat[:, i_t % 2, 28:29],
                        scale=lnstat[:, i_t % 2, 27:28])
                mark(i, "sLNa")

            def ln1T_tile(i_t, half):
                for g in range(2):
                    G = TRG[0]
                    pT = pTa if G % 2 == 0 else pTb
                    if eng == "tensor":
                        E.wait_ge(S["sLNa"], 2 * i_t + 2)
                        E.wait_ge(S["sCONSTg"], 32)
                        if G >= 2:
                            wev(("TRE", G - 2))
                    for k in range(8):
                        fc = 8 * g + k
                        i = t_tr(pT[:, 128 * k:128 * (k + 1)],
                                 lntile[:, i_t % 2, 128 * fc:128 * (fc + 1)],
                                 identb[:, :])
                        if k == 7:
                            mark(i, "sTR")
                    if eng == "vector":
                        E.wait_ge(S["sTR"], G + 1)
                        if G == 16:
                            E.wait_ge(S["sQKV"], 24)
                    tloc = (i_t % 8) * 128
                    v_cp(ln1T[:, 8 * g:8 * g + 8, tloc:tloc + 128],
                         pT[:, :].rearrange("p (a b) -> p a b", b=128))
                    dve_ev(("TRE", G))
                    TRG[0] += 1

            def qkv_half(half):
                for mg in range(NMG):
                    gmg = half * NMG + mg
                    for k in range(16):
                        if eng == "sync":
                            if gmg == 0:
                                E.wait_ge(S["sWQS%d" % (k % 2)],
                                          16 * (k // 2 + 1))
                            if gmg == 1 and k == 0:
                                E.wait_ge(S["sWQS0"], 16 * 8)
                                E.wait_ge(S["sWQS1"], 16 * 8)
                            if gmg >= 2 and k == 0:
                                E.wait_ge(S["sQKV"], 2 * (gmg - 1))
                        i = dma_s(out=wqbB[:, gmg % 2, k, :],
                                  in_=wqkv_bf[128 * k:128 * (k + 1),
                                              256 * mg:256 * (mg + 1)])
                        mark(i, "sWQ%d" % (gmg % 2), 16)
                    for m2 in range(2):
                        m = 2 * mg + m2
                        Mg = 24 * half + m
                        bp = Mg % 2
                        if eng == "tensor":
                            if m2 == 0:
                                E.wait_ge(S["sWQ%d" % (gmg % 2)],
                                          256 * (gmg // 2 + 1))
                                wev(("TRE", 16 * (half + 1) - 1))
                            if Mg >= 2:
                                E.wait_ge(S["sQKVE"], 2 * Mg - 2)
                        for k in range(16):
                            for n in range(2):
                                i = t_mm(pMM[:, 1024 * bp + 512 * n:
                                             1024 * bp + 512 * (n + 1)],
                                         wqbB[:, gmg % 2, k, 128 * m2:128 * (m2 + 1)],
                                         ln1T[:, k, 512 * n:512 * (n + 1)],
                                         start=(k == 0), stop=(k == 15))
                                if k == 15 and n == 1:
                                    mark(i, "sQKV")
                        scl = INV_SQRT_DH if m < 8 else 1.0
                        for n in range(2):
                            if eng == "scalar":
                                E.wait_ge(S["sQKV"], Mg + 1)
                            i = act(qkvs[:, m, 1024 * half + 512 * n:
                                          1024 * half + 512 * (n + 1)],
                                    pMM[:, 1024 * bp + 512 * n:
                                        1024 * bp + 512 * (n + 1)],
                                    AF.Identity, bias=0.0, scale=scl)
                            mark(i, "sQKVE")
                    q_step(26)

            for i_t in range(10):
                ln1_tile(i_t)
                if i_t < 8:
                    ln1T_tile(i_t, 0)
            qkv_half(0)
            for i_t in range(8, 16):
                if 10 <= i_t + 2 < 16:
                    ln1_tile(i_t + 2)
                ln1T_tile(i_t, 1)
            qkv_half(1)

            if stage == 1:
                for m in range(24):
                    weng("sync", "sQKVE", 96)
                    i = dma_s(out=outT["d_qkv"][m, :, :], in_=qkvs[:, m, :])
                    mark(i, "sDBG", 16)
                weng("sync", "sDBG", 16 * 24)
                return

            if eng == "sync":
                pidv = E.partition_id()
                offmy = (pidv % 2) * 1024
            pend_cs = [None]

            def colsum_emit(slot):
                h_, qt_ = slot // 16, slot % 16
                if eng == "tensor":
                    wev(("ZB", slot))
                    if qt_ == 0 and h_ >= 1:
                        wev(("VAL", h_ - 1))
                for kt in range(4):
                    i = t_mm(pMM[:, 512 * kt:512 * (kt + 1)],
                             zbw[:, slot % 2, :],
                             Etile[:, qt_ % 2, 512 * kt:512 * (kt + 1)],
                             start=(qt_ == 0), stop=(qt_ == 15))
                    if kt == 3:
                        mark(i, "sCS")

            def head_finish(h):
                if eng == "vector":
                    E.wait_ge(S["sCS"], 16 * (h + 1))
                for kt in range(4):
                    v_tt(qkvs[:, 16 + h, 512 * kt:512 * (kt + 1)],
                         qkvs[:, 16 + h, 512 * kt:512 * (kt + 1)],
                         pMM[:, 512 * kt:512 * (kt + 1)], ALU.mult)
                dve_ev(("VAL", h))
                if eng == "sync":
                    wev(("VAL", h))
                i = dma_s(out=vals_sh[ds(pidv % 2, 1), h, :, :]
                          if eng == "sync" else None,
                          in_=qkvs[:, 16 + h, :])
                mark(i, "sVS%d" % (h % 2), 16)

            for h in range(NHL):
                if h >= 1:
                    head_finish(h - 1)
                for qt in range(16):
                    slot = 16 * h + qt
                    for kq in range(4):
                        qidx = 4 * slot + kq
                        sb5 = 512 * (qidx % 2)
                        if eng == "tensor":
                            if qidx == 0:
                                E.wait_ge(S["sQKVE"], 96)
                            if qidx >= 2:
                                E.wait_ge(S["sEXP"], qidx - 1)
                        i = t_mm(pSC[:, sb5:sb5 + 512],
                                 qkvs[:, h, 128 * qt:128 * (qt + 1)],
                                 qkvs[:, 8 + h, 512 * kq:512 * (kq + 1)],
                                 start=True, stop=True)
                        mark(i, "sSC")
                        if eng == "scalar":
                            E.wait_ge(S["sSC"], qidx + 1)
                            if slot >= 2 and kq == 0:
                                E.wait_ge(S["sCS"], slot - 1)
                        i = act(Etile[:, qt % 2, 512 * kq:512 * (kq + 1)],
                                pSC[:, sb5:sb5 + 512], AF.Exp,
                                bias=0.0, scale=1.0,
                                accum_out=zpart[:, slot % 2, kq:kq + 1])
                        mark(i, "sEXP")
                    if eng == "vector":
                        E.wait_ge(S["sEXP"], 4 * slot + 4)
                        if slot >= 2:
                            E.wait_ge(S["sCS"], slot - 1)
                    v_red(zinv[:, slot % 2, 0:1],
                          zpart[:, slot % 2, :]
                          .rearrange("p (o s) -> p o s", o=1),
                          AX.X, ALU.add)
                    v_rcp(zinv[:, slot % 2, 0:1], zinv[:, slot % 2, 0:1])
                    v_ts(zbw[:, slot % 2, :], onesb[:, 0:128],
                         zinv[:, slot % 2, 0:1], None, ALU.mult)
                    dve_ev(("ZB", slot))
                    if pend_cs[0] is not None:
                        colsum_emit(pend_cs[0])
                    pend_cs[0] = slot
                if stage >= 4 and h == 4:
                    q_flush_until(8)
                    lateT(list(range(0, 8)), wfc_shW)
                    emit_cc("wfc0", [wfcT_shc[0, :, :]],
                            [wfcT_allc[0, :, :, :]])
            if pend_cs[0] is not None:
                colsum_emit(pend_cs[0])
                pend_cs[0] = None
            head_finish(NHL - 1)
            if eng == "gpsimd":
                E.wait_ge(S["sCONST"], 32)
                E.wait_ge(S["sVS0"], 16 * 4)
                E.wait_ge(S["sVS1"], 16 * 4)
            emit_cc("bar", [bar_in[:, :]], [bar_out[:, :, :]],
                    groups=[[0, 1], [2, 3], [4, 5], [6, 7]])
            if eng == "gpsimd":
                pidv_g = E.partition_id()
                offmy_g = (pidv_g % 2) * 1024
                E.wait_ge(S["sCC"], CCN["bar"])
            for r in range(2):
                for hl in range(NHL):
                    i = dma_g(out=vxs[:, 8 * r + hl, :],
                              in_=vals_sh[r, hl, :, ds(offmy_g, TOK)]
                              if eng == "gpsimd" else None)
                    mark(i, "sVX%d" % (hl % 2), 16)

            if stage == 2:
                for hl in range(NHL):
                    if eng == "sync":
                        wev(("VAL", NHL - 1))
                    i = dma_s(out=outT["d_vals"][hl, :, :], in_=qkvs[:, 16 + hl, :])
                    mark(i, "sDBG", 16)
                weng("sync", "sDBG", 16 * NHL)
                return

            q_step(40)

            FG = [0]
            fgev = {}
            for n in range(2):
                for p4 in range(4):
                    tl = 4 * n + p4
                    if eng == "sync":
                        E.wait_ge(S["sQKV"], 48)
                        if n == 1:
                            E.wait_ge(S["sLN2a"], 2 * (tl - 4) + 2)
                            E.wait_ge(S["sX2S%d" % p4], 16)
                    i = dma_s(out=x2t[:, p4, :],
                              in_=xrows[128 * tl:128 * (tl + 1), :])
                    mark(i, "sXN%d" % p4, 16)
                for p8 in range(2):
                    wk = 2 * n + p8
                    for hc in range(8):
                        if eng == "gpsimd":
                            if wk == 0 and hc == 0:
                                E.wait_ge(S["sCS"], 128)
                            if wk >= 1 and hc == 0:
                                E.wait_ge(S["sOP"], 16 * wk)
                        i = dma_g(out=wosf[:, hc, :],
                                  in_=wo[1024 * p8 + 128 * hc:
                                         1024 * p8 + 128 * (hc + 1), :])
                        mark(i, "sWOS%d" % (wk % 2), 16)
                    for fc in range(16):
                        gidx = wk * 16 + fc
                        r2 = gidx % 2
                        if eng == "tensor":
                            if fc == 0:
                                E.wait_ge(S["sVX0"], 16 * NHL)
                                E.wait_ge(S["sVX1"], 16 * NHL)
                                E.wait_ge(S["sWOS%d" % (wk % 2)],
                                          128 * (wk // 2 + 1))
                            if gidx >= 2:
                                E.wait_ge(S["sSTG"], gidx - 1)
                            if gidx < 2:
                                E.wait_ge(S["sEXP"], 512)
                        for hc in range(8):
                            rhs = vxs[:, 8 * p8 + hc, 512 * n:512 * (n + 1)]
                            i = t_mm(pSC[:, 512 * r2:512 * (r2 + 1)],
                                     wosf[:, hc, 128 * fc:128 * (fc + 1)],
                                     rhs, start=(hc == 0), stop=(hc == 7))
                            if hc == 7:
                                mark(i, "sOP")
                        if eng == "scalar":
                            E.wait_ge(S["sOP"], gidx + 1)
                            if gidx >= 2:
                                E.wait_ge(S["sFT"], 4 * (gidx - 1))
                        i = act(stg[:, r2, :], pSC[:, 512 * r2:512 * (r2 + 1)],
                                AF.Identity, bias=0.0, scale=1.0)
                        mark(i, "sSTG")
                        G2 = FG[0]
                        pT32 = pMM[:, 512 * (G2 % 2):512 * (G2 % 2) + 512]
                        if eng == "tensor":
                            E.wait_ge(S["sSTG"], gidx + 1)
                            if G2 >= 2:
                                wev(("FG", G2 - 2))
                            if G2 < 2:
                                wev(("VAL", NHL - 1))
                        for p4 in range(4):
                            i = t_tr(pT32[:, 128 * p4:128 * (p4 + 1)],
                                     stg[:, r2, 128 * p4:128 * (p4 + 1)],
                                     consts_sb[:, CO_ID:CO_ID + 128])
                            if p4 == 3:
                                mark(i, "sFT", 4)
                        for p4 in range(4):
                            tl = 4 * n + p4
                            if eng == "vector":
                                E.wait_ge(S["sFT"], 4 * gidx + 4)
                                if p8 == 0 and fc == 0:
                                    E.wait_ge(S["sXN%d" % p4], 16 * (n + 1))
                            v_tt(x2t[:, p4, 128 * fc:128 * (fc + 1)],
                                 x2t[:, p4, 128 * fc:128 * (fc + 1)],
                                 pT32[:, 128 * p4:128 * (p4 + 1)],
                                 ALU.add)
                        dve_ev(("FG", G2))
                        FG[0] += 1
                        q_step(7)
                for p4 in range(4):
                    t = 4 * n + p4
                    if eng == "sync":
                        wev(("FG", (2 * n + 1) * 16 + 15))
                    i = dma_s(out=x2_spill[128 * t:128 * (t + 1), :],
                              in_=x2t[:, p4, :])
                    mark(i, "sX2S%d" % p4, 16)
                if n == 0:
                    for p4 in range(4):
                        t = p4
                        if eng == "vector":
                            wev(("FG", 31))
                        emit_ln1_tile(t, x2t[:, p4, :], 0, "sLN2a",
                                      2 + t % 2, "L2")
                        if eng == "scalar":
                            wev(("L2", t, "n"))
                            E.wait_ge(S["sCS"], 128)
                        i = act(ln2st[:, t % 4, :], x2t[:, p4, :], AF.Identity,
                                bias=lnstat[:, 2 + t % 2, 28:29],
                                scale=lnstat[:, 2 + t % 2, 27:28])
                        mark(i, "sLN2a")
                q_step(50)
                if stage >= 4 and n == 0:
                    q_flush_until(16)
                    lateT(list(range(8, 16)), wfc_shW)
                    emit_cc("wfc1", [wfcT_shc[1, :, :]],
                            [wfcT_allc[1, :, :, :]])

            def ln2T_tiles(trange):
                for t in trange:
                    for g in range(2):
                        G = TRG[0]
                        pT = pTa if G % 2 == 0 else pTb
                        if eng == "tensor":
                            E.wait_ge(S["sLN2a"], 2 * t + 2)
                            E.wait_ge(S["sOP"], 64)
                            if G >= 2:
                                wev(("TRE", G - 2))
                        for k in range(8):
                            fc = 8 * g + k
                            i = t_tr(pT[:, 128 * k:128 * (k + 1)],
                                     ln2st[:, t % 4, 128 * fc:128 * (fc + 1)],
                                     identb[:, :])
                            if k == 7:
                                mark(i, "sTR")
                            else:
                                mark(i, "sTRA")
                        if eng == "vector":
                            E.wait_ge(S["sTR"], G + 1)
                            if G == 32:
                                E.wait_ge(S["sOP"], 64)
                        tloc = 128 * t
                        v_cp(ln2T[:, 8 * g:8 * g + 8, tloc:tloc + 128],
                             pT[:, :].rearrange("p (a b) -> p a b", b=128))
                        dve_ev(("TRE", G))
                        TRG[0] += 1

            ln2T_tiles(range(0, 4))
            q_step(60)
            for p4 in range(4):
                t = 4 + p4
                if eng == "vector":
                    wev(("FG", 63))
                emit_ln1_tile(t, x2t[:, p4, :], 0, "sLN2a", 2 + t % 2, "L2")
                if eng == "scalar":
                    wev(("L2", t, "n"))
                    E.wait_ge(S["sTR"], 32 + 2 * (t - 4) + 2)
                i = act(ln2st[:, t % 4, :], x2t[:, p4, :], AF.Identity,
                        bias=lnstat[:, 2 + t % 2, 28:29],
                        scale=lnstat[:, 2 + t % 2, 27:28])
                mark(i, "sLN2a")
            q_step(60)
            ln2T_tiles(range(4, 8))

            if stage == 3:
                for _p4 in range(4):
                    weng("sync", "sX2S%d" % _p4, 32)
                for t in range(8):
                    i = dma_s(out=outT["d_x2"][128 * t:128 * (t + 1), :],
                              in_=x2_spill[128 * t:128 * (t + 1), :])
                    mark(i, "sDBG", 16)
                for fcg in range(16):
                    if eng == "sync":
                        wev(("TRE", 47))
                    i = dma_s(out=outT["d_ln2t"][fcg, :, :], in_=ln2T[:, fcg, :])
                    mark(i, "sDBG", 16)
                weng("sync", "sDBG", 16 * 24)
                return

            if stage == 4:
                for cj in range(2, 4):
                    q_flush_until(8 * (cj + 1))
                    lateT(list(range(8 * cj, 8 * (cj + 1))), wfc_shW)
                    emit_cc("wfc%d" % cj,
                            [wfcT_shc[cj, :, :]], [wfcT_allc[cj, :, :, :]])
                for cj in range(4):
                    q_flush_until(NQT_FC + 8 * (cj + 1))
                    lateT(list(range(NQT_FC + 8 * cj, NQT_FC + 8 * (cj + 1))),
                          wpj_shW)
                    emit_cc("wpj%d" % cj,
                            [wpjT_shc[cj, :, :]], [wpjT_allc[cj, :, :, :]])
                for _p4 in range(4):
                    weng("sync", "sX2S%d" % _p4, 32)
                for t in range(8):
                    i = dma_s(out=outT["d_x2"][128 * t:128 * (t + 1), :],
                              in_=x2_spill[128 * t:128 * (t + 1), :])
                    mark(i, "sDBG", 16)
                weng("sync", "sDW0", 16 * NQT // 2)
                weng("sync", "sDW1", 16 * NQT // 2)
                for cj in range(4):
                    for r in range(4):
                        i = dma_s(out=outT["d_wfcT"][cj, 512 * r:512 * (r + 1), :],
                                  in_=wfcT_shc[cj, 512 * r:512 * (r + 1), :])
                        mark(i, "sDBG", 16)
                    for r in range(4):
                        i = dma_s(out=outT["d_wpjT"][cj, 512 * r:512 * (r + 1), :],
                                  in_=wpjT_shc[cj, 512 * r:512 * (r + 1), :])
                        mark(i, "sDBG", 16)
                weng("sync", "sDBG", 16 * 40)
                return

            ghx_hist = []
            gitc = [0]
            lfgc = [0]
            gLc = [0]
            gfcc = [0]
            for hh in range(2):
              for th in range(2):
                for itl in range(16):
                    j, rl = itl // 4, itl % 4
                    hg = 4 * (4 * hh + rl) + j
                    hgl = 4 * rl + j
                    git = gitc[0]
                    gitc[0] += 1
                    it = git
                    for fc in range(16):
                        if eng == "sync":
                            if git == 0 and fc == 0:
                                E.wait_ge(S["sTR"], C["sTR"])
                                E.wait_ge(S["sTRA"], C["sTRA"])
                            if hh == 0 and th == 0 and itl % 4 == 0 \
                                    and fc == 0:
                                E.wait_ge(S["sCC"], CCN["wfc%d" % j])
                            if git >= 2 and fc == 0:
                                E.wait_ge(S["sH"], 2 * (git - 1))
                        i = dma_s(out=wstr[:, git % 2, fc, :],
                                  in_=wfcT_allc[j, 4 * hh + rl,
                                                128 * fc:128 * (fc + 1), :])
                        mark(i, "sFW%d" % (git % 2), 16)
                    for cc in range(2):
                        hcx = 2 * hgl + cc
                        ghx = 64 * hh + 32 * th + hcx
                        ghx_hist.append(ghx)
                        gx = 2 * git + cc
                        bk = 2 * (git % 2) + cc
                        if eng == "tensor":
                            if cc == 0:
                                E.wait_ge(S["sFW%d" % (git % 2)],
                                          256 * (git // 2 + 1))
                                if git == 0:
                                    wev(("TRE", 47))
                                    for _p4 in range(4):
                                        E.wait_ge(S["sX2S%d" % _p4], 32)
                                    E.wait_ge(S["sSTG"], 64)
                                if git in (32, 33):
                                    E.wait_ge(S["sSTG2"], 32)
                                if git >= 2:
                                    wev(("HE", ghx_hist[2 * (git - 2) + 1]))
                        i = None
                        for fc in range(16):
                            i = t_mm(pMM[:, 512 * bk:512 * (bk + 1)],
                                     wstr[:, git % 2, fc,
                                          128 * cc:128 * (cc + 1)],
                                     ln2T[:, fc, 512 * th:512 * (th + 1)],
                                     start=(fc == 0), stop=(fc == 15))
                        mark(i, "sH")
                        if eng == "scalar":
                            E.wait_ge(S["sH"], 2 * git + cc + 1)
                            if gx >= 2:
                                wev(("HE", ghx_hist[gx - 2]))
                        i = act(htmp[:, cc, :],
                                pMM[:, 512 * bk:512 * (bk + 1)],
                                AF.Relu, bias=0.0, scale=1.0)
                        mark(i, "sRELU")
                        if eng == "vector":
                            E.wait_ge(S["sRELU"], C["sRELU"])
                            if git == 0 and cc == 0:
                                for _p4 in range(4):
                                    E.wait_ge(S["sX2S%d" % _p4], 32)
                            if git == 32 and cc == 0:
                                E.wait_ge(S["sF2"], 64)
                        dst = (hsqA[:, hcx, :] if th == 0
                               else hsqB[:, hcx, :])
                        v_tt(dst, htmp[:, cc, :], htmp[:, cc, :],
                             ALU.mult)
                        dve_ev(("HE", ghx))
                    if hh == 0:
                        q_step(38)
                        if th == 0 and itl == 1:
                            q_flush_until(24)
                            lateT(list(range(16, 24)), wfc_shW)
                            emit_cc("wfc2", [wfcT_shc[2, :, :]],
                                    [wfcT_allc[2, :, :, :]])
                        elif th == 0 and itl == 5:
                            q_flush_until(32)
                            lateT(list(range(24, 32)), wfc_shW)
                            emit_cc("wfc3", [wfcT_shc[3, :, :]],
                                    [wfcT_allc[3, :, :, :]])
                        elif (th, itl) in ((0, 9), (0, 13)):
                            cjp = (itl - 9) // 4
                            q_flush_until(NQT_FC + 8 * (cjp + 1))
                            lateT(list(range(NQT_FC + 8 * cjp,
                                             NQT_FC + 8 * (cjp + 1))),
                                  wpj_shW)
                            emit_cc("wpj%d" % cjp, [wpjT_shc[cjp, :, :]],
                                    [wpjT_allc[cjp, :, :, :]])
                    else:
                        q_step(30)
                        if th == 0 and itl in (1, 5):
                            cjp = 2 + (itl - 1) // 4
                            q_flush_until(NQT_FC + 8 * (cjp + 1))
                            lateT(list(range(NQT_FC + 8 * cjp,
                                             NQT_FC + 8 * (cjp + 1))),
                                  wpj_shW)
                            emit_cc("wpj%d" % cjp, [wpjT_shc[cjp, :, :]],
                                    [wpjT_allc[cjp, :, :, :]])
              for th in range(2):
                for p4 in range(4):
                    tl = 4 * th + p4
                    if eng == "sync":
                        if hh == 0 and th == 0:
                            E.wait_ge(S["sOP"], 64)
                            E.wait_ge(S["sX2S%d" % p4], 32)
                        elif hh == 0:
                            E.wait_ge(S["sYS%d" % p4], 16)
                        else:
                            E.wait_ge(S["sYS%d" % p4], 32)
                            if th == 1:
                                E.wait_ge(S["sOUT%d" % p4], 16)
                    i = dma_s(out=outsb4[:, p4, :],
                              in_=(x2_spill if hh == 0 else y_part)
                              [128 * tl:128 * (tl + 1), :])
                    mark(i, "sXR2%d" % p4, 16)
                for fg in range(8):
                    lfg = lfgc[0]
                    lfgc[0] += 1
                    for hcgl in range(4):
                        cw = 2 * hh + hcgl // 2
                        L = gLc[0]
                        gLc[0] += 1
                        ei = "sync" if L % 2 == 0 else "scalar"
                        if eng == ei:
                            if th == 0 and fg == 0:
                                E.wait_ge(S["sCC"], CCN["wpj%d" % cw])
                            if L >= 2:
                                E.wait_ge(S["sF2"], L - 1)
                        i = (E.dma_start(
                            out=wstr2[:, L % 2, :, :],
                            in_=wpjT_allc[cw, fg,
                                          1024 * (hcgl % 2):
                                          1024 * (hcgl % 2 + 1), :]
                            .rearrange("(k p) c -> p k c", p=128))
                            if eng == ei else None)
                        mark(i, "sF2W%d" % (L % 2), 16)
                        bk2 = 2 * (fg % 2)
                        if eng == "tensor":
                            E.wait_ge(S["sF2W%d" % (L % 2)], 16 * (L // 2 + 1))
                            if hcgl == 0:
                                if fg == 0:
                                    wev(("HE", 64 * hh + 31))
                                    wev(("HE", 64 * hh + 63))
                                if lfg >= 2:
                                    E.wait_ge(S["sSTG2"], 2 * (lfg - 1))
                        for hc8 in range(8):
                            hc = 8 * hcgl + hc8
                            for cc in range(2):
                                i = t_mm(pMM[:, 512 * (bk2 + cc):
                                             512 * (bk2 + cc + 1)],
                                         wstr2[:, L % 2, hc8,
                                               128 * cc:128 * (cc + 1)],
                                         hsqA[:, hc, :] if th == 0
                                         else hsqB[:, hc, :],
                                         start=(hc == 0), stop=(hc == 31))
                                if hc8 == 7 and cc == 1:
                                    mark(i, "sF2")
                    for cc in range(2):
                        fc = 2 * fg + cc
                        gfc = gfcc[0]
                        gfcc[0] += 1
                        bk2 = 2 * (fg % 2)
                        if eng == "scalar":
                            E.wait_ge(S["sF2"], 4 * (lfg + 1))
                            if gfc >= 2:
                                E.wait_ge(S["sFT2"], 4 * (gfc - 1))
                        i = act(stg[:, gfc % 2, :],
                                pMM[:, 512 * (bk2 + cc):512 * (bk2 + cc + 1)],
                                AF.Identity, bias=0.0, scale=1.0)
                        mark(i, "sSTG2")
                        G2 = FG[0]
                        pT32 = pSC[:, 512 * (G2 % 2):512 * (G2 % 2) + 512]
                        if eng == "tensor":
                            E.wait_ge(S["sSTG2"], gfc + 1)
                            if G2 >= 2:
                                wev(("FG", G2 - 2))
                        for p4 in range(4):
                            i = t_tr(pT32[:, 128 * p4:128 * (p4 + 1)],
                                     stg[:, gfc % 2, 128 * p4:128 * (p4 + 1)],
                                     consts_sb[:, CO_ID:CO_ID + 128])
                            if p4 == 3:
                                mark(i, "sFT2", 4)
                        for p4 in range(4):
                            if eng == "vector":
                                E.wait_ge(S["sFT2"], 4 * gfc + 4)
                                if fc == 0:
                                    E.wait_ge(S["sXR2%d" % p4],
                                              C["sXR2%d" % p4])
                            v_tt(outsb4[:, p4, 128 * fc:128 * (fc + 1)],
                                 outsb4[:, p4, 128 * fc:128 * (fc + 1)],
                                 pT32[:, 128 * p4:128 * (p4 + 1)], ALU.add)
                        dve_ev(("FG", G2))
                        dve_ev(("O", hh, th, fc))
                        FG[0] += 1
                for p4 in range(4):
                    tl = 4 * th + p4
                    if eng == "sync":
                        wev(("O", hh, th, 15))
                    if hh == 0:
                        i = dma_s(out=y_part[128 * tl:128 * (tl + 1), :],
                                  in_=outsb4[:, p4, :])
                        mark(i, "sYS%d" % p4, 16)
                    else:
                        i = dma_s(out=outT["out"][128 * tl:128 * (tl + 1), :],
                                  in_=outsb4[:, p4, :])
                        mark(i, "sOUT%d" % p4, 16)
            for _p4 in range(4):
                weng("sync", "sOUT%d" % _p4, 32)

        @block.sync
        def _(e):
            emit("sync", e)

        @block.gpsimd
        def _(e):
            emit("gpsimd", e)

        @block.vector
        def _(e):
            emit("vector", e)

        @block.scalar
        def _(e):
            emit("scalar", e)

        @block.tensor
        def _(e):
            emit("tensor", e)

    return nc, outT


from concourse.bass_utils import run_bass_kernel_spmd


def _make_consts():
    c = np.zeros((128, CO_W), np.float32)
    c[:, CO_ID:CO_ID + 128] = np.eye(128)
    c[:, CO_ONE:CO_ONE + 128] = 1.0
    c[:, CO_INVS:CO_INVS + 16] = np.array(INV_SHRINKS_F32, np.float32)[None, :]
    c[:, CO_NEGS:CO_NEGS + 16] = -np.array(SHRINKS_F32, np.float32)[None, :]
    c[:, CO_S:CO_S + 16] = np.array(SHRINKS_F32, np.float32)[None, :]
    return c


def kernel(L0_attn_x, L0_attn_w_qkv, L0_attn_w_o, L0_ffn_w_fc, L0_ffn_w_proj):
    x = np.ascontiguousarray(np.asarray(L0_attn_x, np.float32))
    w_qkv = np.ascontiguousarray(np.asarray(L0_attn_w_qkv, np.float32))
    w_o = np.ascontiguousarray(np.asarray(L0_attn_w_o, np.float32))
    w_fc = np.ascontiguousarray(np.asarray(L0_ffn_w_fc, np.float32))
    w_pj = np.ascontiguousarray(np.asarray(L0_ffn_w_proj, np.float32))

    nc, outT = build(5)
    consts = _make_consts()

    wqkv_j = []
    for j in range(2):
        cols = []
        for qkv3 in range(3):
            for h in range(8 * j, 8 * j + 8):
                cols.extend(range(qkv3 * 2048 + h * 128,
                                  qkv3 * 2048 + (h + 1) * 128))
        wqkv_j.append(np.ascontiguousarray(w_qkv[:, cols]))

    in_maps = []
    for c in range(8):
        b, j = c // 2, c % 2
        in_maps.append({
            "xb": x[b],
            "xrows": np.ascontiguousarray(x[b, 1024 * j:1024 * (j + 1), :]),
            "wqkv": wqkv_j[j],
            "wo": w_o,
            "wfc": np.ascontiguousarray(w_fc[1024 * c:1024 * (c + 1), :]),
            "wpj": np.ascontiguousarray(w_pj[256 * c:256 * (c + 1), :]),
            "consts": consts,
        })

    res = run_bass_kernel_spmd(nc, in_maps, core_ids=list(range(8))).results

    out = np.empty((4, 2048, 2048), np.float32)
    for c in range(8):
        b, j = c // 2, c % 2
        out[b, 1024 * j:1024 * (j + 1), :] = np.asarray(res[c]["out"])
    return out


# version 21
# speedup vs baseline: 1.0185x; 1.0099x over previous
import math
from contextlib import ExitStack
from collections import Counter

import numpy as np
import ml_dtypes

import concourse.bass as bass
import concourse.mybir as mybir
from concourse.bass import ds

F32 = mybir.dt.float32
BF16 = mybir.dt.bfloat16
I32 = mybir.dt.int32
AF = mybir.ActivationFunctionType
ALU = mybir.AluOpType
AX = mybir.AxisListType

D = 2048
SEQ = 2048
TOK = 1024
NHL = 8
HID = 8192
INV_SQRT_DH = float(np.float32(1.0) / np.float32(np.sqrt(np.float32(128.0))))
EPS = 1e-5
SHRINKS_F32 = [float(np.float32(s)) for s in np.linspace(0.4, 1.0, 16)]
INV_SHRINKS_F32 = [float(np.float32(1.0) / np.float32(s)) for s in SHRINKS_F32]
NS = 16
QMAX = 15.0
INV_QMAX = float(np.float32(1.0) / np.float32(15.0))
QF = 512
NQT_FC = 32
NQT = 64
NBLK = QF // 64
NXT = 16
NMG = 12
NEGBIG = -1e30

CO_ID = 0
CO_ONE = 128
CO_INVS = 256
CO_NEGS = 272
CO_S = 288
CO_W = 304


CAP0 = 212832


def _arena(nc, ctx):
    CAP = CAP0
    backing = ctx.enter_context(
        nc.sbuf_tensor("arena", [128, CAP], mybir.dt.uint8))
    base = nc.lookup_mloc(backing).addr

    def place(name, shape, dtype, off):
        per_part = int(np.prod(shape[1:])) * mybir.dt.size(dtype)
        assert off % 32 == 0 and off + per_part <= CAP, (name, off, per_part)
        h = nc.virtual_sbuf_tensor(name, list(shape), dtype)
        nc._tensor(h.name, list(h.shape), h.dtype, type="SB")
        mloc = nc.lookup_mloc(h)
        mloc.allocated = True
        mloc.addr = base + off
        return h

    return place


def build(stage: int):
    nc = bass.Bass("TRN2", target_bir_lowering=False, debug=False)

    xb = nc.dram_tensor("xb", [SEQ, D], F32, kind="ExternalInput")
    wqkv = nc.dram_tensor("wqkv", [D, 3 * NHL * 128], F32, kind="ExternalInput")
    wo = nc.dram_tensor("wo", [D, D], F32, kind="ExternalInput")
    wfc = nc.dram_tensor("wfc", [1024, D], F32, kind="ExternalInput")
    wpj = nc.dram_tensor("wpj", [256, HID], F32, kind="ExternalInput")
    consts = nc.dram_tensor("consts", [128, CO_W], F32, kind="ExternalInput")
    xrows = nc.dram_tensor("xrows", [TOK, D], F32, kind="ExternalInput")

    wqkv_bf = nc.dram_tensor("wqkv_bf", [D, 3 * NHL * 128], BF16, kind="Internal")
    vals_sh = nc.dram_tensor("vals_sh", [2, NHL, 128, SEQ], BF16,
                             kind="Internal", addr_space="Shared")
    bar_in = nc.dram_tensor("bar_in", [1, 4], F32, kind="Internal")
    bar_out = nc.dram_tensor("bar_out", [2, 1, 4], F32, kind="Internal")
    x2_spill = nc.dram_tensor("x2_spill", [TOK, D], F32, kind="Internal")
    y_part = nc.dram_tensor("y_part", [TOK, D], F32, kind="Internal")
    wfc_nat = nc.dram_tensor("wfc_nat", [1024, D], BF16, kind="Internal")
    wpj_nat = nc.dram_tensor("wpj_nat", [256, HID], BF16, kind="Internal")
    wfcT_shc = nc.dram_tensor("wfcT_shc", [4, D, 256], BF16, kind="Internal")
    wpjT_shc = nc.dram_tensor("wpjT_shc", [4, 2048, 256], BF16, kind="Internal")
    wfcT_allc = nc.dram_tensor("wfcT_allc", [4, 8, D, 256], BF16, kind="Internal")
    wpjT_allc = nc.dram_tensor("wpjT_allc", [4, 8, 2048, 256], BF16, kind="Internal")

    outT = {}
    if stage >= 5:
        outT["out"] = nc.dram_tensor("out", [TOK, D], F32, kind="ExternalOutput")
    if stage == 1:
        outT["d_qkv"] = nc.dram_tensor("d_qkv", [24, 128, SEQ], BF16, kind="ExternalOutput")
    if stage == 2:
        outT["d_vals"] = nc.dram_tensor("d_vals", [NHL, 128, SEQ], BF16, kind="ExternalOutput")
    if stage == 3:
        outT["d_x2"] = nc.dram_tensor("d_x2", [TOK, D], F32, kind="ExternalOutput")
        outT["d_ln2t"] = nc.dram_tensor("d_ln2t", [16, 128, TOK], BF16, kind="ExternalOutput")
    if stage == 4:
        outT["d_x2"] = nc.dram_tensor("d_x2", [TOK, D], F32, kind="ExternalOutput")
        outT["d_wfcT"] = nc.dram_tensor("d_wfcT", [4, D, 256], BF16, kind="ExternalOutput")
        outT["d_wpjT"] = nc.dram_tensor("d_wpjT", [4, 2048, 256], BF16, kind="ExternalOutput")

    with ExitStack() as ctx:
        place = _arena(nc, ctx)
        consts_sb = place("consts_sb", [128, CO_W], F32, 0)
        identb = place("identb", [128, 128], BF16, 1216)
        onesb = place("onesb", [128, 128], BF16, 1472)
        zbw = place("zbw", [128, 2, 128], BF16, 1728)
        lnstat = place("lnstat", [128, 4, 32], F32, 2752)
        zpart = place("zpart", [128, 2, 4], F32, 3264)
        zinv = place("zinv", [128, 2, 1], F32, 3296)
        qblk = place("qblk", [128, 2, NBLK, 8], F32, 3328)
        qerr = place("qerr", [128, 2, NBLK, NS], F32, 3840)
        qmsk = place("qmsk", [128, NBLK, NS], F32, 4864)
        qsel = place("qsel", [128, NBLK, NS], F32, 5376)
        A_END = 5888
        B0 = A_END
        ln1T = place("ln1T", [128, 16, TOK], BF16, B0)
        vxs = place("vxs", [128, 2 * NHL, TOK], BF16, B0)
        hsqA = place("hsqA", [128, 32, 512], BF16, B0)
        B_END = B0 + 32768
        C0 = B_END
        wosf = place("wosf", [128, 8, D], BF16, C0)
        htmp = place("htmp", [128, 2, 512], F32, C0 + 32768)
        wstr2 = place("wstr2", [128, 2, 8, 256], BF16, C0 + 36864)
        outsb4 = place("outsb4", [128, 4, D], F32, C0)
        C_END = C0 + 65536
        qkvs = place("qkvs", [128, 24, SEQ], BF16, C0)
        D0 = C_END
        ln2T = place("ln2T", [128, 16, TOK], BF16, D0)
        D_END = D0 + 32768
        E0 = D_END
        wqbB = place("wqbB", [128, 2, 16, 256], BF16, E0)
        pcst = place("pcst", [128, 2, 3072], BF16, E0 + 16384)
        x2t = place("x2t", [128, 4, D], F32, E0)
        hsqB = place("hsqB", [128, 32, 512], BF16, E0)
        E_END = E0 + 32768
        F0 = E_END
        xs = place("xs", [128, 2, D], F32, F0)
        Etile = place("Etile", [128, 2, SEQ], BF16, F0)
        ln2st = place("ln2st", [128, 4, D], BF16, F0)
        wstr = place("wstr", [128, 2, 16, 256], BF16, F0)
        F_END = F0 + 16384
        G0 = F_END
        lntile = place("lntile", [128, 2, D], BF16, G0)
        stg = place("stg", [128, 2, 512], F32, G0)
        qrl = place("qrl", [128, 2, QF], BF16, G0 + 4096)
        deqT = place("deqT", [128, 2, 4, 128], BF16, G0 + 6144)
        G_END = G0 + 8192
        Q0 = G_END
        qt_t = place("qt_t", [128, 2, QF], F32, Q0)
        qi = place("qi", [128, 4, QF], mybir.dt.int16, Q0 + 4096)
        qq = place("qq", [128, 4, QF], F32, Q0 + 8192)
        qsq = place("qsq", [128, 4, QF], F32, Q0 + 8192)
        deqb = place("deqb", [128, 2, QF], BF16, Q0 + 16384)
        qdq = place("qdq", [128, 2, QF], mybir.dt.int16, Q0 + 16384)
        Q_END = Q0 + 18432
        assert Q_END <= CAP0
        assert Q_END <= 212832, Q_END

        pMM = ctx.enter_context(nc.psum_tensor("pMM", [128, 2048], F32))
        pSC = ctx.enter_context(nc.psum_tensor("pSC", [128, 1024], F32))
        pTa = ctx.enter_context(nc.psum_tensor("pTa", [128, 1024], BF16))
        pTb = ctx.enter_context(nc.psum_tensor("pTb", [128, 1024], BF16))

        semnames = [
            "sDVE", "sPOOL", "sLNa", "sTR", "sQKV", "sQKVE",
            "sSC", "sEXP", "sCS", "sCC", "sTRA", "sVS0", "sVS1", "sVX0", "sVX1",
            "sOP", "sSTG",
            "sFT", "sLN2a", "sQI", "sQS", "sQQ",
            "sH", "sRELU", "sF2", "sSTG2", "sFT2",
            "sX0", "sX1", "sWQ0", "sWQ1", "sQW0", "sQW1",
            "sCB0", "sCB1", "sWOS0", "sWOS1", "sXN0", "sXN1", "sXN2", "sXN3",
            "sDNW0", "sDNW1", "sQRL0", "sQRL1", "sDW0", "sDW1",
            "sFW0", "sFW1", "sF2W0", "sF2W1",
            "sXR20", "sXR21", "sXR22", "sXR23",
            "sYS0", "sYS1", "sYS2", "sYS3",
            "sOUT0", "sOUT1", "sOUT2", "sOUT3",
            "sX2S0", "sX2S1", "sX2S2", "sX2S3",
            "sDBG", "sCONST", "sCONSTg", "sWQC", "sWQS0", "sWQS1",
        ]
        S = {n: ctx.enter_context(nc.semaphore(n)) for n in semnames}

        block = ctx.enter_context(nc.Block())

        def emit(eng, E):
            C = Counter()
            EV = {}

            def inc(sname, amt=1):
                C[sname] += amt
                return C[sname]

            def mark(ins, sname, amt=1):
                if ins is not None:
                    ins.then_inc(S[sname], amt)
                return inc(sname, amt)

            def ev(key):
                pass

            def wev(key):
                sem, val = EV[key]
                if E is not None and val > 0:
                    E.wait_ge(S[sem], val)

            def weng(target, sname, val):
                if eng == target and E is not None and val > 0:
                    E.wait_ge(S[sname], val)

            def _chain(fn_name, *a, **k):
                if eng == "vector":
                    E.wait_ge(S["sDVE"], C["sDVE"])
                    ins = getattr(E, fn_name)(*a, **k)
                    ins.then_inc(S["sDVE"], 1)
                else:
                    ins = None
                inc("sDVE")
                return ins

            def _pchain(fn_name, *a, **k):
                if eng == "gpsimd":
                    E.wait_ge(S["sPOOL"], C["sPOOL"])
                    ins = getattr(E, fn_name)(*a, **k)
                    ins.then_inc(S["sPOOL"], 1)
                else:
                    ins = None
                inc("sPOOL")
                return ins

            def v_ts(*a, **k): return _chain("tensor_scalar", *a, **k)
            def v_tt(*a, **k): return _chain("tensor_tensor", *a, **k)
            def v_stt(*a, **k): return _chain("scalar_tensor_tensor", *a, **k)
            def v_red(*a, **k): return _chain("tensor_reduce", *a, **k)
            def v_cp(*a, **k): return _chain("tensor_copy", *a, **k)
            def v_rcp(*a, **k): return _chain("reciprocal", *a, **k)
            def v_sel(*a, **k): return _chain("select", *a, **k)
            def v_bns(*a, **k): return _chain("bn_stats", *a, **k)
            def v_bna(*a, **k): return _chain("bn_aggr", *a, **k)
            def v_ms(*a, **k): return _chain("memset", *a, **k)
            def p_ts(*a, **k): return _pchain("tensor_scalar", *a, **k)
            def p_red(*a, **k): return _pchain("tensor_reduce", *a, **k)

            def dve_ev(key):
                EV[key] = ("sDVE", C["sDVE"])

            def pool_ev(key):
                EV[key] = ("sPOOL", C["sPOOL"])

            def act(*a, **k):
                return E.activation(*a, **k) if eng == "scalar" else None

            def t_mm(*a, **k):
                return E.matmul(*a, **k) if eng == "tensor" else None

            def t_tr(*a, **k):
                return E.transpose(*a, **k) if eng == "tensor" else None

            def dma_s(*a, **k):
                return E.dma_start(*a, **k) if eng == "sync" else None

            def dma_a(*a, **k):
                return E.dma_start(*a, **k) if eng == "scalar" else None

            def dma_g(*a, **k):
                return E.dma_start(*a, **k) if eng == "gpsimd" else None



            q_items = []
            if stage >= 4:
                tail = []
                for qt in range(NQT):
                    q_items.append(("setup", qt, 0))
                    for t in range(NS + 6):
                        if t % 2 == 0 and 0 <= t - 4 < NS:
                            q_items.append(("st2", qt, t - 4))
                        if t % 2 == 1 and 0 <= t - 5 < NS:
                            q_items.append(("sqp", qt, t - 5))
                        if t % 2 == 0 and 0 <= t - 6 < NS:
                            q_items.append(("redp", qt, t - 6))
                        if t % 2 == 0 and 0 <= t - 2 < NS:
                            q_items.append(("clip2", qt, t - 2))
                        if t < NS:
                            q_items.append(("cvt", qt, t))
                        if t == 2 and tail:
                            q_items.append(tail.pop(0))
                        if t == 4 and tail:
                            q_items.append(tail.pop(0))
                    tail = [("trk", qt, 0), ("deq", qt, 0)]
                q_items.extend(tail)
            q_pos = [0]

            def q_src(qt):
                if qt < NQT_FC:
                    return wfc[128 * (qt // 4):128 * (qt // 4 + 1),
                               QF * (qt % 4):QF * (qt % 4 + 1)]
                lq = qt - NQT_FC
                rt, ct = lq % 2, lq // 2
                return wpj[128 * rt:128 * (rt + 1), QF * ct:QF * (ct + 1)]

            def q_nat(qt):
                if qt < NQT_FC:
                    return wfc_nat[128 * (qt // 4):128 * (qt // 4 + 1),
                                   QF * (qt % 4):QF * (qt % 4 + 1)]
                lq = qt - NQT_FC
                rt, ct = lq % 2, lq // 2
                return wpj_nat[128 * rt:128 * (rt + 1), QF * ct:QF * (ct + 1)]

            def t3(sl):
                return sl.rearrange("p (b k) -> p b k", k=64)

            def emit_q_item(it):
                kind, qt, s = it
                p = qt % 2
                if kind == "setup":
                    if eng == "gpsimd" and qt >= 2:
                        wev(("DQ", qt - 2))
                    i = dma_g(out=qt_t[:, p, :].bitcast(F32), in_=q_src(qt))
                    mark(i, "sQW%d" % (qt % 2), 16)
                    if eng == "vector":
                        E.wait_ge(S["sQW%d" % (qt % 2)], 16 * (qt // 2 + 1))
                        if qt >= 2:
                            wev(("DQ", qt - 2))
                    v_red(qblk[:, p, :, 0:1], t3(qt_t[:, p, :]), AX.X, ALU.max,
                          apply_absolute_value=True)
                    v_ts(qblk[:, p, :, 1:2], qblk[:, p, :, 0:1], 1e-8,
                         INV_QMAX, ALU.max, ALU.mult)
                    v_rcp(qblk[:, p, :, 2:3], qblk[:, p, :, 1:2])
                    v_tt(t3(qt_t[:, p, :]), t3(qt_t[:, p, :]),
                         qblk[:, p, :, 2:3].broadcast_to([128, NBLK, 64]),
                         ALU.mult)
                    dve_ev(("QSU", qt))
                elif kind == "cvt":
                    n = NS * qt + s
                    i4 = n % 4
                    if eng == "scalar":
                        wev(("QSU", qt))
                        if n >= 4:
                            wev(("QDd", (n - 4) | 1))
                    i = act(qi[:, i4, :], qt_t[:, p, :], AF.Identity,
                            bias=0.0, scale=INV_SHRINKS_F32[s])
                    mark(i, "sQI")
                elif kind == "clip2":
                    n = NS * qt + s
                    i4 = n % 4
                    if eng == "gpsimd":
                        E.wait_ge(S["sQI"], n + 2)
                    p_ts(qi[:, i4:i4 + 2, :].rearrange("p a b -> p (a b)"),
                         qi[:, i4:i4 + 2, :].rearrange("p a b -> p (a b)"),
                         QMAX, -QMAX, ALU.min, ALU.max)
                    pool_ev(("QC", n + 1))
                elif kind == "st2":
                    n = NS * qt + s
                    s4 = n % 4
                    if eng == "vector":
                        wev(("QC", n + 1))
                        if n >= 4:
                            wev(("QR", (n - 4) | 1))
                    v_stt(qq[:, s4, :], qi[:, s4, :], -SHRINKS_F32[s],
                          qt_t[:, p, :], ALU.mult, ALU.add)
                    v_stt(qq[:, s4 + 1, :], qi[:, s4 + 1, :],
                          -SHRINKS_F32[s + 1], qt_t[:, p, :],
                          ALU.mult, ALU.add)
                    dve_ev(("QDd", n + 1))
                elif kind == "sqp":
                    n = NS * qt + s
                    s4 = n % 4
                    if eng == "scalar":
                        wev(("QDd", n + 1))
                    i = act(qsq[:, s4:s4 + 2, :].rearrange("p a b -> p (a b)"),
                            qq[:, s4:s4 + 2, :].rearrange("p a b -> p (a b)"),
                            AF.Square)
                    mark(i, "sQS")
                elif kind == "redp":
                    n = NS * qt + s
                    s4 = n % 4
                    if eng == "vector":
                        E.wait_ge(S["sQS"], C["sQS"])
                        if s == 0 and qt >= 2:
                            wev(("QT", qt - 2))
                    v_red(qerr[:, p, :, s:s + 2]
                          .rearrange("p b a -> p a b"),
                          qsq[:, s4:s4 + 2, :]
                          .rearrange("p a (b k) -> p a b k", k=64),
                          AX.X, ALU.add)
                    dve_ev(("QR", n + 1))
                elif kind == "trk":
                    if eng == "vector":
                        E.wait_ge(S["sCONST"], 16)
                        wev(("QR", NS * qt + NS - 1))
                    err = qerr[:, p, :, :]
                    v_red(qblk[:, p, :, 3:4], err, AX.X, ALU.min)
                    v_tt(qmsk[:, :, :], err,
                         qblk[:, p, :, 3:4].broadcast_to([128, NBLK, NS]),
                         ALU.is_equal)
                    v_tt(qsel[:, :, :], qmsk[:, :, :],
                         consts_sb[:, CO_INVS:CO_INVS + NS]
                         .rearrange("p (o s) -> p o s", o=1).broadcast_to([128, NBLK, NS]),
                         ALU.mult)
                    v_red(qblk[:, p, :, 4:5], qsel[:, :, :], AX.X, ALU.max)
                    v_tt(qmsk[:, :, :],
                         consts_sb[:, CO_INVS:CO_INVS + NS]
                         .rearrange("p (o s) -> p o s", o=1)
                         .broadcast_to([128, NBLK, NS]),
                         qblk[:, p, :, 4:5].broadcast_to([128, NBLK, NS]),
                         ALU.is_equal)
                    v_tt(qsel[:, :, :], qmsk[:, :, :],
                         consts_sb[:, CO_S:CO_S + NS]
                         .rearrange("p (o s) -> p o s", o=1)
                         .broadcast_to([128, NBLK, NS]),
                         ALU.mult)
                    v_red(qblk[:, p, :, 5:6], qsel[:, :, :], AX.X, ALU.max)
                    v_tt(qblk[:, p, :, 6:7], qblk[:, p, :, 1:2],
                         qblk[:, p, :, 5:6], ALU.mult)
                    dve_ev(("QT", qt))
                elif kind == "deq":
                    if eng == "vector":
                        if qt >= 2:
                            E.wait_ge(S["sDNW%d" % (qt % 2)], 16 * (qt // 2))
                    v_tt(t3(qt_t[:, p, :]), t3(qt_t[:, p, :]),
                         qblk[:, p, :, 4:5].broadcast_to([128, NBLK, 64]),
                         ALU.mult)
                    dve_ev(("U2", qt))
                    if eng == "scalar":
                        wev(("U2", qt))
                        if qt >= 1:
                            E.wait_ge(S["sDNW%d" % ((qt - 1) % 2)],
                                      16 * ((qt - 1) // 2 + 1))
                    i = act(qdq[:, 1 - p, :], qt_t[:, p, :], AF.Identity,
                            bias=0.0, scale=1.0)
                    mark(i, "sQQ")
                    if eng == "gpsimd":
                        E.wait_ge(S["sQQ"], C["sQQ"])
                    p_ts(qdq[:, 1 - p, :], qdq[:, 1 - p, :], QMAX, -QMAX,
                         ALU.min, ALU.max)
                    pool_ev(("QC2", qt))
                    if eng == "vector":
                        wev(("QC2", qt))
                    v_tt(t3(deqb[:, p, :]), t3(qdq[:, 1 - p, :]),
                         qblk[:, p, :, 6:7].broadcast_to([128, NBLK, 64]),
                         ALU.mult)
                    dve_ev(("DQ", qt))
                    if eng == "gpsimd":
                        wev(("DQ", qt))
                    i = dma_g(out=q_nat(qt), in_=deqb[:, p, :])
                    mark(i, "sDNW%d" % (qt % 2), 16)

            def q_step(k):
                if stage < 4:
                    return
                n = 0
                while n < k and q_pos[0] < len(q_items):
                    emit_q_item(q_items[q_pos[0]])
                    q_pos[0] += 1
                    n += 1

            def q_flush_until(qt_lim):
                if stage < 4:
                    return
                last = -1
                for ii in range(len(q_items)):
                    if q_items[ii][1] < qt_lim:
                        last = ii
                while q_pos[0] <= last:
                    emit_q_item(q_items[q_pos[0]])
                    q_pos[0] += 1

            LQG = [0]
            lateT_hist = []

            def lateT(qt_list, shW_of):
                for qt in qt_list:
                    g = LQG[0]
                    sl = g % 2
                    if eng == "gpsimd":
                        E.wait_ge(S["sDNW%d" % (qt % 2)], 16 * (qt // 2 + 1))
                        if g >= 2:
                            wev(("DT", lateT_hist[g - 2]))
                    i = dma_g(out=qrl[:, sl, :], in_=q_nat(qt))
                    mark(i, "sQRL%d" % sl, 16)
                    G = TRG[0]
                    pT = pTa if G % 2 == 0 else pTb
                    if eng == "tensor":
                        E.wait_ge(S["sQRL%d" % sl], C["sQRL%d" % sl])
                        if G >= 2:
                            wev(("TRE", G - 2))
                    for k in range(4):
                        i = t_tr(pT[:, 128 * k:128 * (k + 1)],
                                 qrl[:, sl, 128 * k:128 * (k + 1)],
                                 identb[:, :])
                        if k == 3:
                            mark(i, "sTR")
                    if eng == "vector":
                        E.wait_ge(S["sTR"], C["sTR"])
                        E.wait_ge(S["sDW%d" % sl], C["sDW%d" % sl])
                    v_cp(deqT[:, sl, :, :],
                         pT[:, 0:512].rearrange("p (a b) -> p a b", b=128))
                    dve_ev(("DT", qt))
                    dve_ev(("TRE", G))
                    if eng == "gpsimd":
                        wev(("DT", qt))
                    i = dma_g(out=shW_of(qt), in_=deqT[:, sl, :, :])
                    mark(i, "sDW%d" % sl, 16)
                    TRG[0] += 1
                    lateT_hist.append(qt)
                    LQG[0] += 1

            def wfc_shW(qt):
                cj = qt // 8
                c0 = 128 * ((qt // 4) % 2)
                f0 = QF * (qt % 4)
                return wfcT_shc[cj, f0:f0 + 512, c0:c0 + 128] \
                    .rearrange("(k p) c -> p k c", p=128)

            def wpj_shW(qt):
                lq = qt - NQT_FC
                cj = lq // 8
                h0 = 512 * ((lq // 2) % 4)
                c0 = 128 * (lq % 2)
                return wpjT_shc[cj, h0:h0 + 512, c0:c0 + 128] \
                    .rearrange("(k p) c -> p k c", p=128)

            CCN = {}

            def emit_cc(name, ins_l, outs_l, groups=None):
                if eng == "gpsimd":
                    E.wait_ge(S["sDW0"], C["sDW0"])
                    E.wait_ge(S["sDW1"], C["sDW1"])
                i = (E.collective_compute(
                    "AllGather", ALU.bypass,
                    replica_groups=(groups if groups is not None
                                    else [[0, 1, 2, 3, 4, 5, 6, 7]]),
                    ins=ins_l, outs=outs_l)
                    if eng == "gpsimd" else None)
                mark(i, "sCC")
                CCN[name] = C["sCC"]

            i = dma_s(out=consts_sb[:, :], in_=consts[:, :])
            mark(i, "sCONST", 16)
            weng("sync", "sCONST", 16)
            i = dma_s(out=bar_in[:, :], in_=consts_sb[0:1, 0:4])
            mark(i, "sCONST", 16)
            weng("gpsimd", "sCONST", 16)
            i = dma_g(out=identb[:, :], in_=consts_sb[:, CO_ID:CO_ID + 128])
            mark(i, "sCONSTg", 16)
            i = dma_g(out=onesb[:, :], in_=consts_sb[:, CO_ONE:CO_ONE + 128])
            mark(i, "sCONSTg", 16)

            for k in range(16):
                if eng == "gpsimd" and k >= 2:
                    E.wait_ge(S["sWQS%d" % (k % 2)], 16 * (k // 2))
                srcv = wqkv[128 * k:128 * (k + 1), :]
                stv = pcst[:, k % 2, :]
                i = dma_g(out=stv, in_=srcv)
                mark(i, "sWQC", 16)
                weng("gpsimd", "sWQC", C["sWQC"])
                i = dma_g(out=wqkv_bf[128 * k:128 * (k + 1), :], in_=stv)
                mark(i, "sWQS%d" % (k % 2), 16)

            q_step(60)

            TRG = [0]
            trev = {}

            def emit_ln1_tile(i_t, xsrc, stoff, sema, lsl, evtag):
                for sb4 in range(4):
                    v_bns(lnstat[:, lsl, 6 * sb4:6 * sb4 + 6],
                          xsrc[:, 512 * sb4:512 * (sb4 + 1)])
                v_bna(lnstat[:, lsl, 24:26], lnstat[:, lsl, 0:24])
                v_ts(lnstat[:, lsl, 25:26], lnstat[:, lsl, 25:26],
                     EPS, None, ALU.add)
                dve_ev((evtag, i_t, "e"))
                if eng == "scalar":
                    wev((evtag, i_t, "e"))
                i = act(lnstat[:, lsl, 26:27], lnstat[:, lsl, 25:26], AF.Sqrt)
                mark(i, sema)
                if eng == "vector":
                    E.wait_ge(S[sema], C[sema])
                v_rcp(lnstat[:, lsl, 27:28], lnstat[:, lsl, 26:27])
                v_stt(lnstat[:, lsl, 28:29], lnstat[:, lsl, 24:25],
                      -1.0, lnstat[:, lsl, 27:28], ALU.mult, ALU.mult)
                dve_ev((evtag, i_t, "n"))

            def ln1_tile(i_t):
                if eng == "sync" and i_t >= 2:
                    E.wait_ge(S["sLNa"], 2 * (i_t - 2) + 2)
                i = dma_s(out=xs[:, i_t % 2, :], in_=xb[128 * i_t:128 * (i_t + 1), :])
                mark(i, "sX%d" % (i_t % 2), 16)
                if eng == "vector":
                    E.wait_ge(S["sX%d" % (i_t % 2)], 16 * (i_t // 2 + 1))
                emit_ln1_tile(i_t, xs[:, i_t % 2, :], 0, "sLNa", i_t % 2, "L1")
                if eng == "scalar":
                    wev(("L1", i_t, "n"))
                    if i_t >= 2:
                        E.wait_ge(S["sTR"], 2 * (i_t - 2) + 2)
                i = act(lntile[:, i_t % 2, :], xs[:, i_t % 2, :], AF.Identity,
                        bias=lnstat[:, i_t % 2, 28:29],
                        scale=lnstat[:, i_t % 2, 27:28])
                mark(i, "sLNa")

            def ln1T_tile(i_t, half):
                for g in range(2):
                    G = TRG[0]
                    pT = pTa if G % 2 == 0 else pTb
                    if eng == "tensor":
                        E.wait_ge(S["sLNa"], 2 * i_t + 2)
                        E.wait_ge(S["sCONSTg"], 32)
                        if G >= 2:
                            wev(("TRE", G - 2))
                    for k in range(8):
                        fc = 8 * g + k
                        i = t_tr(pT[:, 128 * k:128 * (k + 1)],
                                 lntile[:, i_t % 2, 128 * fc:128 * (fc + 1)],
                                 identb[:, :])
                        if k == 7:
                            mark(i, "sTR")
                    if eng == "vector":
                        E.wait_ge(S["sTR"], G + 1)
                        if G == 16:
                            E.wait_ge(S["sQKV"], 24)
                    tloc = (i_t % 8) * 128
                    v_cp(ln1T[:, 8 * g:8 * g + 8, tloc:tloc + 128],
                         pT[:, :].rearrange("p (a b) -> p a b", b=128))
                    dve_ev(("TRE", G))
                    TRG[0] += 1

            def qkv_half(half):
                for mg in range(NMG):
                    gmg = half * NMG + mg
                    if eng == "sync":
                        if gmg == 0:
                            E.wait_ge(S["sWQS0"], 16 * 8)
                            E.wait_ge(S["sWQS1"], 16 * 8)
                        if gmg >= 2:
                            E.wait_ge(S["sQKV"], 2 * (gmg - 1))
                    i = dma_s(out=wqbB[:, gmg % 2, :, :],
                              in_=wqkv_bf[:, 256 * mg:256 * (mg + 1)]
                              .rearrange("(k p) c -> p k c", p=128))
                    mark(i, "sWQ%d" % (gmg % 2), 16)
                    for m2 in range(2):
                        m = 2 * mg + m2
                        Mg = 24 * half + m
                        bp = Mg % 2
                        if eng == "tensor":
                            if m2 == 0:
                                E.wait_ge(S["sWQ%d" % (gmg % 2)],
                                          16 * (gmg // 2 + 1))
                                wev(("TRE", 16 * (half + 1) - 1))
                            if Mg >= 2:
                                E.wait_ge(S["sQKVE"], 2 * Mg - 2)
                        for k in range(16):
                            for n in range(2):
                                i = t_mm(pMM[:, 1024 * bp + 512 * n:
                                             1024 * bp + 512 * (n + 1)],
                                         wqbB[:, gmg % 2, k, 128 * m2:128 * (m2 + 1)],
                                         ln1T[:, k, 512 * n:512 * (n + 1)],
                                         start=(k == 0), stop=(k == 15))
                                if k == 15 and n == 1:
                                    mark(i, "sQKV")
                        scl = INV_SQRT_DH if m < 8 else 1.0
                        for n in range(2):
                            if eng == "scalar":
                                E.wait_ge(S["sQKV"], Mg + 1)
                            i = act(qkvs[:, m, 1024 * half + 512 * n:
                                          1024 * half + 512 * (n + 1)],
                                    pMM[:, 1024 * bp + 512 * n:
                                        1024 * bp + 512 * (n + 1)],
                                    AF.Identity, bias=0.0, scale=scl)
                            mark(i, "sQKVE")
                    q_step(26)

            for i_t in range(10):
                ln1_tile(i_t)
                if i_t < 8:
                    ln1T_tile(i_t, 0)
            qkv_half(0)
            for i_t in range(8, 16):
                if 10 <= i_t + 2 < 16:
                    ln1_tile(i_t + 2)
                ln1T_tile(i_t, 1)
            qkv_half(1)

            if stage == 1:
                for m in range(24):
                    weng("sync", "sQKVE", 96)
                    i = dma_s(out=outT["d_qkv"][m, :, :], in_=qkvs[:, m, :])
                    mark(i, "sDBG", 16)
                weng("sync", "sDBG", 16 * 24)
                return

            if eng == "sync":
                pidv = E.partition_id()
                offmy = (pidv % 2) * 1024
            pend_cs = [None]

            def colsum_emit(slot):
                h_, qt_ = slot // 16, slot % 16
                if eng == "tensor":
                    wev(("ZB", slot))
                    if qt_ == 0 and h_ >= 1:
                        wev(("VAL", h_ - 1))
                for kt in range(4):
                    i = t_mm(pMM[:, 512 * kt:512 * (kt + 1)],
                             zbw[:, slot % 2, :],
                             Etile[:, qt_ % 2, 512 * kt:512 * (kt + 1)],
                             start=(qt_ == 0), stop=(qt_ == 15))
                    if kt == 3:
                        mark(i, "sCS")

            def head_finish(h):
                if eng == "vector":
                    E.wait_ge(S["sCS"], 16 * (h + 1))
                for kt in range(4):
                    v_tt(qkvs[:, 16 + h, 512 * kt:512 * (kt + 1)],
                         qkvs[:, 16 + h, 512 * kt:512 * (kt + 1)],
                         pMM[:, 512 * kt:512 * (kt + 1)], ALU.mult)
                dve_ev(("VAL", h))
                if eng == "sync":
                    wev(("VAL", h))
                i = dma_s(out=vals_sh[ds(pidv % 2, 1), h, :, :]
                          if eng == "sync" else None,
                          in_=qkvs[:, 16 + h, :])
                mark(i, "sVS%d" % (h % 2), 16)

            for h in range(NHL):
                if h >= 1:
                    head_finish(h - 1)
                for qt in range(16):
                    slot = 16 * h + qt
                    for kq in range(4):
                        qidx = 4 * slot + kq
                        sb5 = 512 * (qidx % 2)
                        if eng == "tensor":
                            if qidx == 0:
                                E.wait_ge(S["sQKVE"], 96)
                            if qidx >= 2:
                                E.wait_ge(S["sEXP"], qidx - 1)
                        i = t_mm(pSC[:, sb5:sb5 + 512],
                                 qkvs[:, h, 128 * qt:128 * (qt + 1)],
                                 qkvs[:, 8 + h, 512 * kq:512 * (kq + 1)],
                                 start=True, stop=True)
                        mark(i, "sSC")
                        if eng == "scalar":
                            E.wait_ge(S["sSC"], qidx + 1)
                            if slot >= 2 and kq == 0:
                                E.wait_ge(S["sCS"], slot - 1)
                        i = act(Etile[:, qt % 2, 512 * kq:512 * (kq + 1)],
                                pSC[:, sb5:sb5 + 512], AF.Exp,
                                bias=0.0, scale=1.0,
                                accum_out=zpart[:, slot % 2, kq:kq + 1])
                        mark(i, "sEXP")
                    if eng == "vector":
                        E.wait_ge(S["sEXP"], 4 * slot + 4)
                        if slot >= 2:
                            E.wait_ge(S["sCS"], slot - 1)
                    v_red(zinv[:, slot % 2, 0:1],
                          zpart[:, slot % 2, :]
                          .rearrange("p (o s) -> p o s", o=1),
                          AX.X, ALU.add)
                    v_rcp(zinv[:, slot % 2, 0:1], zinv[:, slot % 2, 0:1])
                    v_ts(zbw[:, slot % 2, :], onesb[:, 0:128],
                         zinv[:, slot % 2, 0:1], None, ALU.mult)
                    dve_ev(("ZB", slot))
                    if pend_cs[0] is not None:
                        colsum_emit(pend_cs[0])
                    pend_cs[0] = slot
                if stage >= 4 and h == 4:
                    q_flush_until(8)
                    lateT(list(range(0, 8)), wfc_shW)
                    emit_cc("wfc0", [wfcT_shc[0, :, :]],
                            [wfcT_allc[0, :, :, :]])
            if pend_cs[0] is not None:
                colsum_emit(pend_cs[0])
                pend_cs[0] = None
            head_finish(NHL - 1)
            if eng == "gpsimd":
                E.wait_ge(S["sCONST"], 32)
                E.wait_ge(S["sVS0"], 16 * 4)
                E.wait_ge(S["sVS1"], 16 * 4)
            emit_cc("bar", [bar_in[:, :]], [bar_out[:, :, :]],
                    groups=[[0, 1], [2, 3], [4, 5], [6, 7]])
            if eng == "gpsimd":
                pidv_g = E.partition_id()
                offmy_g = (pidv_g % 2) * 1024
                E.wait_ge(S["sCC"], CCN["bar"])
            for r in range(2):
                for hl in range(NHL):
                    i = dma_g(out=vxs[:, 8 * r + hl, :],
                              in_=vals_sh[r, hl, :, ds(offmy_g, TOK)]
                              if eng == "gpsimd" else None)
                    mark(i, "sVX%d" % (hl % 2), 16)

            if stage == 2:
                for hl in range(NHL):
                    if eng == "sync":
                        wev(("VAL", NHL - 1))
                    i = dma_s(out=outT["d_vals"][hl, :, :], in_=qkvs[:, 16 + hl, :])
                    mark(i, "sDBG", 16)
                weng("sync", "sDBG", 16 * NHL)
                return

            q_step(40)

            FG = [0]
            fgev = {}
            for n in range(2):
                for p4 in range(4):
                    tl = 4 * n + p4
                    if eng == "sync":
                        E.wait_ge(S["sQKV"], 48)
                        if n == 1:
                            E.wait_ge(S["sLN2a"], 2 * (tl - 4) + 2)
                            E.wait_ge(S["sX2S%d" % p4], 16)
                    i = dma_s(out=x2t[:, p4, :],
                              in_=xrows[128 * tl:128 * (tl + 1), :])
                    mark(i, "sXN%d" % p4, 16)
                for p8 in range(2):
                    wk = 2 * n + p8
                    for hc in range(8):
                        if eng == "gpsimd":
                            if wk == 0 and hc == 0:
                                E.wait_ge(S["sCS"], 128)
                            if wk >= 1 and hc == 0:
                                E.wait_ge(S["sOP"], 16 * wk)
                        i = dma_g(out=wosf[:, hc, :],
                                  in_=wo[1024 * p8 + 128 * hc:
                                         1024 * p8 + 128 * (hc + 1), :])
                        mark(i, "sWOS%d" % (wk % 2), 16)
                    for fc in range(16):
                        gidx = wk * 16 + fc
                        r2 = gidx % 2
                        if eng == "tensor":
                            if fc == 0:
                                E.wait_ge(S["sVX0"], 16 * NHL)
                                E.wait_ge(S["sVX1"], 16 * NHL)
                                E.wait_ge(S["sWOS%d" % (wk % 2)],
                                          128 * (wk // 2 + 1))
                            if gidx >= 2:
                                E.wait_ge(S["sSTG"], gidx - 1)
                            if gidx < 2:
                                E.wait_ge(S["sEXP"], 512)
                        for hc in range(8):
                            rhs = vxs[:, 8 * p8 + hc, 512 * n:512 * (n + 1)]
                            i = t_mm(pSC[:, 512 * r2:512 * (r2 + 1)],
                                     wosf[:, hc, 128 * fc:128 * (fc + 1)],
                                     rhs, start=(hc == 0), stop=(hc == 7))
                            if hc == 7:
                                mark(i, "sOP")
                        if eng == "scalar":
                            E.wait_ge(S["sOP"], gidx + 1)
                            if gidx >= 2:
                                E.wait_ge(S["sFT"], 4 * (gidx - 1))
                        i = act(stg[:, r2, :], pSC[:, 512 * r2:512 * (r2 + 1)],
                                AF.Identity, bias=0.0, scale=1.0)
                        mark(i, "sSTG")
                        G2 = FG[0]
                        pT32 = pMM[:, 512 * (G2 % 2):512 * (G2 % 2) + 512]
                        if eng == "tensor":
                            E.wait_ge(S["sSTG"], gidx + 1)
                            if G2 >= 2:
                                wev(("FG", G2 - 2))
                            if G2 < 2:
                                wev(("VAL", NHL - 1))
                        for p4 in range(4):
                            i = t_tr(pT32[:, 128 * p4:128 * (p4 + 1)],
                                     stg[:, r2, 128 * p4:128 * (p4 + 1)],
                                     consts_sb[:, CO_ID:CO_ID + 128])
                            if p4 == 3:
                                mark(i, "sFT", 4)
                        for p4 in range(4):
                            tl = 4 * n + p4
                            if eng == "vector":
                                E.wait_ge(S["sFT"], 4 * gidx + 4)
                                if p8 == 0 and fc == 0:
                                    E.wait_ge(S["sXN%d" % p4], 16 * (n + 1))
                            v_tt(x2t[:, p4, 128 * fc:128 * (fc + 1)],
                                 x2t[:, p4, 128 * fc:128 * (fc + 1)],
                                 pT32[:, 128 * p4:128 * (p4 + 1)],
                                 ALU.add)
                        dve_ev(("FG", G2))
                        FG[0] += 1
                        q_step(7)
                for p4 in range(4):
                    t = 4 * n + p4
                    if eng == "sync":
                        wev(("FG", (2 * n + 1) * 16 + 15))
                    i = dma_s(out=x2_spill[128 * t:128 * (t + 1), :],
                              in_=x2t[:, p4, :])
                    mark(i, "sX2S%d" % p4, 16)
                if n == 0:
                    for p4 in range(4):
                        t = p4
                        if eng == "vector":
                            wev(("FG", 31))
                        emit_ln1_tile(t, x2t[:, p4, :], 0, "sLN2a",
                                      2 + t % 2, "L2")
                        if eng == "scalar":
                            wev(("L2", t, "n"))
                            E.wait_ge(S["sCS"], 128)
                        i = act(ln2st[:, t % 4, :], x2t[:, p4, :], AF.Identity,
                                bias=lnstat[:, 2 + t % 2, 28:29],
                                scale=lnstat[:, 2 + t % 2, 27:28])
                        mark(i, "sLN2a")
                q_step(50)
                if stage >= 4 and n == 0:
                    q_flush_until(16)
                    lateT(list(range(8, 16)), wfc_shW)
                    emit_cc("wfc1", [wfcT_shc[1, :, :]],
                            [wfcT_allc[1, :, :, :]])

            def ln2T_tiles(trange):
                for t in trange:
                    for g in range(2):
                        G = TRG[0]
                        pT = pTa if G % 2 == 0 else pTb
                        if eng == "tensor":
                            E.wait_ge(S["sLN2a"], 2 * t + 2)
                            E.wait_ge(S["sOP"], 64)
                            if G >= 2:
                                wev(("TRE", G - 2))
                        for k in range(8):
                            fc = 8 * g + k
                            i = t_tr(pT[:, 128 * k:128 * (k + 1)],
                                     ln2st[:, t % 4, 128 * fc:128 * (fc + 1)],
                                     identb[:, :])
                            if k == 7:
                                mark(i, "sTR")
                            else:
                                mark(i, "sTRA")
                        if eng == "vector":
                            E.wait_ge(S["sTR"], G + 1)
                            if G == 32:
                                E.wait_ge(S["sOP"], 64)
                        tloc = 128 * t
                        v_cp(ln2T[:, 8 * g:8 * g + 8, tloc:tloc + 128],
                             pT[:, :].rearrange("p (a b) -> p a b", b=128))
                        dve_ev(("TRE", G))
                        TRG[0] += 1

            ln2T_tiles(range(0, 4))
            q_step(60)
            for p4 in range(4):
                t = 4 + p4
                if eng == "vector":
                    wev(("FG", 63))
                emit_ln1_tile(t, x2t[:, p4, :], 0, "sLN2a", 2 + t % 2, "L2")
                if eng == "scalar":
                    wev(("L2", t, "n"))
                    E.wait_ge(S["sTR"], 32 + 2 * (t - 4) + 2)
                i = act(ln2st[:, t % 4, :], x2t[:, p4, :], AF.Identity,
                        bias=lnstat[:, 2 + t % 2, 28:29],
                        scale=lnstat[:, 2 + t % 2, 27:28])
                mark(i, "sLN2a")
            q_step(60)
            ln2T_tiles(range(4, 8))

            if stage == 3:
                for _p4 in range(4):
                    weng("sync", "sX2S%d" % _p4, 32)
                for t in range(8):
                    i = dma_s(out=outT["d_x2"][128 * t:128 * (t + 1), :],
                              in_=x2_spill[128 * t:128 * (t + 1), :])
                    mark(i, "sDBG", 16)
                for fcg in range(16):
                    if eng == "sync":
                        wev(("TRE", 47))
                    i = dma_s(out=outT["d_ln2t"][fcg, :, :], in_=ln2T[:, fcg, :])
                    mark(i, "sDBG", 16)
                weng("sync", "sDBG", 16 * 24)
                return

            if stage == 4:
                for cj in range(2, 4):
                    q_flush_until(8 * (cj + 1))
                    lateT(list(range(8 * cj, 8 * (cj + 1))), wfc_shW)
                    emit_cc("wfc%d" % cj,
                            [wfcT_shc[cj, :, :]], [wfcT_allc[cj, :, :, :]])
                for cj in range(4):
                    q_flush_until(NQT_FC + 8 * (cj + 1))
                    lateT(list(range(NQT_FC + 8 * cj, NQT_FC + 8 * (cj + 1))),
                          wpj_shW)
                    emit_cc("wpj%d" % cj,
                            [wpjT_shc[cj, :, :]], [wpjT_allc[cj, :, :, :]])
                for _p4 in range(4):
                    weng("sync", "sX2S%d" % _p4, 32)
                for t in range(8):
                    i = dma_s(out=outT["d_x2"][128 * t:128 * (t + 1), :],
                              in_=x2_spill[128 * t:128 * (t + 1), :])
                    mark(i, "sDBG", 16)
                weng("sync", "sDW0", 16 * NQT // 2)
                weng("sync", "sDW1", 16 * NQT // 2)
                for cj in range(4):
                    for r in range(4):
                        i = dma_s(out=outT["d_wfcT"][cj, 512 * r:512 * (r + 1), :],
                                  in_=wfcT_shc[cj, 512 * r:512 * (r + 1), :])
                        mark(i, "sDBG", 16)
                    for r in range(4):
                        i = dma_s(out=outT["d_wpjT"][cj, 512 * r:512 * (r + 1), :],
                                  in_=wpjT_shc[cj, 512 * r:512 * (r + 1), :])
                        mark(i, "sDBG", 16)
                weng("sync", "sDBG", 16 * 40)
                return

            ghx_hist = []
            gitc = [0]
            lfgc = [0]
            gLc = [0]
            gfcc = [0]
            for hh in range(2):
              for th in range(2):
                for itl in range(16):
                    j, rl = itl // 4, itl % 4
                    hg = 4 * (4 * hh + rl) + j
                    hgl = 4 * rl + j
                    git = gitc[0]
                    gitc[0] += 1
                    it = git
                    if eng == "sync":
                        if git == 0:
                            E.wait_ge(S["sTR"], C["sTR"])
                            E.wait_ge(S["sTRA"], C["sTRA"])
                        if hh == 0 and th == 0 and itl % 4 == 0:
                            E.wait_ge(S["sCC"], CCN["wfc%d" % j])
                        if git >= 2:
                            E.wait_ge(S["sH"], 2 * (git - 1))
                    i = dma_s(out=wstr[:, git % 2, :, :],
                              in_=wfcT_allc[j, 4 * hh + rl, :, :]
                              .rearrange("(k p) c -> p k c", p=128))
                    mark(i, "sFW%d" % (git % 2), 16)
                    for cc in range(2):
                        hcx = 2 * hgl + cc
                        ghx = 64 * hh + 32 * th + hcx
                        ghx_hist.append(ghx)
                        gx = 2 * git + cc
                        bk = 2 * (git % 2) + cc
                        if eng == "tensor":
                            if cc == 0:
                                E.wait_ge(S["sFW%d" % (git % 2)],
                                          16 * (git // 2 + 1))
                                if git == 0:
                                    wev(("TRE", 47))
                                    for _p4 in range(4):
                                        E.wait_ge(S["sX2S%d" % _p4], 32)
                                    E.wait_ge(S["sSTG"], 64)
                                if git in (32, 33):
                                    E.wait_ge(S["sSTG2"], 32)
                                if git >= 2:
                                    wev(("HE", ghx_hist[2 * (git - 2) + 1]))
                        i = None
                        for fc in range(16):
                            i = t_mm(pMM[:, 512 * bk:512 * (bk + 1)],
                                     wstr[:, git % 2, fc,
                                          128 * cc:128 * (cc + 1)],
                                     ln2T[:, fc, 512 * th:512 * (th + 1)],
                                     start=(fc == 0), stop=(fc == 15))
                        mark(i, "sH")
                        if eng == "scalar":
                            E.wait_ge(S["sH"], 2 * git + cc + 1)
                            if gx >= 2:
                                wev(("HE", ghx_hist[gx - 2]))
                        i = act(htmp[:, cc, :],
                                pMM[:, 512 * bk:512 * (bk + 1)],
                                AF.Relu, bias=0.0, scale=1.0)
                        mark(i, "sRELU")
                        if eng == "vector":
                            E.wait_ge(S["sRELU"], C["sRELU"])
                            if git == 0 and cc == 0:
                                for _p4 in range(4):
                                    E.wait_ge(S["sX2S%d" % _p4], 32)
                            if git == 32 and cc == 0:
                                E.wait_ge(S["sF2"], 64)
                        dst = (hsqA[:, hcx, :] if th == 0
                               else hsqB[:, hcx, :])
                        v_tt(dst, htmp[:, cc, :], htmp[:, cc, :],
                             ALU.mult)
                        dve_ev(("HE", ghx))
                    if hh == 0:
                        q_step(38)
                        if th == 0 and itl == 1:
                            q_flush_until(24)
                            lateT(list(range(16, 24)), wfc_shW)
                            emit_cc("wfc2", [wfcT_shc[2, :, :]],
                                    [wfcT_allc[2, :, :, :]])
                        elif th == 0 and itl == 5:
                            q_flush_until(32)
                            lateT(list(range(24, 32)), wfc_shW)
                            emit_cc("wfc3", [wfcT_shc[3, :, :]],
                                    [wfcT_allc[3, :, :, :]])
                        elif (th, itl) in ((0, 9), (0, 13)):
                            cjp = (itl - 9) // 4
                            q_flush_until(NQT_FC + 8 * (cjp + 1))
                            lateT(list(range(NQT_FC + 8 * cjp,
                                             NQT_FC + 8 * (cjp + 1))),
                                  wpj_shW)
                            emit_cc("wpj%d" % cjp, [wpjT_shc[cjp, :, :]],
                                    [wpjT_allc[cjp, :, :, :]])
                    else:
                        q_step(30)
                        if th == 0 and itl in (1, 5):
                            cjp = 2 + (itl - 1) // 4
                            q_flush_until(NQT_FC + 8 * (cjp + 1))
                            lateT(list(range(NQT_FC + 8 * cjp,
                                             NQT_FC + 8 * (cjp + 1))),
                                  wpj_shW)
                            emit_cc("wpj%d" % cjp, [wpjT_shc[cjp, :, :]],
                                    [wpjT_allc[cjp, :, :, :]])
              for th in range(2):
                for p4 in range(4):
                    tl = 4 * th + p4
                    if eng == "sync":
                        if hh == 0 and th == 0:
                            E.wait_ge(S["sOP"], 64)
                            E.wait_ge(S["sX2S%d" % p4], 32)
                        elif hh == 0:
                            E.wait_ge(S["sYS%d" % p4], 16)
                        else:
                            E.wait_ge(S["sYS%d" % p4], 32)
                            if th == 1:
                                E.wait_ge(S["sOUT%d" % p4], 16)
                    i = dma_s(out=outsb4[:, p4, :],
                              in_=(x2_spill if hh == 0 else y_part)
                              [128 * tl:128 * (tl + 1), :])
                    mark(i, "sXR2%d" % p4, 16)
                for fg in range(8):
                    lfg = lfgc[0]
                    lfgc[0] += 1
                    for hcgl in range(4):
                        cw = 2 * hh + hcgl // 2
                        L = gLc[0]
                        gLc[0] += 1
                        if eng == "sync":
                            if th == 0 and fg == 0:
                                E.wait_ge(S["sCC"], CCN["wpj%d" % cw])
                            if L >= 2:
                                E.wait_ge(S["sF2"], L - 1)
                        i = dma_s(out=wstr2[:, L % 2, :, :],
                                  in_=wpjT_allc[cw, fg,
                                                1024 * (hcgl % 2):
                                                1024 * (hcgl % 2 + 1), :]
                                  .rearrange("(k p) c -> p k c", p=128))
                        mark(i, "sF2W%d" % (L % 2), 16)
                        bk2 = 2 * (fg % 2)
                        if eng == "tensor":
                            E.wait_ge(S["sF2W%d" % (L % 2)], 16 * (L // 2 + 1))
                            if hcgl == 0:
                                if fg == 0:
                                    wev(("HE", 64 * hh + 31))
                                    wev(("HE", 64 * hh + 63))
                                if lfg >= 2:
                                    E.wait_ge(S["sSTG2"], 2 * (lfg - 1))
                        for hc8 in range(8):
                            hc = 8 * hcgl + hc8
                            for cc in range(2):
                                i = t_mm(pMM[:, 512 * (bk2 + cc):
                                             512 * (bk2 + cc + 1)],
                                         wstr2[:, L % 2, hc8,
                                               128 * cc:128 * (cc + 1)],
                                         hsqA[:, hc, :] if th == 0
                                         else hsqB[:, hc, :],
                                         start=(hc == 0), stop=(hc == 31))
                                if hc8 == 7 and cc == 1:
                                    mark(i, "sF2")
                    for cc in range(2):
                        fc = 2 * fg + cc
                        gfc = gfcc[0]
                        gfcc[0] += 1
                        bk2 = 2 * (fg % 2)
                        if eng == "scalar":
                            E.wait_ge(S["sF2"], 4 * (lfg + 1))
                            if gfc >= 2:
                                E.wait_ge(S["sFT2"], 4 * (gfc - 1))
                        i = act(stg[:, gfc % 2, :],
                                pMM[:, 512 * (bk2 + cc):512 * (bk2 + cc + 1)],
                                AF.Identity, bias=0.0, scale=1.0)
                        mark(i, "sSTG2")
                        G2 = FG[0]
                        pT32 = pSC[:, 512 * (G2 % 2):512 * (G2 % 2) + 512]
                        if eng == "tensor":
                            E.wait_ge(S["sSTG2"], gfc + 1)
                            if G2 >= 2:
                                wev(("FG", G2 - 2))
                        for p4 in range(4):
                            i = t_tr(pT32[:, 128 * p4:128 * (p4 + 1)],
                                     stg[:, gfc % 2, 128 * p4:128 * (p4 + 1)],
                                     consts_sb[:, CO_ID:CO_ID + 128])
                            if p4 == 3:
                                mark(i, "sFT2", 4)
                        for p4 in range(4):
                            if eng == "vector":
                                E.wait_ge(S["sFT2"], 4 * gfc + 4)
                                if fc == 0:
                                    E.wait_ge(S["sXR2%d" % p4],
                                              C["sXR2%d" % p4])
                            v_tt(outsb4[:, p4, 128 * fc:128 * (fc + 1)],
                                 outsb4[:, p4, 128 * fc:128 * (fc + 1)],
                                 pT32[:, 128 * p4:128 * (p4 + 1)], ALU.add)
                        dve_ev(("FG", G2))
                        dve_ev(("O", hh, th, fc))
                        FG[0] += 1
                for p4 in range(4):
                    tl = 4 * th + p4
                    if eng == "sync":
                        wev(("O", hh, th, 15))
                    if hh == 0:
                        i = dma_s(out=y_part[128 * tl:128 * (tl + 1), :],
                                  in_=outsb4[:, p4, :])
                        mark(i, "sYS%d" % p4, 16)
                    else:
                        i = dma_s(out=outT["out"][128 * tl:128 * (tl + 1), :],
                                  in_=outsb4[:, p4, :])
                        mark(i, "sOUT%d" % p4, 16)
            for _p4 in range(4):
                weng("sync", "sOUT%d" % _p4, 32)

        @block.sync
        def _(e):
            emit("sync", e)

        @block.gpsimd
        def _(e):
            emit("gpsimd", e)

        @block.vector
        def _(e):
            emit("vector", e)

        @block.scalar
        def _(e):
            emit("scalar", e)

        @block.tensor
        def _(e):
            emit("tensor", e)

    return nc, outT


from concourse.bass_utils import run_bass_kernel_spmd


def _make_consts():
    c = np.zeros((128, CO_W), np.float32)
    c[:, CO_ID:CO_ID + 128] = np.eye(128)
    c[:, CO_ONE:CO_ONE + 128] = 1.0
    c[:, CO_INVS:CO_INVS + 16] = np.array(INV_SHRINKS_F32, np.float32)[None, :]
    c[:, CO_NEGS:CO_NEGS + 16] = -np.array(SHRINKS_F32, np.float32)[None, :]
    c[:, CO_S:CO_S + 16] = np.array(SHRINKS_F32, np.float32)[None, :]
    return c


def kernel(L0_attn_x, L0_attn_w_qkv, L0_attn_w_o, L0_ffn_w_fc, L0_ffn_w_proj):
    x = np.ascontiguousarray(np.asarray(L0_attn_x, np.float32))
    w_qkv = np.ascontiguousarray(np.asarray(L0_attn_w_qkv, np.float32))
    w_o = np.ascontiguousarray(np.asarray(L0_attn_w_o, np.float32))
    w_fc = np.ascontiguousarray(np.asarray(L0_ffn_w_fc, np.float32))
    w_pj = np.ascontiguousarray(np.asarray(L0_ffn_w_proj, np.float32))

    nc, outT = build(5)
    consts = _make_consts()

    wqkv_j = []
    for j in range(2):
        cols = []
        for qkv3 in range(3):
            for h in range(8 * j, 8 * j + 8):
                cols.extend(range(qkv3 * 2048 + h * 128,
                                  qkv3 * 2048 + (h + 1) * 128))
        wqkv_j.append(np.ascontiguousarray(w_qkv[:, cols]))

    in_maps = []
    for c in range(8):
        b, j = c // 2, c % 2
        in_maps.append({
            "xb": x[b],
            "xrows": np.ascontiguousarray(x[b, 1024 * j:1024 * (j + 1), :]),
            "wqkv": wqkv_j[j],
            "wo": w_o,
            "wfc": np.ascontiguousarray(w_fc[1024 * c:1024 * (c + 1), :]),
            "wpj": np.ascontiguousarray(w_pj[256 * c:256 * (c + 1), :]),
            "consts": consts,
        })

    res = run_bass_kernel_spmd(nc, in_maps, core_ids=list(range(8))).results

    out = np.empty((4, 2048, 2048), np.float32)
    for c in range(8):
        b, j = c // 2, c % 2
        out[b, 1024 * j:1024 * (j + 1), :] = np.asarray(res[c]["out"])
    return out


# version 22 (final)
# speedup vs baseline: 1.0339x; 1.0152x over previous
import math
from contextlib import ExitStack
from collections import Counter

import numpy as np
import ml_dtypes

import concourse.bass as bass
import concourse.mybir as mybir
from concourse.bass import ds

F32 = mybir.dt.float32
BF16 = mybir.dt.bfloat16
I32 = mybir.dt.int32
AF = mybir.ActivationFunctionType
ALU = mybir.AluOpType
AX = mybir.AxisListType

D = 2048
SEQ = 2048
TOK = 1024
NHL = 8
HID = 8192
INV_SQRT_DH = float(np.float32(1.0) / np.float32(np.sqrt(np.float32(128.0))))
EPS = 1e-5
SHRINKS_F32 = [float(np.float32(s)) for s in np.linspace(0.4, 1.0, 16)]
INV_SHRINKS_F32 = [float(np.float32(1.0) / np.float32(s)) for s in SHRINKS_F32]
NS = 16
QMAX = 15.0
INV_QMAX = float(np.float32(1.0) / np.float32(15.0))
QF = 512
NQT_FC = 32
NQT = 64
NBLK = QF // 64
NXT = 16
NMG = 12
NEGBIG = -1e30

CO_ID = 0
CO_ONE = 128
CO_INVS = 256
CO_NEGS = 272
CO_S = 288
CO_W = 304


CAP0 = 212832


def _arena(nc, ctx):
    CAP = CAP0
    backing = ctx.enter_context(
        nc.sbuf_tensor("arena", [128, CAP], mybir.dt.uint8))
    base = nc.lookup_mloc(backing).addr

    def place(name, shape, dtype, off):
        per_part = int(np.prod(shape[1:])) * mybir.dt.size(dtype)
        assert off % 32 == 0 and off + per_part <= CAP, (name, off, per_part)
        h = nc.virtual_sbuf_tensor(name, list(shape), dtype)
        nc._tensor(h.name, list(h.shape), h.dtype, type="SB")
        mloc = nc.lookup_mloc(h)
        mloc.allocated = True
        mloc.addr = base + off
        return h

    return place


def build(stage: int):
    nc = bass.Bass("TRN2", target_bir_lowering=False, debug=False)

    xb = nc.dram_tensor("xb", [SEQ, D], F32, kind="ExternalInput")
    wqkv = nc.dram_tensor("wqkv", [D, 3 * NHL * 128], F32, kind="ExternalInput")
    wo = nc.dram_tensor("wo", [D, D], F32, kind="ExternalInput")
    wfc = nc.dram_tensor("wfc", [1024, D], F32, kind="ExternalInput")
    wpj = nc.dram_tensor("wpj", [256, HID], F32, kind="ExternalInput")
    consts = nc.dram_tensor("consts", [128, CO_W], F32, kind="ExternalInput")
    xrows = nc.dram_tensor("xrows", [TOK, D], F32, kind="ExternalInput")

    wqkv_bf = nc.dram_tensor("wqkv_bf", [D, 3 * NHL * 128], BF16, kind="Internal")
    vals_sh = nc.dram_tensor("vals_sh", [2, NHL, 128, SEQ], BF16,
                             kind="Internal", addr_space="Shared")
    bar_in = nc.dram_tensor("bar_in", [1, 4], F32, kind="Internal")
    bar_out = nc.dram_tensor("bar_out", [2, 1, 4], F32, kind="Internal")
    x2_spill = nc.dram_tensor("x2_spill", [TOK, D], F32, kind="Internal")
    y_part = nc.dram_tensor("y_part", [TOK, D], F32, kind="Internal")
    wfc_nat = nc.dram_tensor("wfc_nat", [1024, D], BF16, kind="Internal")
    wpj_nat = nc.dram_tensor("wpj_nat", [256, HID], BF16, kind="Internal")
    wfcT_shc = nc.dram_tensor("wfcT_shc", [4, D, 256], BF16, kind="Internal")
    wpjT_shc = nc.dram_tensor("wpjT_shc", [4, 2048, 256], BF16, kind="Internal")
    wfcT_allc = nc.dram_tensor("wfcT_allc", [4, 8, D, 256], BF16, kind="Internal")
    wpjT_allc = nc.dram_tensor("wpjT_allc", [4, 8, 2048, 256], BF16, kind="Internal")

    outT = {}
    if stage >= 5:
        outT["out"] = nc.dram_tensor("out", [TOK, D], F32, kind="ExternalOutput")
    if stage == 1:
        outT["d_qkv"] = nc.dram_tensor("d_qkv", [24, 128, SEQ], BF16, kind="ExternalOutput")
    if stage == 2:
        outT["d_vals"] = nc.dram_tensor("d_vals", [NHL, 128, SEQ], BF16, kind="ExternalOutput")
    if stage == 3:
        outT["d_x2"] = nc.dram_tensor("d_x2", [TOK, D], F32, kind="ExternalOutput")
        outT["d_ln2t"] = nc.dram_tensor("d_ln2t", [16, 128, TOK], BF16, kind="ExternalOutput")
    if stage == 4:
        outT["d_x2"] = nc.dram_tensor("d_x2", [TOK, D], F32, kind="ExternalOutput")
        outT["d_wfcT"] = nc.dram_tensor("d_wfcT", [4, D, 256], BF16, kind="ExternalOutput")
        outT["d_wpjT"] = nc.dram_tensor("d_wpjT", [4, 2048, 256], BF16, kind="ExternalOutput")

    with ExitStack() as ctx:
        place = _arena(nc, ctx)
        consts_sb = place("consts_sb", [128, CO_W], F32, 0)
        identb = place("identb", [128, 128], BF16, 1216)
        onesb = place("onesb", [128, 128], BF16, 1472)
        zbw = place("zbw", [128, 2, 128], BF16, 1728)
        lnstat = place("lnstat", [128, 4, 32], F32, 2752)
        zpart = place("zpart", [128, 2, 4], F32, 3264)
        zinv = place("zinv", [128, 2, 1], F32, 3296)
        qblk = place("qblk", [128, 2, NBLK, 8], F32, 3328)
        qerr = place("qerr", [128, 2, NBLK, NS], F32, 3840)
        qmsk = place("qmsk", [128, NBLK, NS], F32, 4864)
        qsel = place("qsel", [128, NBLK, NS], F32, 5376)
        A_END = 5888
        B0 = A_END
        ln1T = place("ln1T", [128, 16, TOK], BF16, B0)
        vxs = place("vxs", [128, 2 * NHL, TOK], BF16, B0)
        hsqA = place("hsqA", [128, 32, 512], BF16, B0)
        B_END = B0 + 32768
        C0 = B_END
        wosf = place("wosf", [128, 8, D], BF16, C0)
        htmp = place("htmp", [128, 2, 512], F32, C0 + 32768)
        wstr2 = place("wstr2", [128, 2, 8, 256], BF16, C0 + 36864)
        outsb4 = place("outsb4", [128, 4, D], F32, C0)
        C_END = C0 + 65536
        qkvs = place("qkvs", [128, 24, SEQ], BF16, C0)
        D0 = C_END
        ln2T = place("ln2T", [128, 16, TOK], BF16, D0)
        D_END = D0 + 32768
        E0 = D_END
        wqbB = place("wqbB", [128, 2, 16, 256], BF16, E0)
        pcst = place("pcst", [128, 2, 3072], BF16, E0 + 16384)
        x2t = place("x2t", [128, 4, D], F32, E0)
        hsqB = place("hsqB", [128, 32, 512], BF16, E0)
        E_END = E0 + 32768
        F0 = E_END
        xs = place("xs", [128, 2, D], F32, F0)
        Etile = place("Etile", [128, 2, SEQ], BF16, F0)
        ln2st = place("ln2st", [128, 4, D], BF16, F0)
        wstr = place("wstr", [128, 2, 16, 256], BF16, F0)
        F_END = F0 + 16384
        G0 = F_END
        lntile = place("lntile", [128, 2, D], BF16, G0)
        stg = place("stg", [128, 2, 512], F32, G0)
        qrl = place("qrl", [128, 2, QF], BF16, G0 + 4096)
        deqT = place("deqT", [128, 2, 4, 128], BF16, G0 + 6144)
        G_END = G0 + 8192
        Q0 = G_END
        qt_t = place("qt_t", [128, 2, QF], F32, Q0)
        qi = place("qi", [128, 4, QF], mybir.dt.int16, Q0 + 4096)
        qq = place("qq", [128, 4, QF], F32, Q0 + 8192)
        qsq = place("qsq", [128, 4, QF], F32, Q0 + 8192)
        deqb = place("deqb", [128, 2, QF], BF16, Q0 + 16384)
        qdq = place("qdq", [128, 2, QF], mybir.dt.int16, Q0 + 16384)
        Q_END = Q0 + 18432
        assert Q_END <= CAP0
        assert Q_END <= 212832, Q_END

        pMM = ctx.enter_context(nc.psum_tensor("pMM", [128, 2048], F32))
        pSC = ctx.enter_context(nc.psum_tensor("pSC", [128, 1024], F32))
        pTa = ctx.enter_context(nc.psum_tensor("pTa", [128, 1024], BF16))
        pTb = ctx.enter_context(nc.psum_tensor("pTb", [128, 1024], BF16))

        semnames = [
            "sDVE", "sPOOL", "sLNa", "sTR", "sQKV", "sQKVE",
            "sSC", "sEXP", "sCS", "sCC", "sTRA", "sVS0", "sVS1", "sVX0", "sVX1",
            "sOP", "sSTG",
            "sFT", "sLN2a", "sQI", "sQS", "sQQ",
            "sH", "sRELU", "sF2", "sSTG2", "sFT2",
            "sX0", "sX1", "sWQ0", "sWQ1", "sQW0", "sQW1",
            "sCB0", "sCB1", "sWOS0", "sWOS1", "sXN0", "sXN1", "sXN2", "sXN3",
            "sDNW0", "sDNW1", "sQRL0", "sQRL1", "sDW0", "sDW1",
            "sFW0", "sFW1", "sF2W0", "sF2W1",
            "sXR20", "sXR21", "sXR22", "sXR23",
            "sYS0", "sYS1", "sYS2", "sYS3",
            "sOUT0", "sOUT1", "sOUT2", "sOUT3",
            "sX2S0", "sX2S1", "sX2S2", "sX2S3",
            "sDBG", "sCONST", "sCONSTg", "sWQC", "sWQS0", "sWQS1",
        ]
        S = {n: ctx.enter_context(nc.semaphore(n)) for n in semnames}

        block = ctx.enter_context(nc.Block())

        def emit(eng, E):
            C = Counter()
            EV = {}

            def inc(sname, amt=1):
                C[sname] += amt
                return C[sname]

            def mark(ins, sname, amt=1):
                if ins is not None:
                    ins.then_inc(S[sname], amt)
                return inc(sname, amt)

            def ev(key):
                pass

            def wev(key):
                sem, val = EV[key]
                if E is not None and val > 0:
                    E.wait_ge(S[sem], val)

            def weng(target, sname, val):
                if eng == target and E is not None and val > 0:
                    E.wait_ge(S[sname], val)

            def _chain(fn_name, *a, **k):
                if eng == "vector":
                    E.wait_ge(S["sDVE"], C["sDVE"])
                    ins = getattr(E, fn_name)(*a, **k)
                    ins.then_inc(S["sDVE"], 1)
                else:
                    ins = None
                inc("sDVE")
                return ins

            def _pchain(fn_name, *a, **k):
                if eng == "gpsimd":
                    E.wait_ge(S["sPOOL"], C["sPOOL"])
                    ins = getattr(E, fn_name)(*a, **k)
                    ins.then_inc(S["sPOOL"], 1)
                else:
                    ins = None
                inc("sPOOL")
                return ins

            def v_ts(*a, **k): return _chain("tensor_scalar", *a, **k)
            def v_tt(*a, **k): return _chain("tensor_tensor", *a, **k)
            def v_stt(*a, **k): return _chain("scalar_tensor_tensor", *a, **k)
            def v_red(*a, **k): return _chain("tensor_reduce", *a, **k)
            def v_cp(*a, **k): return _chain("tensor_copy", *a, **k)
            def v_rcp(*a, **k): return _chain("reciprocal", *a, **k)
            def v_sel(*a, **k): return _chain("select", *a, **k)
            def v_bns(*a, **k): return _chain("bn_stats", *a, **k)
            def v_bna(*a, **k): return _chain("bn_aggr", *a, **k)
            def v_ms(*a, **k): return _chain("memset", *a, **k)
            def p_ts(*a, **k): return _pchain("tensor_scalar", *a, **k)
            def p_red(*a, **k): return _pchain("tensor_reduce", *a, **k)

            def dve_ev(key):
                EV[key] = ("sDVE", C["sDVE"])

            def pool_ev(key):
                EV[key] = ("sPOOL", C["sPOOL"])

            def act(*a, **k):
                return E.activation(*a, **k) if eng == "scalar" else None

            def t_mm(*a, **k):
                return E.matmul(*a, **k) if eng == "tensor" else None

            def t_tr(*a, **k):
                return E.transpose(*a, **k) if eng == "tensor" else None

            def dma_s(*a, **k):
                return E.dma_start(*a, **k) if eng == "sync" else None

            def dma_a(*a, **k):
                return E.dma_start(*a, **k) if eng == "scalar" else None

            def dma_g(*a, **k):
                return E.dma_start(*a, **k) if eng == "gpsimd" else None



            q_items = []
            if stage >= 4:
                tail = []
                for qt in range(NQT):
                    q_items.append(("setup", qt, 0))
                    for t in range(NS + 6):
                        if t % 2 == 0 and 0 <= t - 4 < NS:
                            q_items.append(("st2", qt, t - 4))
                        if t % 2 == 1 and 0 <= t - 5 < NS:
                            q_items.append(("sqp", qt, t - 5))
                        if t % 2 == 0 and 0 <= t - 6 < NS:
                            q_items.append(("redp", qt, t - 6))
                        if t % 2 == 0 and 0 <= t - 2 < NS:
                            q_items.append(("clip2", qt, t - 2))
                        if t < NS:
                            q_items.append(("cvt", qt, t))
                        if t == 2 and tail:
                            q_items.append(tail.pop(0))
                        if t == 4 and tail:
                            q_items.append(tail.pop(0))
                    tail = [("trk", qt, 0), ("deq", qt, 0)]
                q_items.extend(tail)
            q_pos = [0]

            def q_src(qt):
                if qt < NQT_FC:
                    return wfc[128 * (qt // 4):128 * (qt // 4 + 1),
                               QF * (qt % 4):QF * (qt % 4 + 1)]
                lq = qt - NQT_FC
                rt, ct = lq % 2, lq // 2
                return wpj[128 * rt:128 * (rt + 1), QF * ct:QF * (ct + 1)]

            def q_nat(qt):
                if qt < NQT_FC:
                    return wfc_nat[128 * (qt // 4):128 * (qt // 4 + 1),
                                   QF * (qt % 4):QF * (qt % 4 + 1)]
                lq = qt - NQT_FC
                rt, ct = lq % 2, lq // 2
                return wpj_nat[128 * rt:128 * (rt + 1), QF * ct:QF * (ct + 1)]

            def t3(sl):
                return sl.rearrange("p (b k) -> p b k", k=64)

            def emit_q_item(it):
                kind, qt, s = it
                p = qt % 2
                if kind == "setup":
                    if eng == "gpsimd" and qt >= 2:
                        wev(("DQ", qt - 2))
                    i = dma_g(out=qt_t[:, p, :].bitcast(F32), in_=q_src(qt))
                    mark(i, "sQW%d" % (qt % 2), 16)
                    if eng == "vector":
                        E.wait_ge(S["sQW%d" % (qt % 2)], 16 * (qt // 2 + 1))
                        if qt >= 2:
                            wev(("DQ", qt - 2))
                    v_red(qblk[:, p, :, 0:1], t3(qt_t[:, p, :]), AX.X, ALU.max,
                          apply_absolute_value=True)
                    v_ts(qblk[:, p, :, 1:2], qblk[:, p, :, 0:1], 1e-8,
                         INV_QMAX, ALU.max, ALU.mult)
                    v_rcp(qblk[:, p, :, 2:3], qblk[:, p, :, 1:2])
                    v_tt(t3(qt_t[:, p, :]), t3(qt_t[:, p, :]),
                         qblk[:, p, :, 2:3].broadcast_to([128, NBLK, 64]),
                         ALU.mult)
                    dve_ev(("QSU", qt))
                elif kind == "cvt":
                    n = NS * qt + s
                    i4 = n % 4
                    if eng == "scalar":
                        wev(("QSU", qt))
                        if n >= 4:
                            wev(("QDd", (n - 4) | 1))
                    i = act(qi[:, i4, :], qt_t[:, p, :], AF.Identity,
                            bias=0.0, scale=INV_SHRINKS_F32[s])
                    mark(i, "sQI")
                elif kind == "clip2":
                    n = NS * qt + s
                    i4 = n % 4
                    if eng == "gpsimd":
                        E.wait_ge(S["sQI"], n + 2)
                    p_ts(qi[:, i4:i4 + 2, :].rearrange("p a b -> p (a b)"),
                         qi[:, i4:i4 + 2, :].rearrange("p a b -> p (a b)"),
                         QMAX, -QMAX, ALU.min, ALU.max)
                    pool_ev(("QC", n + 1))
                elif kind == "st2":
                    n = NS * qt + s
                    s4 = n % 4
                    if eng == "vector":
                        wev(("QC", n + 1))
                        if n >= 4:
                            wev(("QR", (n - 4) | 1))
                    v_stt(qq[:, s4, :], qi[:, s4, :], -SHRINKS_F32[s],
                          qt_t[:, p, :], ALU.mult, ALU.add)
                    v_stt(qq[:, s4 + 1, :], qi[:, s4 + 1, :],
                          -SHRINKS_F32[s + 1], qt_t[:, p, :],
                          ALU.mult, ALU.add)
                    dve_ev(("QDd", n + 1))
                elif kind == "sqp":
                    n = NS * qt + s
                    s4 = n % 4
                    if eng == "scalar":
                        wev(("QDd", n + 1))
                    i = act(qsq[:, s4:s4 + 2, :].rearrange("p a b -> p (a b)"),
                            qq[:, s4:s4 + 2, :].rearrange("p a b -> p (a b)"),
                            AF.Square)
                    mark(i, "sQS")
                elif kind == "redp":
                    n = NS * qt + s
                    s4 = n % 4
                    if eng == "vector":
                        E.wait_ge(S["sQS"], C["sQS"])
                        if s == 0 and qt >= 2:
                            wev(("QT", qt - 2))
                    v_red(qerr[:, p, :, s:s + 2]
                          .rearrange("p b a -> p a b"),
                          qsq[:, s4:s4 + 2, :]
                          .rearrange("p a (b k) -> p a b k", k=64),
                          AX.X, ALU.add)
                    dve_ev(("QR", n + 1))
                elif kind == "trk":
                    if eng == "vector":
                        E.wait_ge(S["sCONST"], 16)
                        wev(("QR", NS * qt + NS - 1))
                    err = qerr[:, p, :, :]
                    v_red(qblk[:, p, :, 3:4], err, AX.X, ALU.min)
                    v_tt(qmsk[:, :, :], err,
                         qblk[:, p, :, 3:4].broadcast_to([128, NBLK, NS]),
                         ALU.is_equal)
                    v_tt(qsel[:, :, :], qmsk[:, :, :],
                         consts_sb[:, CO_INVS:CO_INVS + NS]
                         .rearrange("p (o s) -> p o s", o=1).broadcast_to([128, NBLK, NS]),
                         ALU.mult)
                    v_red(qblk[:, p, :, 4:5], qsel[:, :, :], AX.X, ALU.max)
                    v_tt(qmsk[:, :, :],
                         consts_sb[:, CO_INVS:CO_INVS + NS]
                         .rearrange("p (o s) -> p o s", o=1)
                         .broadcast_to([128, NBLK, NS]),
                         qblk[:, p, :, 4:5].broadcast_to([128, NBLK, NS]),
                         ALU.is_equal)
                    v_tt(qsel[:, :, :], qmsk[:, :, :],
                         consts_sb[:, CO_S:CO_S + NS]
                         .rearrange("p (o s) -> p o s", o=1)
                         .broadcast_to([128, NBLK, NS]),
                         ALU.mult)
                    v_red(qblk[:, p, :, 5:6], qsel[:, :, :], AX.X, ALU.max)
                    v_tt(qblk[:, p, :, 6:7], qblk[:, p, :, 1:2],
                         qblk[:, p, :, 5:6], ALU.mult)
                    dve_ev(("QT", qt))
                elif kind == "deq":
                    if eng == "vector":
                        if qt >= 2:
                            E.wait_ge(S["sDNW%d" % (qt % 2)], 16 * (qt // 2))
                    v_tt(t3(qt_t[:, p, :]), t3(qt_t[:, p, :]),
                         qblk[:, p, :, 4:5].broadcast_to([128, NBLK, 64]),
                         ALU.mult)
                    dve_ev(("U2", qt))
                    if eng == "scalar":
                        wev(("U2", qt))
                        if qt >= 1:
                            E.wait_ge(S["sDNW%d" % ((qt - 1) % 2)],
                                      16 * ((qt - 1) // 2 + 1))
                    i = act(qdq[:, 1 - p, :], qt_t[:, p, :], AF.Identity,
                            bias=0.0, scale=1.0)
                    mark(i, "sQQ")
                    if eng == "gpsimd":
                        E.wait_ge(S["sQQ"], C["sQQ"])
                    p_ts(qdq[:, 1 - p, :], qdq[:, 1 - p, :], QMAX, -QMAX,
                         ALU.min, ALU.max)
                    pool_ev(("QC2", qt))
                    if eng == "vector":
                        wev(("QC2", qt))
                    v_tt(t3(deqb[:, p, :]), t3(qdq[:, 1 - p, :]),
                         qblk[:, p, :, 6:7].broadcast_to([128, NBLK, 64]),
                         ALU.mult)
                    dve_ev(("DQ", qt))
                    if eng == "gpsimd":
                        wev(("DQ", qt))
                    i = dma_g(out=q_nat(qt), in_=deqb[:, p, :])
                    mark(i, "sDNW%d" % (qt % 2), 16)

            def q_step(k):
                if stage < 4:
                    return
                n = 0
                while n < k and q_pos[0] < len(q_items):
                    emit_q_item(q_items[q_pos[0]])
                    q_pos[0] += 1
                    n += 1

            def q_flush_until(qt_lim):
                if stage < 4:
                    return
                last = -1
                for ii in range(len(q_items)):
                    if q_items[ii][1] < qt_lim:
                        last = ii
                while q_pos[0] <= last:
                    emit_q_item(q_items[q_pos[0]])
                    q_pos[0] += 1

            LQG = [0]
            lateT_hist = []

            def lateT(qt_list, shW_of):
                for qt in qt_list:
                    g = LQG[0]
                    sl = g % 2
                    if eng == "gpsimd":
                        E.wait_ge(S["sDNW%d" % (qt % 2)], 16 * (qt // 2 + 1))
                        if g >= 2:
                            wev(("DT", lateT_hist[g - 2]))
                    i = dma_g(out=qrl[:, sl, :], in_=q_nat(qt))
                    mark(i, "sQRL%d" % sl, 16)
                    G = TRG[0]
                    pT = pTa if G % 2 == 0 else pTb
                    if eng == "tensor":
                        E.wait_ge(S["sQRL%d" % sl], C["sQRL%d" % sl])
                        if G >= 2:
                            wev(("TRE", G - 2))
                    for k in range(4):
                        i = t_tr(pT[:, 128 * k:128 * (k + 1)],
                                 qrl[:, sl, 128 * k:128 * (k + 1)],
                                 identb[:, :])
                        if k == 3:
                            mark(i, "sTR")
                    if eng == "vector":
                        E.wait_ge(S["sTR"], C["sTR"])
                        E.wait_ge(S["sDW%d" % sl], C["sDW%d" % sl])
                    v_cp(deqT[:, sl, :, :],
                         pT[:, 0:512].rearrange("p (a b) -> p a b", b=128))
                    dve_ev(("DT", qt))
                    dve_ev(("TRE", G))
                    if eng == "gpsimd":
                        wev(("DT", qt))
                    i = dma_g(out=shW_of(qt), in_=deqT[:, sl, :, :])
                    mark(i, "sDW%d" % sl, 16)
                    TRG[0] += 1
                    lateT_hist.append(qt)
                    LQG[0] += 1

            def wfc_shW(qt):
                cj = qt // 8
                c0 = 128 * ((qt // 4) % 2)
                f0 = QF * (qt % 4)
                return wfcT_shc[cj, f0:f0 + 512, c0:c0 + 128] \
                    .rearrange("(k p) c -> p k c", p=128)

            def wpj_shW(qt):
                lq = qt - NQT_FC
                cj = lq // 8
                h0 = 512 * ((lq // 2) % 4)
                c0 = 128 * (lq % 2)
                return wpjT_shc[cj, h0:h0 + 512, c0:c0 + 128] \
                    .rearrange("(k p) c -> p k c", p=128)

            CCN = {}

            def emit_cc(name, ins_l, outs_l, groups=None):
                if eng == "gpsimd":
                    E.wait_ge(S["sDW0"], C["sDW0"])
                    E.wait_ge(S["sDW1"], C["sDW1"])
                i = (E.collective_compute(
                    "AllGather", ALU.bypass,
                    replica_groups=(groups if groups is not None
                                    else [[0, 1, 2, 3, 4, 5, 6, 7]]),
                    ins=ins_l, outs=outs_l)
                    if eng == "gpsimd" else None)
                mark(i, "sCC")
                CCN[name] = C["sCC"]

            i = dma_s(out=consts_sb[:, :], in_=consts[:, :])
            mark(i, "sCONST", 16)
            weng("sync", "sCONST", 16)
            i = dma_s(out=bar_in[:, :], in_=consts_sb[0:1, 0:4])
            mark(i, "sCONST", 16)
            weng("gpsimd", "sCONST", 16)
            i = dma_g(out=identb[:, :], in_=consts_sb[:, CO_ID:CO_ID + 128])
            mark(i, "sCONSTg", 16)
            i = dma_g(out=onesb[:, :], in_=consts_sb[:, CO_ONE:CO_ONE + 128])
            mark(i, "sCONSTg", 16)

            i = dma_g(out=wqkv_bf[:, :], in_=wqkv[:, :])
            mark(i, "sWQS0", 16)

            q_step(60)

            TRG = [0]
            trev = {}

            def emit_ln1_tile(i_t, xsrc, stoff, sema, lsl, evtag):
                for sb4 in range(4):
                    v_bns(lnstat[:, lsl, 6 * sb4:6 * sb4 + 6],
                          xsrc[:, 512 * sb4:512 * (sb4 + 1)])
                v_bna(lnstat[:, lsl, 24:26], lnstat[:, lsl, 0:24])
                v_ts(lnstat[:, lsl, 25:26], lnstat[:, lsl, 25:26],
                     EPS, None, ALU.add)
                dve_ev((evtag, i_t, "e"))
                if eng == "scalar":
                    wev((evtag, i_t, "e"))
                i = act(lnstat[:, lsl, 26:27], lnstat[:, lsl, 25:26], AF.Sqrt)
                mark(i, sema)
                if eng == "vector":
                    E.wait_ge(S[sema], C[sema])
                v_rcp(lnstat[:, lsl, 27:28], lnstat[:, lsl, 26:27])
                v_stt(lnstat[:, lsl, 28:29], lnstat[:, lsl, 24:25],
                      -1.0, lnstat[:, lsl, 27:28], ALU.mult, ALU.mult)
                dve_ev((evtag, i_t, "n"))

            def ln1_tile(i_t):
                if eng == "sync" and i_t >= 2:
                    E.wait_ge(S["sLNa"], 2 * (i_t - 2) + 2)
                i = dma_s(out=xs[:, i_t % 2, :], in_=xb[128 * i_t:128 * (i_t + 1), :])
                mark(i, "sX%d" % (i_t % 2), 16)
                if eng == "vector":
                    E.wait_ge(S["sX%d" % (i_t % 2)], 16 * (i_t // 2 + 1))
                emit_ln1_tile(i_t, xs[:, i_t % 2, :], 0, "sLNa", i_t % 2, "L1")
                if eng == "scalar":
                    wev(("L1", i_t, "n"))
                    if i_t >= 2:
                        E.wait_ge(S["sTR"], 2 * (i_t - 2) + 2)
                i = act(lntile[:, i_t % 2, :], xs[:, i_t % 2, :], AF.Identity,
                        bias=lnstat[:, i_t % 2, 28:29],
                        scale=lnstat[:, i_t % 2, 27:28])
                mark(i, "sLNa")

            def ln1T_tile(i_t, half):
                for g in range(2):
                    G = TRG[0]
                    pT = pTa if G % 2 == 0 else pTb
                    if eng == "tensor":
                        E.wait_ge(S["sLNa"], 2 * i_t + 2)
                        E.wait_ge(S["sCONSTg"], 32)
                        if G >= 2:
                            wev(("TRE", G - 2))
                    for k in range(8):
                        fc = 8 * g + k
                        i = t_tr(pT[:, 128 * k:128 * (k + 1)],
                                 lntile[:, i_t % 2, 128 * fc:128 * (fc + 1)],
                                 identb[:, :])
                        if k == 7:
                            mark(i, "sTR")
                    if eng == "vector":
                        E.wait_ge(S["sTR"], G + 1)
                        if G == 16:
                            E.wait_ge(S["sQKV"], 24)
                    tloc = (i_t % 8) * 128
                    v_cp(ln1T[:, 8 * g:8 * g + 8, tloc:tloc + 128],
                         pT[:, :].rearrange("p (a b) -> p a b", b=128))
                    dve_ev(("TRE", G))
                    TRG[0] += 1

            def qkv_half(half):
                for mg in range(NMG):
                    gmg = half * NMG + mg
                    if eng == "sync":
                        if gmg == 0:
                            E.wait_ge(S["sWQS0"], 16)
                        if gmg >= 2:
                            E.wait_ge(S["sQKV"], 2 * (gmg - 1))
                    i = dma_s(out=wqbB[:, gmg % 2, :, :],
                              in_=wqkv_bf[:, 256 * mg:256 * (mg + 1)]
                              .rearrange("(k p) c -> p k c", p=128))
                    mark(i, "sWQ%d" % (gmg % 2), 16)
                    for m2 in range(2):
                        m = 2 * mg + m2
                        Mg = 24 * half + m
                        bp = Mg % 2
                        if eng == "tensor":
                            if m2 == 0:
                                E.wait_ge(S["sWQ%d" % (gmg % 2)],
                                          16 * (gmg // 2 + 1))
                                wev(("TRE", 16 * (half + 1) - 1))
                            if Mg >= 2:
                                E.wait_ge(S["sQKVE"], 2 * Mg - 2)
                        for k in range(16):
                            for n in range(2):
                                i = t_mm(pMM[:, 1024 * bp + 512 * n:
                                             1024 * bp + 512 * (n + 1)],
                                         wqbB[:, gmg % 2, k, 128 * m2:128 * (m2 + 1)],
                                         ln1T[:, k, 512 * n:512 * (n + 1)],
                                         start=(k == 0), stop=(k == 15))
                                if k == 15 and n == 1:
                                    mark(i, "sQKV")
                        scl = INV_SQRT_DH if m < 8 else 1.0
                        for n in range(2):
                            if eng == "scalar":
                                E.wait_ge(S["sQKV"], Mg + 1)
                            i = act(qkvs[:, m, 1024 * half + 512 * n:
                                          1024 * half + 512 * (n + 1)],
                                    pMM[:, 1024 * bp + 512 * n:
                                        1024 * bp + 512 * (n + 1)],
                                    AF.Identity, bias=0.0, scale=scl)
                            mark(i, "sQKVE")
                    q_step(26)

            for i_t in range(10):
                ln1_tile(i_t)
                if i_t < 8:
                    ln1T_tile(i_t, 0)
            qkv_half(0)
            for i_t in range(8, 16):
                if 10 <= i_t + 2 < 16:
                    ln1_tile(i_t + 2)
                ln1T_tile(i_t, 1)
            qkv_half(1)

            if stage == 1:
                for m in range(24):
                    weng("sync", "sQKVE", 96)
                    i = dma_s(out=outT["d_qkv"][m, :, :], in_=qkvs[:, m, :])
                    mark(i, "sDBG", 16)
                weng("sync", "sDBG", 16 * 24)
                return

            if eng == "sync":
                pidv = E.partition_id()
                offmy = (pidv % 2) * 1024
            pend_cs = [None]

            def colsum_emit(slot):
                h_, qt_ = slot // 16, slot % 16
                if eng == "tensor":
                    wev(("ZB", slot))
                    if qt_ == 0 and h_ >= 1:
                        wev(("VAL", h_ - 1))
                for kt in range(4):
                    i = t_mm(pMM[:, 512 * kt:512 * (kt + 1)],
                             zbw[:, slot % 2, :],
                             Etile[:, qt_ % 2, 512 * kt:512 * (kt + 1)],
                             start=(qt_ == 0), stop=(qt_ == 15))
                    if kt == 3:
                        mark(i, "sCS")

            def head_finish(h):
                if eng == "vector":
                    E.wait_ge(S["sCS"], 16 * (h + 1))
                for kt in range(4):
                    v_tt(qkvs[:, 16 + h, 512 * kt:512 * (kt + 1)],
                         qkvs[:, 16 + h, 512 * kt:512 * (kt + 1)],
                         pMM[:, 512 * kt:512 * (kt + 1)], ALU.mult)
                dve_ev(("VAL", h))
                if eng == "sync":
                    wev(("VAL", h))
                i = dma_s(out=vals_sh[ds(pidv % 2, 1), h, :, :]
                          if eng == "sync" else None,
                          in_=qkvs[:, 16 + h, :])
                mark(i, "sVS%d" % (h % 2), 16)

            for h in range(NHL):
                if h >= 1:
                    head_finish(h - 1)
                for qt in range(16):
                    slot = 16 * h + qt
                    for kq in range(4):
                        qidx = 4 * slot + kq
                        sb5 = 512 * (qidx % 2)
                        if eng == "tensor":
                            if qidx == 0:
                                E.wait_ge(S["sQKVE"], 96)
                            if qidx >= 2:
                                E.wait_ge(S["sEXP"], qidx - 1)
                        i = t_mm(pSC[:, sb5:sb5 + 512],
                                 qkvs[:, h, 128 * qt:128 * (qt + 1)],
                                 qkvs[:, 8 + h, 512 * kq:512 * (kq + 1)],
                                 start=True, stop=True)
                        mark(i, "sSC")
                        if eng == "scalar":
                            E.wait_ge(S["sSC"], qidx + 1)
                            if slot >= 2 and kq == 0:
                                E.wait_ge(S["sCS"], slot - 1)
                        i = act(Etile[:, qt % 2, 512 * kq:512 * (kq + 1)],
                                pSC[:, sb5:sb5 + 512], AF.Exp,
                                bias=0.0, scale=1.0,
                                accum_out=zpart[:, slot % 2, kq:kq + 1])
                        mark(i, "sEXP")
                    if eng == "vector":
                        E.wait_ge(S["sEXP"], 4 * slot + 4)
                        if slot >= 2:
                            E.wait_ge(S["sCS"], slot - 1)
                    v_red(zinv[:, slot % 2, 0:1],
                          zpart[:, slot % 2, :]
                          .rearrange("p (o s) -> p o s", o=1),
                          AX.X, ALU.add)
                    v_rcp(zinv[:, slot % 2, 0:1], zinv[:, slot % 2, 0:1])
                    v_ts(zbw[:, slot % 2, :], onesb[:, 0:128],
                         zinv[:, slot % 2, 0:1], None, ALU.mult)
                    dve_ev(("ZB", slot))
                    if pend_cs[0] is not None:
                        colsum_emit(pend_cs[0])
                    pend_cs[0] = slot
                if stage >= 4 and h == 4:
                    q_flush_until(8)
                    lateT(list(range(0, 8)), wfc_shW)
                    emit_cc("wfc0", [wfcT_shc[0, :, :]],
                            [wfcT_allc[0, :, :, :]])
            if pend_cs[0] is not None:
                colsum_emit(pend_cs[0])
                pend_cs[0] = None
            head_finish(NHL - 1)
            if eng == "gpsimd":
                E.wait_ge(S["sCONST"], 32)
                E.wait_ge(S["sVS0"], 16 * 4)
                E.wait_ge(S["sVS1"], 16 * 4)
            emit_cc("bar", [bar_in[:, :]], [bar_out[:, :, :]],
                    groups=[[0, 1], [2, 3], [4, 5], [6, 7]])
            if eng == "gpsimd":
                pidv_g = E.partition_id()
                offmy_g = (pidv_g % 2) * 1024
                E.wait_ge(S["sCC"], CCN["bar"])
            for r in range(2):
                for hl in range(NHL):
                    i = dma_g(out=vxs[:, 8 * r + hl, :],
                              in_=vals_sh[r, hl, :, ds(offmy_g, TOK)]
                              if eng == "gpsimd" else None)
                    mark(i, "sVX%d" % (hl % 2), 16)

            if stage == 2:
                for hl in range(NHL):
                    if eng == "sync":
                        wev(("VAL", NHL - 1))
                    i = dma_s(out=outT["d_vals"][hl, :, :], in_=qkvs[:, 16 + hl, :])
                    mark(i, "sDBG", 16)
                weng("sync", "sDBG", 16 * NHL)
                return

            q_step(40)

            FG = [0]
            fgev = {}
            for n in range(2):
                for p4 in range(4):
                    tl = 4 * n + p4
                    if eng == "sync":
                        E.wait_ge(S["sQKV"], 48)
                        if n == 1:
                            E.wait_ge(S["sLN2a"], 2 * (tl - 4) + 2)
                            E.wait_ge(S["sX2S%d" % p4], 16)
                    i = dma_s(out=x2t[:, p4, :],
                              in_=xrows[128 * tl:128 * (tl + 1), :])
                    mark(i, "sXN%d" % p4, 16)
                for p8 in range(2):
                    wk = 2 * n + p8
                    for hc in range(8):
                        if eng == "gpsimd":
                            if wk == 0 and hc == 0:
                                E.wait_ge(S["sCS"], 128)
                            if wk >= 1 and hc == 0:
                                E.wait_ge(S["sOP"], 16 * wk)
                        i = dma_g(out=wosf[:, hc, :],
                                  in_=wo[1024 * p8 + 128 * hc:
                                         1024 * p8 + 128 * (hc + 1), :])
                        mark(i, "sWOS%d" % (wk % 2), 16)
                    for fc in range(16):
                        gidx = wk * 16 + fc
                        r2 = gidx % 2
                        if eng == "tensor":
                            if fc == 0:
                                E.wait_ge(S["sVX0"], 16 * NHL)
                                E.wait_ge(S["sVX1"], 16 * NHL)
                                E.wait_ge(S["sWOS%d" % (wk % 2)],
                                          128 * (wk // 2 + 1))
                            if gidx >= 2:
                                E.wait_ge(S["sSTG"], gidx - 1)
                            if gidx < 2:
                                E.wait_ge(S["sEXP"], 512)
                        for hc in range(8):
                            rhs = vxs[:, 8 * p8 + hc, 512 * n:512 * (n + 1)]
                            i = t_mm(pSC[:, 512 * r2:512 * (r2 + 1)],
                                     wosf[:, hc, 128 * fc:128 * (fc + 1)],
                                     rhs, start=(hc == 0), stop=(hc == 7))
                            if hc == 7:
                                mark(i, "sOP")
                        if eng == "scalar":
                            E.wait_ge(S["sOP"], gidx + 1)
                            if gidx >= 2:
                                E.wait_ge(S["sFT"], 4 * (gidx - 1))
                        i = act(stg[:, r2, :], pSC[:, 512 * r2:512 * (r2 + 1)],
                                AF.Identity, bias=0.0, scale=1.0)
                        mark(i, "sSTG")
                        G2 = FG[0]
                        pT32 = pMM[:, 512 * (G2 % 2):512 * (G2 % 2) + 512]
                        if eng == "tensor":
                            E.wait_ge(S["sSTG"], gidx + 1)
                            if G2 >= 2:
                                wev(("FG", G2 - 2))
                            if G2 < 2:
                                wev(("VAL", NHL - 1))
                        for p4 in range(4):
                            i = t_tr(pT32[:, 128 * p4:128 * (p4 + 1)],
                                     stg[:, r2, 128 * p4:128 * (p4 + 1)],
                                     consts_sb[:, CO_ID:CO_ID + 128])
                            if p4 == 3:
                                mark(i, "sFT", 4)
                        for p4 in range(4):
                            tl = 4 * n + p4
                            if eng == "vector":
                                E.wait_ge(S["sFT"], 4 * gidx + 4)
                                if p8 == 0 and fc == 0:
                                    E.wait_ge(S["sXN%d" % p4], 16 * (n + 1))
                            v_tt(x2t[:, p4, 128 * fc:128 * (fc + 1)],
                                 x2t[:, p4, 128 * fc:128 * (fc + 1)],
                                 pT32[:, 128 * p4:128 * (p4 + 1)],
                                 ALU.add)
                        dve_ev(("FG", G2))
                        FG[0] += 1
                        q_step(7)
                for p4 in range(4):
                    t = 4 * n + p4
                    if eng == "sync":
                        wev(("FG", (2 * n + 1) * 16 + 15))
                    i = dma_s(out=x2_spill[128 * t:128 * (t + 1), :],
                              in_=x2t[:, p4, :])
                    mark(i, "sX2S%d" % p4, 16)
                if n == 0:
                    for p4 in range(4):
                        t = p4
                        if eng == "vector":
                            wev(("FG", 31))
                        emit_ln1_tile(t, x2t[:, p4, :], 0, "sLN2a",
                                      2 + t % 2, "L2")
                        if eng == "scalar":
                            wev(("L2", t, "n"))
                            E.wait_ge(S["sCS"], 128)
                        i = act(ln2st[:, t % 4, :], x2t[:, p4, :], AF.Identity,
                                bias=lnstat[:, 2 + t % 2, 28:29],
                                scale=lnstat[:, 2 + t % 2, 27:28])
                        mark(i, "sLN2a")
                q_step(50)
                if stage >= 4 and n == 0:
                    q_flush_until(16)
                    lateT(list(range(8, 16)), wfc_shW)
                    emit_cc("wfc1", [wfcT_shc[1, :, :]],
                            [wfcT_allc[1, :, :, :]])

            def ln2T_tiles(trange):
                for t in trange:
                    for g in range(2):
                        G = TRG[0]
                        pT = pTa if G % 2 == 0 else pTb
                        if eng == "tensor":
                            E.wait_ge(S["sLN2a"], 2 * t + 2)
                            E.wait_ge(S["sOP"], 64)
                            if G >= 2:
                                wev(("TRE", G - 2))
                        for k in range(8):
                            fc = 8 * g + k
                            i = t_tr(pT[:, 128 * k:128 * (k + 1)],
                                     ln2st[:, t % 4, 128 * fc:128 * (fc + 1)],
                                     identb[:, :])
                            if k == 7:
                                mark(i, "sTR")
                            else:
                                mark(i, "sTRA")
                        if eng == "vector":
                            E.wait_ge(S["sTR"], G + 1)
                            if G == 32:
                                E.wait_ge(S["sOP"], 64)
                        tloc = 128 * t
                        v_cp(ln2T[:, 8 * g:8 * g + 8, tloc:tloc + 128],
                             pT[:, :].rearrange("p (a b) -> p a b", b=128))
                        dve_ev(("TRE", G))
                        TRG[0] += 1

            ln2T_tiles(range(0, 4))
            q_step(60)
            for p4 in range(4):
                t = 4 + p4
                if eng == "vector":
                    wev(("FG", 63))
                emit_ln1_tile(t, x2t[:, p4, :], 0, "sLN2a", 2 + t % 2, "L2")
                if eng == "scalar":
                    wev(("L2", t, "n"))
                    E.wait_ge(S["sTR"], 32 + 2 * (t - 4) + 2)
                i = act(ln2st[:, t % 4, :], x2t[:, p4, :], AF.Identity,
                        bias=lnstat[:, 2 + t % 2, 28:29],
                        scale=lnstat[:, 2 + t % 2, 27:28])
                mark(i, "sLN2a")
            q_step(60)
            ln2T_tiles(range(4, 8))

            if stage == 3:
                for _p4 in range(4):
                    weng("sync", "sX2S%d" % _p4, 32)
                for t in range(8):
                    i = dma_s(out=outT["d_x2"][128 * t:128 * (t + 1), :],
                              in_=x2_spill[128 * t:128 * (t + 1), :])
                    mark(i, "sDBG", 16)
                for fcg in range(16):
                    if eng == "sync":
                        wev(("TRE", 47))
                    i = dma_s(out=outT["d_ln2t"][fcg, :, :], in_=ln2T[:, fcg, :])
                    mark(i, "sDBG", 16)
                weng("sync", "sDBG", 16 * 24)
                return

            if stage == 4:
                for cj in range(2, 4):
                    q_flush_until(8 * (cj + 1))
                    lateT(list(range(8 * cj, 8 * (cj + 1))), wfc_shW)
                    emit_cc("wfc%d" % cj,
                            [wfcT_shc[cj, :, :]], [wfcT_allc[cj, :, :, :]])
                for cj in range(4):
                    q_flush_until(NQT_FC + 8 * (cj + 1))
                    lateT(list(range(NQT_FC + 8 * cj, NQT_FC + 8 * (cj + 1))),
                          wpj_shW)
                    emit_cc("wpj%d" % cj,
                            [wpjT_shc[cj, :, :]], [wpjT_allc[cj, :, :, :]])
                for _p4 in range(4):
                    weng("sync", "sX2S%d" % _p4, 32)
                for t in range(8):
                    i = dma_s(out=outT["d_x2"][128 * t:128 * (t + 1), :],
                              in_=x2_spill[128 * t:128 * (t + 1), :])
                    mark(i, "sDBG", 16)
                weng("sync", "sDW0", 16 * NQT // 2)
                weng("sync", "sDW1", 16 * NQT // 2)
                for cj in range(4):
                    for r in range(4):
                        i = dma_s(out=outT["d_wfcT"][cj, 512 * r:512 * (r + 1), :],
                                  in_=wfcT_shc[cj, 512 * r:512 * (r + 1), :])
                        mark(i, "sDBG", 16)
                    for r in range(4):
                        i = dma_s(out=outT["d_wpjT"][cj, 512 * r:512 * (r + 1), :],
                                  in_=wpjT_shc[cj, 512 * r:512 * (r + 1), :])
                        mark(i, "sDBG", 16)
                weng("sync", "sDBG", 16 * 40)
                return

            ghx_hist = []
            gitc = [0]
            lfgc = [0]
            gLc = [0]
            gfcc = [0]
            for hh in range(2):
              for th in range(2):
                for itl in range(16):
                    j, rl = itl // 4, itl % 4
                    hg = 4 * (4 * hh + rl) + j
                    hgl = 4 * rl + j
                    git = gitc[0]
                    gitc[0] += 1
                    it = git
                    if eng == "sync":
                        if git == 0:
                            E.wait_ge(S["sTR"], C["sTR"])
                            E.wait_ge(S["sTRA"], C["sTRA"])
                        if hh == 0 and th == 0 and itl % 4 == 0:
                            E.wait_ge(S["sCC"], CCN["wfc%d" % j])
                        if git >= 2:
                            E.wait_ge(S["sH"], 2 * (git - 1))
                    i = dma_s(out=wstr[:, git % 2, :, :],
                              in_=wfcT_allc[j, 4 * hh + rl, :, :]
                              .rearrange("(k p) c -> p k c", p=128))
                    mark(i, "sFW%d" % (git % 2), 16)
                    for cc in range(2):
                        hcx = 2 * hgl + cc
                        ghx = 64 * hh + 32 * th + hcx
                        ghx_hist.append(ghx)
                        gx = 2 * git + cc
                        bk = 2 * (git % 2) + cc
                        if eng == "tensor":
                            if cc == 0:
                                E.wait_ge(S["sFW%d" % (git % 2)],
                                          16 * (git // 2 + 1))
                                if git == 0:
                                    wev(("TRE", 47))
                                    for _p4 in range(4):
                                        E.wait_ge(S["sX2S%d" % _p4], 32)
                                    E.wait_ge(S["sSTG"], 64)
                                if git in (32, 33):
                                    E.wait_ge(S["sSTG2"], 32)
                                if git >= 2:
                                    wev(("HE", ghx_hist[2 * (git - 2) + 1]))
                        i = None
                        for fc in range(16):
                            i = t_mm(pMM[:, 512 * bk:512 * (bk + 1)],
                                     wstr[:, git % 2, fc,
                                          128 * cc:128 * (cc + 1)],
                                     ln2T[:, fc, 512 * th:512 * (th + 1)],
                                     start=(fc == 0), stop=(fc == 15))
                        mark(i, "sH")
                        if eng == "scalar":
                            E.wait_ge(S["sH"], 2 * git + cc + 1)
                            if gx >= 2:
                                wev(("HE", ghx_hist[gx - 2]))
                        i = act(htmp[:, cc, :],
                                pMM[:, 512 * bk:512 * (bk + 1)],
                                AF.Relu, bias=0.0, scale=1.0)
                        mark(i, "sRELU")
                        if eng == "vector":
                            E.wait_ge(S["sRELU"], C["sRELU"])
                            if git == 0 and cc == 0:
                                for _p4 in range(4):
                                    E.wait_ge(S["sX2S%d" % _p4], 32)
                            if git == 32 and cc == 0:
                                E.wait_ge(S["sF2"], 64)
                        dst = (hsqA[:, hcx, :] if th == 0
                               else hsqB[:, hcx, :])
                        v_tt(dst, htmp[:, cc, :], htmp[:, cc, :],
                             ALU.mult)
                        dve_ev(("HE", ghx))
                    if hh == 0:
                        q_step(38)
                        if th == 0 and itl == 1:
                            q_flush_until(24)
                            lateT(list(range(16, 24)), wfc_shW)
                            emit_cc("wfc2", [wfcT_shc[2, :, :]],
                                    [wfcT_allc[2, :, :, :]])
                        elif th == 0 and itl == 5:
                            q_flush_until(32)
                            lateT(list(range(24, 32)), wfc_shW)
                            emit_cc("wfc3", [wfcT_shc[3, :, :]],
                                    [wfcT_allc[3, :, :, :]])
                        elif (th, itl) in ((0, 9), (0, 13)):
                            cjp = (itl - 9) // 4
                            q_flush_until(NQT_FC + 8 * (cjp + 1))
                            lateT(list(range(NQT_FC + 8 * cjp,
                                             NQT_FC + 8 * (cjp + 1))),
                                  wpj_shW)
                            emit_cc("wpj%d" % cjp, [wpjT_shc[cjp, :, :]],
                                    [wpjT_allc[cjp, :, :, :]])
                    else:
                        q_step(30)
                        if th == 0 and itl in (1, 5):
                            cjp = 2 + (itl - 1) // 4
                            q_flush_until(NQT_FC + 8 * (cjp + 1))
                            lateT(list(range(NQT_FC + 8 * cjp,
                                             NQT_FC + 8 * (cjp + 1))),
                                  wpj_shW)
                            emit_cc("wpj%d" % cjp, [wpjT_shc[cjp, :, :]],
                                    [wpjT_allc[cjp, :, :, :]])
              for th in range(2):
                for p4 in range(4):
                    tl = 4 * th + p4
                    if eng == "sync":
                        if hh == 0 and th == 0:
                            E.wait_ge(S["sOP"], 64)
                            E.wait_ge(S["sX2S%d" % p4], 32)
                        elif hh == 0:
                            E.wait_ge(S["sYS%d" % p4], 16)
                        else:
                            E.wait_ge(S["sYS%d" % p4], 32)
                            if th == 1:
                                E.wait_ge(S["sOUT%d" % p4], 16)
                    i = dma_s(out=outsb4[:, p4, :],
                              in_=(x2_spill if hh == 0 else y_part)
                              [128 * tl:128 * (tl + 1), :])
                    mark(i, "sXR2%d" % p4, 16)
                for fg in range(8):
                    lfg = lfgc[0]
                    lfgc[0] += 1
                    for hcgl in range(4):
                        cw = 2 * hh + hcgl // 2
                        L = gLc[0]
                        gLc[0] += 1
                        if eng == "sync":
                            if th == 0 and fg == 0:
                                E.wait_ge(S["sCC"], CCN["wpj%d" % cw])
                            if L >= 2:
                                E.wait_ge(S["sF2"], L - 1)
                        i = dma_s(out=wstr2[:, L % 2, :, :],
                                  in_=wpjT_allc[cw, fg,
                                                1024 * (hcgl % 2):
                                                1024 * (hcgl % 2 + 1), :]
                                  .rearrange("(k p) c -> p k c", p=128))
                        mark(i, "sF2W%d" % (L % 2), 16)
                        bk2 = 2 * (fg % 2)
                        if eng == "tensor":
                            E.wait_ge(S["sF2W%d" % (L % 2)], 16 * (L // 2 + 1))
                            if hcgl == 0:
                                if fg == 0:
                                    wev(("HE", 64 * hh + 31))
                                    wev(("HE", 64 * hh + 63))
                                if lfg >= 2:
                                    E.wait_ge(S["sSTG2"], 2 * (lfg - 1))
                        for hc8 in range(8):
                            hc = 8 * hcgl + hc8
                            for cc in range(2):
                                i = t_mm(pMM[:, 512 * (bk2 + cc):
                                             512 * (bk2 + cc + 1)],
                                         wstr2[:, L % 2, hc8,
                                               128 * cc:128 * (cc + 1)],
                                         hsqA[:, hc, :] if th == 0
                                         else hsqB[:, hc, :],
                                         start=(hc == 0), stop=(hc == 31))
                                if hc8 == 7 and cc == 1:
                                    mark(i, "sF2")
                    for cc in range(2):
                        fc = 2 * fg + cc
                        gfc = gfcc[0]
                        gfcc[0] += 1
                        bk2 = 2 * (fg % 2)
                        if eng == "scalar":
                            E.wait_ge(S["sF2"], 4 * (lfg + 1))
                            if gfc >= 2:
                                E.wait_ge(S["sFT2"], 4 * (gfc - 1))
                        i = act(stg[:, gfc % 2, :],
                                pMM[:, 512 * (bk2 + cc):512 * (bk2 + cc + 1)],
                                AF.Identity, bias=0.0, scale=1.0)
                        mark(i, "sSTG2")
                        G2 = FG[0]
                        pT32 = pSC[:, 512 * (G2 % 2):512 * (G2 % 2) + 512]
                        if eng == "tensor":
                            E.wait_ge(S["sSTG2"], gfc + 1)
                            if G2 >= 2:
                                wev(("FG", G2 - 2))
                        for p4 in range(4):
                            i = t_tr(pT32[:, 128 * p4:128 * (p4 + 1)],
                                     stg[:, gfc % 2, 128 * p4:128 * (p4 + 1)],
                                     consts_sb[:, CO_ID:CO_ID + 128])
                            if p4 == 3:
                                mark(i, "sFT2", 4)
                        for p4 in range(4):
                            if eng == "vector":
                                E.wait_ge(S["sFT2"], 4 * gfc + 4)
                                if fc == 0:
                                    E.wait_ge(S["sXR2%d" % p4],
                                              C["sXR2%d" % p4])
                            v_tt(outsb4[:, p4, 128 * fc:128 * (fc + 1)],
                                 outsb4[:, p4, 128 * fc:128 * (fc + 1)],
                                 pT32[:, 128 * p4:128 * (p4 + 1)], ALU.add)
                        dve_ev(("FG", G2))
                        dve_ev(("O", hh, th, fc))
                        FG[0] += 1
                for p4 in range(4):
                    tl = 4 * th + p4
                    if eng == "sync":
                        wev(("O", hh, th, 15))
                    if hh == 0:
                        i = dma_s(out=y_part[128 * tl:128 * (tl + 1), :],
                                  in_=outsb4[:, p4, :])
                        mark(i, "sYS%d" % p4, 16)
                    else:
                        i = dma_s(out=outT["out"][128 * tl:128 * (tl + 1), :],
                                  in_=outsb4[:, p4, :])
                        mark(i, "sOUT%d" % p4, 16)
            for _p4 in range(4):
                weng("sync", "sOUT%d" % _p4, 32)

        @block.sync
        def _(e):
            emit("sync", e)

        @block.gpsimd
        def _(e):
            emit("gpsimd", e)

        @block.vector
        def _(e):
            emit("vector", e)

        @block.scalar
        def _(e):
            emit("scalar", e)

        @block.tensor
        def _(e):
            emit("tensor", e)

    return nc, outT


from concourse.bass_utils import run_bass_kernel_spmd


def _make_consts():
    c = np.zeros((128, CO_W), np.float32)
    c[:, CO_ID:CO_ID + 128] = np.eye(128)
    c[:, CO_ONE:CO_ONE + 128] = 1.0
    c[:, CO_INVS:CO_INVS + 16] = np.array(INV_SHRINKS_F32, np.float32)[None, :]
    c[:, CO_NEGS:CO_NEGS + 16] = -np.array(SHRINKS_F32, np.float32)[None, :]
    c[:, CO_S:CO_S + 16] = np.array(SHRINKS_F32, np.float32)[None, :]
    return c


def kernel(L0_attn_x, L0_attn_w_qkv, L0_attn_w_o, L0_ffn_w_fc, L0_ffn_w_proj):
    x = np.ascontiguousarray(np.asarray(L0_attn_x, np.float32))
    w_qkv = np.ascontiguousarray(np.asarray(L0_attn_w_qkv, np.float32))
    w_o = np.ascontiguousarray(np.asarray(L0_attn_w_o, np.float32))
    w_fc = np.ascontiguousarray(np.asarray(L0_ffn_w_fc, np.float32))
    w_pj = np.ascontiguousarray(np.asarray(L0_ffn_w_proj, np.float32))

    nc, outT = build(5)
    consts = _make_consts()

    wqkv_j = []
    for j in range(2):
        cols = []
        for qkv3 in range(3):
            for h in range(8 * j, 8 * j + 8):
                cols.extend(range(qkv3 * 2048 + h * 128,
                                  qkv3 * 2048 + (h + 1) * 128))
        wqkv_j.append(np.ascontiguousarray(w_qkv[:, cols]))

    in_maps = []
    for c in range(8):
        b, j = c // 2, c % 2
        in_maps.append({
            "xb": x[b],
            "xrows": np.ascontiguousarray(x[b, 1024 * j:1024 * (j + 1), :]),
            "wqkv": wqkv_j[j],
            "wo": w_o,
            "wfc": np.ascontiguousarray(w_fc[1024 * c:1024 * (c + 1), :]),
            "wpj": np.ascontiguousarray(w_pj[256 * c:256 * (c + 1), :]),
            "consts": consts,
        })

    res = run_bass_kernel_spmd(nc, in_maps, core_ids=list(range(8))).results

    out = np.empty((4, 2048, 2048), np.float32)
    for c in range(8):
        b, j = c // 2, c % 2
        out[b, 1024 * j:1024 * (j + 1), :] = np.asarray(res[c]["out"])
    return out
